# version 03
# baseline (speedup 1.0000x reference)
import math
import os
import numpy as np
import ml_dtypes
import concourse.bass as bass
import concourse.mybir as mybir
from concourse.bass_utils import run_bass_kernel_spmd

F32 = mybir.dt.float32
BF16 = mybir.dt.bfloat16
U8 = mybir.dt.uint8
I32 = mybir.dt.int32
AL = mybir.AluOpType
ACT = mybir.ActivationFunctionType
bf16 = ml_dtypes.bfloat16

L = 2048
D = 2048
HID = 5632
NB = 8
TWO_PI = 2.0 * math.pi

MB = 65536
SEG = 16384
OFF_P = [MB + i * SEG for i in range(7)]
OFF_B = MB + 7 * SEG
OFF_T = [OFF_B + i * 8192 for i in range(4)]
OFF_KVT = OFF_B + 4 * 8192


def _consts_dft():
    s_ = np.arange(32); p_ = np.arange(64); r_ = np.arange(128); q_ = np.arange(32)
    th1 = 2 * np.pi * (np.einsum('p,r->pr', p_, r_)[None] / 128
                       + np.einsum('s,r->sr', s_, r_)[:, None, :] / 4096)
    W1c, W1s = np.cos(th1), -np.sin(th1)
    W1R = np.zeros((128, 2048), np.float32); W1I = np.zeros((128, 2048), np.float32)
    for u in range(16):
        for phi in range(2):
            W1R[phi * 64:phi * 64 + 64, u * 128:u * 128 + 128] = W1c[2 * u + phi]
            W1I[phi * 64:phi * 64 + 64, u * 128:u * 128 + 128] = W1s[2 * u + phi]
    th2 = 2 * np.pi * np.einsum('s,q->sq', s_, q_) / 32
    W2re, W2im = np.cos(th2), -np.sin(th2)
    th3 = 2 * np.pi * np.einsum('q,s->qs', q_, s_) / 32
    W3re, W3im = np.cos(th3) / 4096.0, np.sin(th3) / 4096.0
    th4 = 2 * np.pi * (np.einsum('p,r->pr', p_, r_)[None] / 128
                       + np.einsum('s,r->sr', s_, r_)[:, None, :] / 4096)
    W4re, W4im = np.cos(th4), np.sin(th4)
    W4R = np.zeros((128, 2048), np.float32); W4IN = np.zeros((128, 2048), np.float32)
    for s in range(32):
        W4R[:, s * 64:s * 64 + 64] = W4re[s].T
        W4IN[:, s * 64:s * 64 + 64] = -W4im[s].T
    rep4 = lambda m: np.tile(m, (4, 1)).astype(np.float32)
    c = {"W1R": W1R, "W1I": W1I, "W4R": W4R, "W4IN": W4IN,
         "W2RE": rep4(W2re), "W2IM": rep4(W2im), "W2IMN": rep4(-W2im),
         "W3RE": rep4(W3re), "W3IM": rep4(W3im), "W3IMN": rep4(-W3im)}
    return {k: v.astype(bf16) for k, v in c.items()}


def _consts_filter():
    t = np.linspace(0.0, 1.0, L)[:, None]
    t_resc = np.linspace(0.0, L - 1.0, L)
    w = 2.0 * np.pi * t_resc / L
    f = np.linspace(0.0001, 0.0, 1)
    ang = f[None, :] * w[:, None]
    z = np.concatenate([t, np.cos(ang), -np.sin(ang)], axis=-1)
    zT = np.ascontiguousarray(z.T).astype(bf16)
    deltas = np.abs(np.linspace(np.log(0.01) / 1.5, np.log(0.01) / 0.3, D))
    decay = np.exp(-t * deltas[None, :]).astype(np.float32)
    dect = np.zeros((128, NB * 4096), np.float32)
    pp = np.arange(64)
    for b in range(NB):
        for u in range(16):
            for phi in range(2):
                tv = 2 * u + phi + 32 * pp
                dect[phi * 64:phi * 64 + 64, b * 4096 + u * 256:b * 4096 + (u + 1) * 256] = \
                    decay[tv][:, b * 256:(b + 1) * 256]
    return zT, dect.astype(bf16)


_NC_CACHE = {}


def build_program():
    if "nc" in _NC_CACHE:
        return _NC_CACHE["nc"]
    nc = bass.Bass("TRN2", target_bir_lowering=False)

    dr = {}
    def din(name, shape, dtype=F32):
        dr[name] = nc.dram_tensor(name, list(shape), dtype, kind="ExternalInput").ap()

    din("xT", [D, L])
    din("WmixT", [D, 3 * D])
    din("bmix", [128, 48]); din("swT", [128, 144]); din("sbT", [128, 48])
    din("fbias", [128, 16])
    din("WoutT", [D, D]); din("bout", [128, 16])
    din("w0T", [3, 64]); din("w1T", [64, 64]); din("w2T", [64, 64]); din("w3T", [64, D])
    din("b0", [64, 1]); din("b1", [64, 1]); din("b2", [64, 1]); din("freq", [64, 1])
    din("zT", [3, L], BF16)
    din("dect", [128, NB * 4096], BF16)
    for n in ("W1R", "W1I", "W4R", "W4IN"):
        din(n, [128, 2048], BF16)
    for n in ("W2RE", "W2IM", "W2IMN", "W3RE", "W3IM", "W3IMN"):
        din(n, [128, 32], BF16)
    din("wgT", [D, HID], U8); din("wuT", [D, HID], U8); din("wdT", [HID, D], U8)
    din("sg", [128, 44]); din("su", [128, 44]); din("sd", [128, 16])
    din("negc", [1, 128], BF16)

    out_t = nc.dram_tensor("outT", [D, L], F32, kind="ExternalOutput").ap()
    y_dram = nc.dram_tensor("Yscr", [D, L], BF16, kind="Internal").ap()
    x2_dram = nc.dram_tensor("X2scr", [D, L], F32, kind="Internal").ap()

    def sbn(name, shape, dtype=BF16):
        return nc.alloc_sbuf_tensor(name, list(shape), dtype)

    C = {}
    for n in ("W1R", "W1I", "W4R", "W4IN"):
        C[n] = sbn("C" + n, [128, 2048])
    for n in ("W2RE", "W2IM", "W2IMN", "W3RE", "W3IM", "W3IMN"):
        C[n] = sbn("C" + n, [128, 32])
    bmix_t = sbn("bmix_t", [128, 48], F32)
    swT_t = sbn("swT_t", [128, 144], F32)
    sbT_t = sbn("sbT_t", [128, 48], F32)
    fbias_t = sbn("fbias_t", [128, 16], F32)
    bout_t = sbn("bout_t", [128, 16], F32)
    sg_t = sbn("sg_t", [128, 44], F32); su_t = sbn("su_t", [128, 44], F32)
    sd_t = sbn("sd_t", [128, 16], F32)
    w0T_t = sbn("w0T_t", [3, 64]); w1T_t = sbn("w1T_t", [64, 64]); w2T_t = sbn("w2T_t", [64, 64])
    w3T_t = sbn("w3T_t", [64, D])
    b0_t = sbn("b0_t", [64, 1], F32); b1_t = sbn("b1_t", [64, 1], F32)
    b2_t = sbn("b2_t", [64, 1], F32); fq_t = sbn("fq_t", [64, 1], F32)
    hhp_t = sbn("hhp_t", [64, L])
    ones_t = sbn("ones_t", [128, 1])
    onesr_t = sbn("onesr_t", [1, 128], F32)
    negc_t = sbn("negc_t", [1, 128])
    rows_t = sbn("rows_t", [1, 4096], F32)
    sxzb_t = sbn("sxzb_t", [1, 2048])

    def srow(a, b_):
        return rows_t[0:1, a:b_]
    def srow2(a, b_):
        return rows_t[0:1, 1024 + a:1024 + b_]
    def mrow(a, b_):
        return rows_t[0:1, 2048 + a:2048 + b_]
    def vrow(a, b_):
        return rows_t[0:1, 3072 + a:3072 + b_]
    def sxb(a, b_):
        return sxzb_t[0:1, a:b_]
    def szb(a, b_):
        return sxzb_t[0:1, 1024 + a:1024 + b_]
    assert nc.sbuf_bytes_remaining > (229344 - MB), nc.sbuf_bytes_remaining

    def sbm(name, offset, shape, dtype=BF16):
        return nc.alloc_sbuf_tensor_at(name, list(shape), dtype, offset=offset, align_bytes=64)

    P = [sbm(f"P{i+1}", OFF_P[i], [128, 8192]) for i in range(7)]
    T0 = sbm("T0", OFF_T[0], [128, 4096]); T1 = sbm("T1", OFF_T[1], [128, 4096])
    T2 = sbm("T2", OFF_T[2], [128, 4096]); T3 = sbm("T3", OFF_T[3], [128, 4096])
    KVT = sbm("KVT", OFF_KVT, [128, 4096])
    arg_t = sbm("arg_t", OFF_P[0], [64, 2048], F32)
    un_t = sbm("un_t", OFF_P[1], [64, 2048], F32)
    ni_t = sbm("ni_t", OFF_P[2], [64, 2048], I32)
    nf_t = sbm("nf_t", OFF_P[3], [64, 2048], F32)
    hh_t = sbm("hh_t", OFF_P[4], [64, 2048])
    zT_t = sbm("zT_t", OFF_P[5], [3, L])
    CtR, CtI = P[0], P[1]
    TCR, TCI = P[2], P[3]
    XRE, XIM = P[0], P[1]
    KRE, KIM = P[4], P[5]
    S1 = P[6]
    TER, TEI = P[4], P[5]
    ype = sbm("ype", OFF_P[6], [128, 4096])
    yperm = sbm("yperm", OFF_P[6] + 8192, [128, 4096])
    dect_b = sbm("dect_b", OFF_P[6], [128, 4096])
    xtile = P[0]
    wmixb = sbm("wmixb", OFF_P[1], [128, 12288])
    wo_sl = sbm("wo_sl", OFF_P[0], [128, 4096])
    ytile = sbm("ytile", OFF_P[4], [128, 8192])
    r1_t = sbm("r1_t", OFF_P[5], [128, 8192], F32)
    r1b_t = sbm("r1b_t", OFF_P[1], [128, 8192])
    sq_t = sbm("sq_t", OFF_P[2], [128, 8192])
    z_t = sbm("z_t", MB, [128, 45056])
    x2b_t = sbm("x2b_t", MB + 90112, [128, 16384])
    wd_sl = sbm("wd_sl", MB + 122880, [128, 5632])
    wg_sl = sbm("wg_sl", MB + 134144, [128, 2048])
    wu_sl = sbm("wu_sl", MB + 138240, [128, 2048])
    xr2_t = sbm("xr2_t", MB + 142336, [128, 1024], F32)
    r2f_t = sbm("r2f_t", MB + 146432, [128, 1024], F32)
    mb2_t = sbm("mb2_t", MB + 150528, [128, 1024], F32)
    ib2_t = sbm("ib2_t", MB + 154624, [128, 1024], F32)
    r2b_t = sbm("r2b_t", MB + 158720, [128, 1024])
    sqb_t = sbm("sqb_t", MB + 160768, [128, 1024])

    mb_t = mb2_t
    ib_t = ib2_t
    xr_t = xr2_t

    PS = []
    def psum(name, shape):
        g = nc.psum_tensor(name, list(shape), F32)
        t = g.__enter__(); PS.append(g)
        return t
    PA = psum("PA", [128, 512])
    PB = psum("PB", [128, 512])
    PCe = psum("PCe", [128, 256])
    PCo = psum("PCo", [128, 256])
    PXR = psum("PXR", [128, 512])
    PXI = psum("PXI", [128, 512])
    P2m = psum("P2m", [128, 256])
    PR = psum("PR", [128, 512])

    sems = {}
    for nm in ("ld", "mm", "ev", "tp", "st", "bar"):
        g = nc.semaphore("s_" + nm)
        sems[nm] = g.__enter__()
    cnt = {"ld": 0, "mm": 0, "ev": 0, "tp": 0, "st": 0, "bar": 0}
    ENG = ("sync", "scalar", "vector", "tensor", "gpsimd")

    def barrier():
        cnt["bar"] += 1
        tgt = 5 * cnt["bar"]
        with nc.Block() as b:
            for e in ENG:
                def mk(e=e):
                    def f(eng):
                        eng.sem_inc(sems["bar"], 1)
                        eng.wait_ge(sems["bar"], tgt)
                    return f
                getattr(b, e)(mk())

    with nc.Block() as b:
        @b.gpsimd
        def _(g):
            for nm, t in (("bmix", bmix_t), ("swT", swT_t), ("sbT", sbT_t), ("fbias", fbias_t),
                          ("bout", bout_t), ("sg", sg_t), ("su", su_t), ("sd", sd_t),
                          ("b0", b0_t), ("b1", b1_t), ("b2", b2_t), ("freq", fq_t),
                          ("w0T", w0T_t), ("w1T", w1T_t), ("w2T", w2T_t), ("w3T", w3T_t)):
                g.dma_start(out=t[:], in_=dr[nm][:]).then_inc(sems["ld"], 16)
                cnt["ld"] += 16
            g.memset(ones_t[:], 1.0)
            g.memset(onesr_t[:], 1.0)
        @b.sync
        def _(s):
            for nm, t in (("zT", zT_t), ("negc", negc_t),
                          ("W1R", C["W1R"]), ("W1I", C["W1I"]), ("W4R", C["W4R"]), ("W4IN", C["W4IN"]),
                          ("W2RE", C["W2RE"]), ("W2IM", C["W2IM"]), ("W2IMN", C["W2IMN"]),
                          ("W3RE", C["W3RE"]), ("W3IM", C["W3IM"]), ("W3IMN", C["W3IMN"])):
                s.dma_start(out=t[:], in_=dr[nm][:]).then_inc(sems["ld"], 16)
                cnt["ld"] += 16
            s.wait_ge(sems["ld"], cnt["ld"])
    barrier()

    def mlp_layer(lhsT, rhs_q, bias):
        with nc.Block() as b:
            m0 = cnt["mm"]; e0 = cnt["ev"]
            @b.tensor
            def _(t):
                for j in range(4):
                    if j > 0:
                        t.wait_ge(sems["ev"], e0 + j)
                    ins = t.matmul(PA[:64, :], lhsT, rhs_q[j], start=True, stop=True)
                    ins.then_inc(sems["mm"], 1)
            @b.vector
            def _(v):
                for j in range(4):
                    v.wait_ge(sems["mm"], m0 + j + 1)
                    cols = slice(j * 512, (j + 1) * 512)
                    v.tensor_scalar(arg_t[:, cols], PA[:64, :], bias[:], fq_t[:], AL.add, AL.mult)
                    v.tensor_scalar(un_t[:, cols], arg_t[:, cols], 1.0 / TWO_PI, None, AL.mult)
                    v.tensor_copy(ni_t[:, cols], un_t[:, cols])
                    v.tensor_copy(nf_t[:, cols], ni_t[:, cols])
                    ins = v.scalar_tensor_tensor(arg_t[:, cols], nf_t[:, cols], -TWO_PI,
                                                 arg_t[:, cols], AL.mult, AL.add)
                    ins.then_inc(sems["ev"], 1)
            @b.scalar
            def _(sc):
                for j in range(4):
                    sc.wait_ge(sems["ev"], e0 + j + 1)
                    cols = slice(j * 512, (j + 1) * 512)
                    sc.activation(hh_t[:, cols], arg_t[:, cols], ACT.Sin)
            cnt["mm"] = m0 + 4; cnt["ev"] = e0 + 4
        barrier()

    mlp_layer(w0T_t[:], [zT_t[:, j * 512:(j + 1) * 512] for j in range(4)], b0_t)
    mlp_layer(w1T_t[:], [hh_t[:, j * 512:(j + 1) * 512] for j in range(4)], b1_t)
    mlp_layer(w2T_t[:], [hh_t[:, j * 512:(j + 1) * 512] for j in range(4)], b2_t)
    with nc.Block() as b:
        @b.vector
        def _(v):
            ap_in = bass.AP(hh_t, 0, [[L, 64], [1, 32], [32, 64]])
            v.tensor_copy(hhp_t[:].rearrange("a (s p) -> a s p", s=32), ap_in)
    barrier()

    def F1(srcT):
        with nc.Block() as b:
            m0 = cnt["mm"]; e0 = cnt["ev"]
            @b.tensor
            def _(t):
                nm = 0
                for h in range(2):
                    for u in range(16):
                        if nm >= 1:
                            t.wait_ge(sems["ev"], e0 + 4 * nm)
                        rhsE = srcT[0:64, h * 2048 + u * 128: h * 2048 + (u + 1) * 128]
                        rhsO = srcT[64:128, h * 2048 + u * 128: h * 2048 + (u + 1) * 128]
                        wre = C["W1R"][:, u * 128:(u + 1) * 128]
                        wim = C["W1I"][:, u * 128:(u + 1) * 128]
                        t.matmul(PCe[:, 0:128], wre[0:64, :], rhsE, start=True, stop=True, tile_position=(0, 0))
                        t.matmul(PCe[:, 128:256], wim[0:64, :], rhsE, start=True, stop=True, tile_position=(0, 0))
                        t.matmul(PCo[:, 0:128], wre[64:128, :], rhsO, start=True, stop=True, tile_position=(64, 0))
                        ins = t.matmul(PCo[:, 128:256], wim[64:128, :], rhsO, start=True, stop=True, tile_position=(64, 0))
                        nm += 1
                        ins.then_inc(sems["mm"], 1)
                cnt["mm"] = m0 + nm
            @b.vector
            def _(v):
                ne = 0
                for h in range(2):
                    for u in range(16):
                        v.wait_ge(sems["mm"], m0 + h * 16 + u + 1)
                        for pi, dst in ((0, CtR), (1, CtI)):
                            for so, pcsrc in ((0, PCe), (1, PCo)):
                                ap_in = bass.AP(pcsrc, pi * 128, [[256, 128], [1, 128]])
                                ap_out = bass.AP(dst, (h * 32) * 128 + 2 * u + so,
                                                 [[8192, 128], [128, 32], [32, 4]])
                                ins = v.tensor_copy(ap_out, ap_in)
                                ne += 1
                                ins.then_inc(sems["ev"], 1)
                cnt["ev"] = e0 + ne
        barrier()

    def turn(srcR, srcI, dstR, dstI):
        with nc.Block() as b:
            @b.sync
            def _(s):
                for sr, ds in ((srcR, dstR), (srcI, dstI)):
                    s.dma_start_transpose(
                        ds[:].rearrange("a (g r) -> a g r", g=64),
                        sr[:].rearrange("a (g f) -> a g f", g=64)).then_inc(sems["tp"], 16)
                    cnt["tp"] += 16
                s.wait_ge(sems["tp"], cnt["tp"])
        barrier()

    def F2(WRE, WIM, WIMN, srcR, srcI, dstRE, dstIM):
        with nc.Block() as b:
            m0 = cnt["mm"]; e0 = cnt["ev"]
            @b.tensor
            def _(t):
                nm = 0
                for blk in range(16):
                    if nm >= 1:
                        t.wait_ge(sems["ev"], e0 + 2 * nm)
                    cols = slice(blk * 512, blk * 512 + 512)
                    for c4 in range(4):
                        band = slice(32 * c4, 32 * c4 + 32)
                        tpos = (32 * c4, 32 * c4)
                        t.matmul(PXR[band, :], C[WRE][band, :], srcR[band, cols], start=True, stop=False, tile_position=tpos)
                        t.matmul(PXR[band, :], C[WIMN][band, :], srcI[band, cols], start=False, stop=True, tile_position=tpos)
                        t.matmul(PXI[band, :], C[WIM][band, :], srcR[band, cols], start=True, stop=False, tile_position=tpos)
                        ins = t.matmul(PXI[band, :], C[WRE][band, :], srcI[band, cols], start=False, stop=True, tile_position=tpos)
                    nm += 1
                    ins.then_inc(sems["mm"], 1)
                cnt["mm"] = m0 + nm
            @b.vector
            def _(v):
                ne = 0
                for blk in range(16):
                    v.wait_ge(sems["mm"], m0 + blk + 1)
                    cols = slice(blk * 512, blk * 512 + 512)
                    v.tensor_copy(dstRE[:, cols], PXR[:])
                    ins = v.tensor_copy(dstIM[:, cols], PXI[:])
                    ne += 2
                    ins.then_inc(sems["ev"], 2)
                cnt["ev"] = e0 + ne
        barrier()

    def conv_batch(bi):
        with nc.Block() as b:
            @b.sync
            def _(s):
                s.dma_start(out=dect_b[:], in_=dr["dect"][:, bi * 4096:(bi + 1) * 4096]).then_inc(sems["tp"], 16)
                cnt["tp"] += 16
                s.wait_ge(sems["tp"], cnt["tp"])
        barrier()
        with nc.Block() as b:
            m0 = cnt["mm"]; e0 = cnt["ev"]
            @b.tensor
            def _(t):
                for u in range(16):
                    if u > 0:
                        t.wait_ge(sems["ev"], e0 + u)
                    ins = t.matmul(PA[:, 0:256], hhp_t[:, u * 128:(u + 1) * 128],
                                   w3T_t[:, bi * 256:(bi + 1) * 256], start=True, stop=True)
                    ins.then_inc(sems["mm"], 1)
            @b.vector
            def _(v):
                for u in range(16):
                    v.wait_ge(sems["mm"], m0 + u + 1)
                    for h in range(2):
                        ins = v.tensor_tensor(
                            KVT[:, h * 2048 + u * 128: h * 2048 + u * 128 + 128],
                            PA[:, h * 128:(h + 1) * 128],
                            dect_b[:, u * 256 + h * 128: u * 256 + (h + 1) * 128], AL.mult)
                    ins.then_inc(sems["ev"], 1)
            cnt["mm"] = m0 + 16; cnt["ev"] = e0 + 16
        barrier()
        F1(KVT)
        turn(CtR, CtI, TCR, TCI)
        F2("W2RE", "W2IM", "W2IMN", TCR, TCI, KRE, KIM)

        with nc.Block() as b:
            l0 = cnt["ld"]
            @b.gpsimd
            def _(g):
                for dk in range(16):
                    g.dma_start(out=wmixb[:, dk * 768:(dk + 1) * 768],
                                in_=dr["WmixT"][dk * 128:(dk + 1) * 128, bi * 768:(bi + 1) * 768]).then_inc(sems["ld"], 16)
                    cnt["ld"] += 16
                g.wait_ge(sems["ld"], cnt["ld"])
        barrier()
        for tt in range(4):
            with nc.Block() as b:
                m0 = cnt["mm"]; e0 = cnt["ev"]; l0 = cnt["ld"]
                @b.gpsimd
                def _(g):
                    for dk in range(16):
                        g.dma_start(out=xtile[:, dk * 512:(dk + 1) * 512],
                                    in_=dr["xT"][dk * 128:(dk + 1) * 128, tt * 512:(tt + 1) * 512]).then_inc(sems["ld"], 16)
                        cnt["ld"] += 16
                @b.tensor
                def _(t):
                    t.wait_ge(sems["ld"], l0 + 256)
                    for rc in range(6):
                        if rc >= 2:
                            t.wait_ge(sems["ev"], e0 + rc - 1)
                        ps = PA if rc % 2 == 0 else PB
                        for dk in range(16):
                            ins = t.matmul(ps[:], wmixb[:, dk * 768 + rc * 128: dk * 768 + (rc + 1) * 128],
                                           xtile[:, dk * 512:(dk + 1) * 512],
                                           start=(dk == 0), stop=(dk == 15))
                        ins.then_inc(sems["mm"], 1)
                @b.scalar
                def _(sc):
                    for rc in range(6):
                        sc.wait_ge(sems["mm"], m0 + rc + 1)
                        ps = PA if rc % 2 == 0 else PB
                        role, half = rc // 2, rc % 2
                        dstT = (T0, T1, T2)[role]
                        ins = sc.activation(dstT[:, half * 2048 + tt * 512: half * 2048 + (tt + 1) * 512],
                                            ps[:], ACT.Identity, bias=bmix_t[:, bi * 6 + rc: bi * 6 + rc + 1])
                        ins.then_inc(sems["ev"], 1)
                cnt["mm"] = m0 + 6; cnt["ev"] = e0 + 6
            barrier()
        with nc.Block() as b:
            @b.vector
            def _(v):
                for (src, dst, role) in ((T0, T3, 0), (T1, T0, 1), (T2, T1, 2)):
                    for h in range(2):
                        j = bi * 6 + role * 2 + h
                        cs = h * 2048
                        w0s = swT_t[:, j * 3 + 0: j * 3 + 1]
                        w1s = swT_t[:, j * 3 + 1: j * 3 + 2]
                        w2s = swT_t[:, j * 3 + 2: j * 3 + 3]
                        v.tensor_scalar(dst[:, cs:cs + 2048], src[:, cs:cs + 2048],
                                        w2s, sbT_t[:, j:j + 1], AL.mult, AL.add)
                        v.scalar_tensor_tensor(dst[:, cs + 1:cs + 2048], src[:, cs:cs + 2047],
                                               w1s, dst[:, cs + 1:cs + 2048], AL.mult, AL.add)
                        v.scalar_tensor_tensor(dst[:, cs + 2:cs + 2048], src[:, cs:cs + 2046],
                                               w0s, dst[:, cs + 2:cs + 2048], AL.mult, AL.add)
                v.tensor_tensor(T1[:], T1[:], T0[:], AL.mult)
                for h in range(2):
                    ap_in = bass.AP(T1, h * 2048, [[4096, 128], [1, 32], [32, 64]])
                    out_ap = bass.AP(T2, h * 2048, [[4096, 128], [64, 32], [1, 64]])
                    v.tensor_copy(out_ap, ap_in)
        barrier()
        with nc.Block() as b:
            @b.sync
            def _(s):
                s.dma_start_transpose(
                    KVT[:].rearrange("a (t f) -> a t f", t=32),
                    T2[:].rearrange("a (t f) -> a t f", t=32)).then_inc(sems["tp"], 16)
                cnt["tp"] += 16
                s.wait_ge(sems["tp"], cnt["tp"])
        barrier()
        F1(KVT)
        turn(CtR, CtI, TCR, TCI)
        F2("W2RE", "W2IM", "W2IMN", TCR, TCI, XRE, XIM)
        with nc.Block() as b:
            @b.vector
            def _(v):
                v.tensor_tensor(S1[:], XRE[:], KIM[:], AL.mult)
                v.tensor_tensor(XRE[:], XRE[:], KRE[:], AL.mult)
                v.tensor_tensor(KRE[:], XIM[:], KRE[:], AL.mult)
                v.tensor_tensor(XIM[:], XIM[:], KIM[:], AL.mult)
                v.tensor_tensor(XRE[:], XRE[:], XIM[:], AL.subtract)
                v.tensor_tensor(S1[:], S1[:], KRE[:], AL.add)
        barrier()
        F2("W3RE", "W3IM", "W3IMN", XRE, S1, TCR, TCI)
        turn(TCR, TCI, TER, TEI)
        with nc.Block() as b:
            m0 = cnt["mm"]; e0 = cnt["ev"]
            @b.tensor
            def _(t):
                for u in range(16):
                    if u > 0:
                        t.wait_ge(sems["ev"], e0 + u)
                    for phi in range(2):
                        s = 2 * u + phi
                        rhsR = bass.AP(TER, s, [[8192, 128], [128, 64], [32, 4]])
                        rhsI = bass.AP(TEI, s, [[8192, 128], [128, 64], [32, 4]])
                        tpos = (0, 64 * phi)
                        outp = P2m[64 * phi:64 * phi + 64, :]
                        t.matmul(outp, C["W4R"][:, s * 64:s * 64 + 64], rhsR, start=True, stop=False, tile_position=tpos)
                        ins = t.matmul(outp, C["W4IN"][:, s * 64:s * 64 + 64], rhsI, start=False, stop=True, tile_position=tpos)
                    ins.then_inc(sems["mm"], 1)
            @b.vector
            def _(v):
                for u in range(16):
                    v.wait_ge(sems["mm"], m0 + u + 1)
                    v.tensor_copy(bass.AP(ype, u * 128, [[4096, 128], [1, 128]]), P2m[:, 0:128])
                    ins = v.tensor_copy(bass.AP(ype, 2048 + u * 128, [[4096, 128], [1, 128]]), P2m[:, 128:256])
                    ins.then_inc(sems["ev"], 1)
            cnt["mm"] = m0 + 16; cnt["ev"] = e0 + 16
        barrier()
        with nc.Block() as b:
            @b.sync
            def _(s):
                s.dma_start_transpose(
                    yperm[:].rearrange("a (t f) -> a t f", t=32),
                    ype[:].rearrange("a (t f) -> a t f", t=32)).then_inc(sems["tp"], 16)
                cnt["tp"] += 16
                s.wait_ge(sems["tp"], cnt["tp"])
        barrier()
        with nc.Block() as b:
            e0 = cnt["ev"]
            @b.vector
            def _(v):
                for h in range(2):
                    ap_y = bass.AP(yperm, h * 2048, [[4096, 128], [1, 64], [128, 16], [64, 2]])
                    ap_g1o = bass.AP(T2, h * 2048, [[4096, 128], [32, 64], [2, 16], [1, 2]])
                    fb = fbias_t[:, bi * 2 + h: bi * 2 + h + 1]
                    vt_ap = bass.AP(T1, h * 2048, [[4096, 128], [32, 64], [2, 16], [1, 2]])
                    v.scalar_tensor_tensor(ap_g1o, vt_ap, fb, ap_y, AL.mult, AL.add)
                    ins = v.tensor_tensor(T0[:, h * 2048:(h + 1) * 2048],
                                          T2[:, h * 2048:(h + 1) * 2048],
                                          T3[:, h * 2048:(h + 1) * 2048], AL.mult)
                    ins.then_inc(sems["ev"], 1)
            @b.sync
            def _(s):
                for h in range(2):
                    s.wait_ge(sems["ev"], e0 + h + 1)
                    s.dma_start(out=y_dram[(bi * 2 + h) * 128:(bi * 2 + h + 1) * 128, :],
                                in_=T0[:, h * 2048:(h + 1) * 2048]).then_inc(sems["st"], 16)
                    cnt["st"] += 16
                s.wait_ge(sems["st"], cnt["st"])
            cnt["ev"] = e0 + 2
        barrier()

    KPH = int(os.environ.get("KPH", "99"))
    KNB = int(os.environ.get("KNB", str(NB)))
    if KPH >= 1:
        for bi in range(KNB):
            conv_batch(bi)

    for tt in range(4 if KPH >= 2 else 0):
        with nc.Block() as b:
            @b.sync
            def _(s):
                for ck in range(16):
                    s.dma_start(out=ytile[:, ck * 512:(ck + 1) * 512],
                                in_=y_dram[ck * 128:(ck + 1) * 128, tt * 512:(tt + 1) * 512]).then_inc(sems["tp"], 16)
                    cnt["tp"] += 16
                s.wait_ge(sems["tp"], cnt["tp"])
        barrier()
        for dk in range(16):
            with nc.Block() as b:
                m0 = cnt["mm"]; e0 = cnt["ev"]; l0 = cnt["ld"]
                buf = dk % 2
                @b.gpsimd
                def _(g):
                    ap_in = bass.AP(dr["WoutT"].tensor, dk * 128, [[D, 128], [128 * D, 16], [1, 128]])
                    g.dma_start(out=wo_sl[:, buf * 2048:(buf + 1) * 2048], in_=ap_in).then_inc(sems["ld"], 16)
                    cnt["ld"] += 16
                @b.sync
                def _(s):
                    s.dma_start(out=xr_t[:, 0:512], in_=dr["xT"][dk * 128:(dk + 1) * 128, tt * 512:(tt + 1) * 512]).then_inc(sems["ld"], 16)
                    cnt["ld"] += 16
                @b.tensor
                def _(t):
                    t.wait_ge(sems["ld"], l0 + 32)
                    ps = PA if dk % 2 == 0 else PB
                    for ck in range(16):
                        ins = t.matmul(ps[:], wo_sl[:, buf * 2048 + ck * 128: buf * 2048 + (ck + 1) * 128],
                                       ytile[:, ck * 512:(ck + 1) * 512],
                                       start=(ck == 0), stop=(ck == 15))
                    ins.then_inc(sems["mm"], 1)
                @b.scalar
                def _(sc):
                    sc.wait_ge(sems["mm"], m0 + 1)
                    ps = PA if dk % 2 == 0 else PB
                    ins = sc.activation(bass.AP(r1_t, dk * 512, [[8192, 128], [1, 512]]),
                                        ps[:], ACT.Identity, bias=bout_t[:, dk:dk + 1])
                    ins.then_inc(sems["ev"], 1)
                @b.vector
                def _(v):
                    v.wait_ge(sems["ev"], e0 + 1)
                    r1c = bass.AP(r1_t, dk * 512, [[8192, 128], [1, 512]])
                    v.tensor_tensor(r1c, r1c, xr_t[:, 0:512], AL.add)
                    ins = v.tensor_copy(bass.AP(r1b_t, dk * 512, [[8192, 128], [1, 512]]), r1c)
                    ins.then_inc(sems["ev"], 1)
                cnt["mm"] = m0 + 1; cnt["ev"] = e0 + 2
            barrier()
        with nc.Block() as b:
            @b.scalar
            def _(sc):
                for dk in range(16):
                    sc.activation(bass.AP(sq_t, dk * 512, [[8192, 128], [1, 512]]),
                                  bass.AP(r1_t, dk * 512, [[8192, 128], [1, 512]]), ACT.Square)
        barrier()
        with nc.Block() as b:
            m0 = cnt["mm"]; e0 = cnt["ev"]
            @b.tensor
            def _(t):
                for ck in range(16):
                    t.matmul(PXR[0:1, :], ones_t[:], bass.AP(r1b_t, ck * 512, [[8192, 128], [1, 512]]),
                             start=(ck == 0), stop=(ck == 15))
                for ck in range(16):
                    ins = t.matmul(PXI[0:1, :], ones_t[:], bass.AP(sq_t, ck * 512, [[8192, 128], [1, 512]]),
                                   start=(ck == 0), stop=(ck == 15))
                ins.then_inc(sems["mm"], 1)
                t.wait_ge(sems["ev"], e0 + 2)
                ins = t.matmul(PR[:], onesr_t[:], srow(0, 512), start=True, stop=True)
                ins.then_inc(sems["mm"], 1)
                t.wait_ge(sems["ev"], e0 + 3)
                ins = t.matmul(PR[:], onesr_t[:], vrow(0, 512), start=True, stop=True)
                ins.then_inc(sems["mm"], 1)
            @b.vector
            def _(v):
                v.wait_ge(sems["mm"], m0 + 1)
                v.tensor_scalar(srow(0, 512), PXR[0:1, :], 1.0 / D, None, AL.mult)
                v.tensor_scalar(srow2(0, 512), PXI[0:1, :], 1.0 / D, None, AL.mult)
                v.tensor_tensor(vrow(0, 512), srow(0, 512), srow(0, 512), AL.mult)
                v.tensor_tensor(vrow(0, 512), srow2(0, 512), vrow(0, 512), AL.subtract)
                v.tensor_scalar(vrow(0, 512), vrow(0, 512), 1e-5, None, AL.add)
                ins = v.reciprocal(mrow(0, 512), vrow(0, 512))
                ins.then_inc(sems["ev"], 1)
                v.wait_ge(sems["mm"], m0 + 2)
                ins = v.tensor_copy(mb_t[:, 0:512], PR[:])
                ins.then_inc(sems["ev"], 1)
                v.wait_ge(sems["mm"], m0 + 3)
                ins = v.tensor_copy(ib_t[:, 0:512], PR[:])
                ins.then_inc(sems["ev"], 1)
            @b.scalar
            def _(sc):
                sc.wait_ge(sems["ev"], e0 + 1)
                ins = sc.activation(vrow(0, 512), mrow(0, 512), ACT.Sqrt)
                ins.then_inc(sems["ev"], 1)
            cnt["mm"] = m0 + 3; cnt["ev"] = e0 + 4
        barrier()
        with nc.Block() as b:
            e0 = cnt["ev"]
            @b.vector
            def _(v):
                for dk in range(16):
                    r1c = bass.AP(r1_t, dk * 512, [[8192, 128], [1, 512]])
                    v.tensor_tensor(r1c, r1c, mb_t[:, 0:512], AL.subtract)
                    ins = v.tensor_tensor(r1c, r1c, ib_t[:, 0:512], AL.mult)
                    ins.then_inc(sems["ev"], 1)
            @b.sync
            def _(s):
                for dk in range(16):
                    s.wait_ge(sems["ev"], e0 + dk + 1)
                    s.dma_start(out=x2_dram[dk * 128:(dk + 1) * 128, tt * 512:(tt + 1) * 512],
                                in_=bass.AP(r1_t, dk * 512, [[8192, 128], [1, 512]])).then_inc(sems["st"], 16)
                    cnt["st"] += 16
                s.wait_ge(sems["st"], cnt["st"])
            cnt["ev"] = e0 + 16
        barrier()

    for t2 in range(2 if KPH >= 3 else 0):
        tb = t2 * 1024
        with nc.Block() as b:
            @b.gpsimd
            def _(g):
                for dk in range(16):
                    g.dma_start(out=x2b_t[:, dk * 1024:(dk + 1) * 1024],
                                in_=x2_dram[dk * 128:(dk + 1) * 128, tb:tb + 1024]).then_inc(sems["ld"], 16)
                    cnt["ld"] += 16
                g.wait_ge(sems["ld"], cnt["ld"])
        barrier()
        with nc.Block() as b:
            m0 = cnt["mm"]
            @b.tensor
            def _(t):
                for tq in range(2):
                    ps = PXR if tq == 0 else PXI
                    for dk in range(16):
                        ins = t.matmul(ps[0:1, :], ones_t[:],
                                       x2b_t[:, dk * 1024 + tq * 512: dk * 1024 + tq * 512 + 512],
                                       start=(dk == 0), stop=(dk == 15))
                ins.then_inc(sems["mm"], 1)
            @b.vector
            def _(v):
                v.wait_ge(sems["mm"], m0 + 1)
                v.tensor_copy(sxb(0, 512), PXR[0:1, :])
                v.tensor_copy(sxb(512, 1024), PXI[0:1, :])
            cnt["mm"] = m0 + 1
        barrier()
        for hk in range(44):
            with nc.Block() as b:
                m0 = cnt["mm"]; e0 = cnt["ev"]; l0 = cnt["ld"]
                buf = 0
                @b.gpsimd
                def _(g):
                    ap_g = bass.AP(dr["wgT"].tensor, hk * 128, [[HID, 128], [128 * HID, 16], [1, 128]])
                    ap_u = bass.AP(dr["wuT"].tensor, hk * 128, [[HID, 128], [128 * HID, 16], [1, 128]])
                    g.dma_start(out=wg_sl[:, 0:2048], in_=ap_g).then_inc(sems["ld"], 16)
                    g.dma_start(out=wu_sl[:, 0:2048], in_=ap_u).then_inc(sems["ld"], 16)
                    cnt["ld"] += 32
                @b.tensor
                def _(t):
                    t.wait_ge(sems["ld"], l0 + 32)
                    if hk > 0:
                        t.wait_ge(sems["ev"], e0)
                    for tq in range(2):
                        psG = PA if tq == 0 else PXR
                        psU = PB if tq == 0 else PXI
                        xs = slice(0, 0)
                        for dk in range(16):
                            t.matmul(psG[:], wg_sl[:, dk * 128:(dk + 1) * 128],
                                     x2b_t[:, dk * 1024 + tq * 512: dk * 1024 + tq * 512 + 512],
                                     start=(dk == 0), stop=False)
                        ins = t.matmul(psG[:], negc_t[:], sxb(tq * 512, (tq + 1) * 512), start=False, stop=True)
                        ins.then_inc(sems["mm"], 1)
                        for dk in range(16):
                            t.matmul(psU[:], wu_sl[:, dk * 128:(dk + 1) * 128],
                                     x2b_t[:, dk * 1024 + tq * 512: dk * 1024 + tq * 512 + 512],
                                     start=(dk == 0), stop=False)
                        ins = t.matmul(psU[:], negc_t[:], sxb(tq * 512, (tq + 1) * 512), start=False, stop=True)
                        ins.then_inc(sems["mm"], 1)
                    cnt["mm"] = m0 + 4
                @b.scalar
                def _(sc):
                    for tq in range(2):
                        sc.wait_ge(sems["mm"], m0 + 2 * tq + 1)
                        psG = PA if tq == 0 else PXR
                        ins = sc.activation(r2b_t[:, tq * 512:(tq + 1) * 512], psG[:],
                                            ACT.Silu, scale=sg_t[:, hk:hk + 1])
                        ins.then_inc(sems["ev"], 1)
                @b.vector
                def _(v):
                    for tq in range(2):
                        v.wait_ge(sems["mm"], m0 + 2 * tq + 2)
                        psU = PB if tq == 0 else PXI
                        v.tensor_scalar(sqb_t[:, tq * 512:(tq + 1) * 512], psU[:],
                                        su_t[:, hk:hk + 1], None, AL.mult)
                        v.wait_ge(sems["ev"], e0 + tq + 1)
                        ins = v.tensor_tensor(z_t[:, hk * 1024 + tq * 512: hk * 1024 + (tq + 1) * 512],
                                              r2b_t[:, tq * 512:(tq + 1) * 512],
                                              sqb_t[:, tq * 512:(tq + 1) * 512], AL.mult)
                        ins.then_inc(sems["ev"], 1)
                    cnt["ev"] = e0 + 4
            barrier()
        with nc.Block() as b:
            m0 = cnt["mm"]
            @b.tensor
            def _(t):
                for tq in range(2):
                    ps = PXR if tq == 0 else PXI
                    for hk in range(44):
                        ins = t.matmul(ps[0:1, :], ones_t[:],
                                       z_t[:, hk * 1024 + tq * 512: hk * 1024 + tq * 512 + 512],
                                       start=(hk == 0), stop=(hk == 43))
                ins.then_inc(sems["mm"], 1)
            @b.vector
            def _(v):
                v.wait_ge(sems["mm"], m0 + 1)
                v.tensor_copy(szb(0, 512), PXR[0:1, :])
                v.tensor_copy(szb(512, 1024), PXI[0:1, :])
            cnt["mm"] = m0 + 1
        barrier()
        for dk in range(16):
            with nc.Block() as b:
                m0 = cnt["mm"]; e0 = cnt["ev"]; l0 = cnt["ld"]
                buf = 0
                @b.gpsimd
                def _(g):
                    ap_d = bass.AP(dr["wdT"].tensor, dk * 128, [[D, 128], [128 * D, 44], [1, 128]])
                    g.dma_start(out=wd_sl[:, 0:5632], in_=ap_d).then_inc(sems["ld"], 16)
                    cnt["ld"] += 16
                @b.sync
                def _(s):
                    s.dma_start(out=xr2_t[:], in_=x2_dram[dk * 128:(dk + 1) * 128, tb:tb + 1024]).then_inc(sems["ld"], 16)
                    cnt["ld"] += 16
                    for tq in range(2):
                        s.wait_ge(sems["ev"], e0 + tq + 1)
                        s.dma_start(out=x2_dram[dk * 128:(dk + 1) * 128, tb + tq * 512: tb + (tq + 1) * 512],
                                    in_=r2f_t[:, tq * 512:(tq + 1) * 512]).then_inc(sems["st"], 16)
                        cnt["st"] += 16
                    s.wait_ge(sems["st"], cnt["st"])
                @b.tensor
                def _(t):
                    t.wait_ge(sems["ld"], l0 + 32)
                    if dk > 0:
                        t.wait_ge(sems["ev"], e0)
                    for tq in range(2):
                        ps = PA if tq == 0 else PB
                        for hk in range(44):
                            t.matmul(ps[:], wd_sl[:, hk * 128:(hk + 1) * 128],
                                     z_t[:, hk * 1024 + tq * 512: hk * 1024 + tq * 512 + 512],
                                     start=(hk == 0), stop=False)
                        ins = t.matmul(ps[:], negc_t[:], szb(tq * 512, (tq + 1) * 512), start=False, stop=True)
                        ins.then_inc(sems["mm"], 1)
                    cnt["mm"] = m0 + 2
                @b.vector
                def _(v):
                    for tq in range(2):
                        v.wait_ge(sems["mm"], m0 + tq + 1)
                        ps = PA if tq == 0 else PB
                        v.tensor_scalar(r2f_t[:, tq * 512:(tq + 1) * 512], ps[:],
                                        sd_t[:, dk:dk + 1], None, AL.mult)
                        ins = v.tensor_tensor(r2f_t[:, tq * 512:(tq + 1) * 512],
                                              r2f_t[:, tq * 512:(tq + 1) * 512],
                                              xr2_t[:, tq * 512:(tq + 1) * 512], AL.add)
                        ins.then_inc(sems["ev"], 1)
                    cnt["ev"] = e0 + 2
            barrier()
        with nc.Block() as b:
            m0 = cnt["mm"]; e0 = cnt["ev"]; l0 = cnt["ld"]
            @b.sync
            def _(s):
                for dk in range(16):
                    if dk > 0:
                        s.wait_ge(sems["mm"], m0 + dk)
                    s.dma_start(out=xr2_t[:], in_=x2_dram[dk * 128:(dk + 1) * 128, tb:tb + 1024]).then_inc(sems["ld"], 16)
                    cnt["ld"] += 16
            @b.vector
            def _(v):
                for dk in range(16):
                    v.wait_ge(sems["ld"], l0 + 16 * (dk + 1))
                    ins = v.tensor_copy(r2b_t[:], xr2_t[:])
                    ins.then_inc(sems["ev"], 1)
            @b.scalar
            def _(sc):
                for dk in range(16):
                    sc.wait_ge(sems["ev"], e0 + 2 * dk + 1)
                    ins = sc.activation(sqb_t[:], r2b_t[:], ACT.Square)
                    ins.then_inc(sems["ev"], 1)
            @b.tensor
            def _(t):
                for dk in range(16):
                    t.wait_ge(sems["ev"], e0 + 2 * (dk + 1))
                    t.matmul(PXR[0:1, :], ones_t[:], r2b_t[:, 0:512], start=(dk == 0), stop=(dk == 15))
                    t.matmul(PXI[0:1, :], ones_t[:], r2b_t[:, 512:1024], start=(dk == 0), stop=(dk == 15))
                    t.matmul(PA[0:1, :], ones_t[:], sqb_t[:, 0:512], start=(dk == 0), stop=(dk == 15))
                    ins = t.matmul(PB[0:1, :], ones_t[:], sqb_t[:, 512:1024], start=(dk == 0), stop=(dk == 15))
                    ins.then_inc(sems["mm"], 1)
            cnt["mm"] = m0 + 16; cnt["ev"] = e0 + 32
        barrier()
        with nc.Block() as b:
            m0 = cnt["mm"]; e0 = cnt["ev"]
            @b.vector
            def _(v):
                v.tensor_scalar(srow(0, 512), PXR[0:1, :], 1.0 / D, None, AL.mult)
                v.tensor_scalar(srow(512, 1024), PXI[0:1, :], 1.0 / D, None, AL.mult)
                v.tensor_scalar(srow2(0, 512), PA[0:1, :], 1.0 / D, None, AL.mult)
                v.tensor_scalar(srow2(512, 1024), PB[0:1, :], 1.0 / D, None, AL.mult)
                v.tensor_tensor(vrow(0, 1024), srow(0, 1024), srow(0, 1024), AL.mult)
                v.tensor_tensor(vrow(0, 1024), srow2(0, 1024), vrow(0, 1024), AL.subtract)
                v.tensor_scalar(vrow(0, 1024), vrow(0, 1024), 1e-5, None, AL.add)
                ins = v.reciprocal(mrow(0, 1024), vrow(0, 1024))
                ins.then_inc(sems["ev"], 1)
                for j in range(4):
                    v.wait_ge(sems["mm"], m0 + j + 1)
                    dst = (mb2_t, mb2_t, ib2_t, ib2_t)[j]
                    ins = v.tensor_copy(dst[:, (j % 2) * 512:(j % 2) * 512 + 512], PR[:])
                    ins.then_inc(sems["ev"], 1)
            @b.scalar
            def _(sc):
                sc.wait_ge(sems["ev"], e0 + 1)
                ins = sc.activation(vrow(0, 1024), mrow(0, 1024), ACT.Sqrt)
                ins.then_inc(sems["ev"], 1)
            @b.tensor
            def _(t):
                srcs = (srow(0, 512), srow(512, 1024), vrow(0, 512), vrow(512, 1024))
                for j in range(4):
                    t.wait_ge(sems["ev"], e0 + 2 + j)
                    ins = t.matmul(PR[:], onesr_t[:], srcs[j], start=True, stop=True)
                    ins.then_inc(sems["mm"], 1)
            cnt["mm"] = m0 + 4; cnt["ev"] = e0 + 6
        barrier()
        with nc.Block() as b:
            e0 = cnt["ev"]; l0 = cnt["ld"]; s0 = cnt["st"]
            @b.sync
            def _(s):
                for dk in range(16):
                    if dk > 0:
                        s.wait_ge(sems["ev"], e0 + dk)
                    s.dma_start(out=xr2_t[:], in_=x2_dram[dk * 128:(dk + 1) * 128, tb:tb + 1024]).then_inc(sems["ld"], 16)
                    cnt["ld"] += 16
            @b.vector
            def _(v):
                for dk in range(16):
                    v.wait_ge(sems["ld"], l0 + 16 * (dk + 1))
                    if dk > 0:
                        v.wait_ge(sems["st"], s0 + 16 * dk)
                    v.tensor_tensor(r2f_t[:], xr2_t[:], mb2_t[:], AL.subtract)
                    ins = v.tensor_tensor(r2f_t[:], r2f_t[:], ib2_t[:], AL.mult)
                    ins.then_inc(sems["ev"], 1)
            @b.scalar
            def _(sc):
                for dk in range(16):
                    sc.wait_ge(sems["ev"], e0 + dk + 1)
                    sc.dma_start(out=out_t[dk * 128:(dk + 1) * 128, tb:tb + 1024],
                                 in_=r2f_t[:]).then_inc(sems["st"], 16)
                    cnt["st"] += 16
                sc.wait_ge(sems["st"], cnt["st"])
            cnt["ev"] = e0 + 16
        barrier()

    barrier()

    _NC_CACHE["nc"] = nc
    return nc


def _prep_shared(inputs):
    f32 = np.float32
    w = np.asarray(inputs["L0_attn_in_proj_w"], f32)
    WmixT = np.ascontiguousarray(
        w.reshape(3, NB, 256, D).transpose(1, 0, 2, 3).reshape(NB * 3 * 256, D).T)
    bmix = np.ascontiguousarray(
        np.asarray(inputs["L0_attn_in_proj_b"], f32).reshape(3, NB, 256)
        .transpose(1, 0, 2).reshape(48, 128).T)
    sw = np.asarray(inputs["L0_attn_short_w"], f32)[:, 0, :]
    swT = np.ascontiguousarray(
        sw.reshape(3, NB, 256, 3).transpose(1, 0, 2, 3).reshape(48, 128, 3)
        .transpose(1, 0, 2).reshape(128, 144))
    sbT = np.ascontiguousarray(
        np.asarray(inputs["L0_attn_short_b"], f32).reshape(3, NB, 256)
        .transpose(1, 0, 2).reshape(48, 128).T)
    fbias = np.ascontiguousarray(np.asarray(inputs["L0_attn_filter_bias"], f32).reshape(16, 128).T)
    WoutT = np.ascontiguousarray(np.asarray(inputs["L0_attn_out_proj_w"], f32).T)
    bout = np.ascontiguousarray(np.asarray(inputs["L0_attn_out_proj_b"], f32).reshape(16, 128).T)
    zT, dect = _consts_filter()
    dft = _consts_dft()

    def u8view(a):
        a = np.ascontiguousarray(np.asarray(a, np.int32))
        return a.view(np.uint8).reshape(a.shape[0], a.shape[1], 4)[..., 0]

    wgT = np.ascontiguousarray(u8view(inputs["L0_ffn_wq_gate"]).T)
    wuT = np.ascontiguousarray(u8view(inputs["L0_ffn_wq_up"]).T)
    wdT = np.ascontiguousarray(u8view(inputs["L0_ffn_wq_down"]).T)
    sg = np.ascontiguousarray(np.asarray(inputs["L0_ffn_s_gate"], f32).reshape(44, 128).T)
    su = np.ascontiguousarray(np.asarray(inputs["L0_ffn_s_up"], f32).reshape(44, 128).T)
    sd = np.ascontiguousarray(np.asarray(inputs["L0_ffn_s_down"], f32).reshape(16, 128).T)
    shared = {
        "WmixT": WmixT, "bmix": bmix, "swT": swT, "sbT": sbT, "fbias": fbias,
        "WoutT": WoutT, "bout": bout,
        "w0T": np.ascontiguousarray(np.asarray(inputs["L0_attn_mlp_w0"], f32).T),
        "w1T": np.ascontiguousarray(np.asarray(inputs["L0_attn_mlp_w1"], f32).T),
        "w2T": np.ascontiguousarray(np.asarray(inputs["L0_attn_mlp_w2"], f32).T),
        "w3T": np.ascontiguousarray(np.asarray(inputs["L0_attn_mlp_w3"], f32).T),
        "b0": np.asarray(inputs["L0_attn_mlp_b0"], f32).reshape(64, 1),
        "b1": np.asarray(inputs["L0_attn_mlp_b1"], f32).reshape(64, 1),
        "b2": np.asarray(inputs["L0_attn_mlp_b2"], f32).reshape(64, 1),
        "freq": np.ascontiguousarray(np.asarray(inputs["L0_attn_sin_freq"], f32).reshape(64, 1)),
        "zT": zT, "dect": dect,
        "wgT": wgT, "wuT": wuT, "wdT": wdT, "sg": sg, "su": su, "sd": sd,
        "negc": np.full((1, 128), -128.0, bf16),
    }
    shared.update(dft)
    return shared


def kernel(**inputs):
    x = np.asarray(inputs["L0_attn_x"], np.float32)
    shared = _prep_shared(inputs)
    nc = build_program()
    in_maps = []
    for b in range(8):
        m = dict(shared)
        m["xT"] = np.ascontiguousarray(x[b].T)
        in_maps.append(m)
    res = run_bass_kernel_spmd(nc, in_maps, core_ids=list(range(8)))
    out = np.empty((8, L, D), np.float32)
    for b in range(8):
        out[b] = res.results[b]["outT"].T
    return out


# version 05
# speedup vs baseline: 1.1997x; 1.1997x over previous
import math
import os
import numpy as np
import ml_dtypes
import concourse.bass as bass
import concourse.mybir as mybir
from concourse.bass_utils import run_bass_kernel_spmd

F32 = mybir.dt.float32
BF16 = mybir.dt.bfloat16
U8 = mybir.dt.uint8
I32 = mybir.dt.int32
AL = mybir.AluOpType
ACT = mybir.ActivationFunctionType
bf16 = ml_dtypes.bfloat16

L = 2048
D = 2048
HID = 5632
NB = 8
TWO_PI = 2.0 * math.pi

MB = 65536
SEG = 16384
OFF_P = [MB + i * SEG for i in range(7)]
OFF_B = MB + 7 * SEG
OFF_T = [OFF_B + i * 8192 for i in range(4)]
OFF_KVT = OFF_B + 4 * 8192


def _consts_dft():
    s_ = np.arange(32); p_ = np.arange(64); r_ = np.arange(128); q_ = np.arange(32)
    th1 = 2 * np.pi * (np.einsum('p,r->pr', p_, r_)[None] / 128
                       + np.einsum('s,r->sr', s_, r_)[:, None, :] / 4096)
    W1c, W1s = np.cos(th1), -np.sin(th1)
    W1R = np.zeros((128, 2048), np.float32); W1I = np.zeros((128, 2048), np.float32)
    for u in range(16):
        for phi in range(2):
            W1R[phi * 64:phi * 64 + 64, u * 128:u * 128 + 128] = W1c[2 * u + phi]
            W1I[phi * 64:phi * 64 + 64, u * 128:u * 128 + 128] = W1s[2 * u + phi]
    th2 = 2 * np.pi * np.einsum('s,q->sq', s_, q_) / 32
    W2re, W2im = np.cos(th2), -np.sin(th2)
    th3 = 2 * np.pi * np.einsum('q,s->qs', q_, s_) / 32
    W3re, W3im = np.cos(th3) / 4096.0, np.sin(th3) / 4096.0
    th4 = 2 * np.pi * (np.einsum('p,r->pr', p_, r_)[None] / 128
                       + np.einsum('s,r->sr', s_, r_)[:, None, :] / 4096)
    W4re, W4im = np.cos(th4), np.sin(th4)
    W4R = np.zeros((128, 2048), np.float32); W4IN = np.zeros((128, 2048), np.float32)
    for s in range(32):
        W4R[:, s * 64:s * 64 + 64] = W4re[s].T
        W4IN[:, s * 64:s * 64 + 64] = -W4im[s].T
    rep4 = lambda m: np.tile(m, (4, 1)).astype(np.float32)
    c = {"W1R": W1R, "W1I": W1I, "W4R": W4R, "W4IN": W4IN,
         "W2RE": rep4(W2re), "W2IM": rep4(W2im), "W2IMN": rep4(-W2im),
         "W3RE": rep4(W3re), "W3IM": rep4(W3im), "W3IMN": rep4(-W3im)}
    return {k: v.astype(bf16) for k, v in c.items()}


def _consts_filter():
    t = np.linspace(0.0, 1.0, L)[:, None]
    t_resc = np.linspace(0.0, L - 1.0, L)
    w = 2.0 * np.pi * t_resc / L
    f = np.linspace(0.0001, 0.0, 1)
    ang = f[None, :] * w[:, None]
    z = np.concatenate([t, np.cos(ang), -np.sin(ang)], axis=-1)
    zT = np.ascontiguousarray(z.T).astype(bf16)
    deltas = np.abs(np.linspace(np.log(0.01) / 1.5, np.log(0.01) / 0.3, D))
    decay = np.exp(-t * deltas[None, :]).astype(np.float32)
    dect = np.zeros((128, NB * 4096), np.float32)
    pp = np.arange(64)
    for b in range(NB):
        for u in range(16):
            for phi in range(2):
                tv = 2 * u + phi + 32 * pp
                dect[phi * 64:phi * 64 + 64, b * 4096 + u * 256:b * 4096 + (u + 1) * 256] = \
                    decay[tv][:, b * 256:(b + 1) * 256]
    return zT, dect.astype(bf16)


_NC_CACHE = {}


def build_program():
    if "nc" in _NC_CACHE:
        return _NC_CACHE["nc"]
    nc = bass.Bass("TRN2", target_bir_lowering=False)

    dr = {}
    def din(name, shape, dtype=F32):
        dr[name] = nc.dram_tensor(name, list(shape), dtype, kind="ExternalInput").ap()

    din("xT", [D, L])
    din("WmixT", [D, 3 * D])
    din("bmix", [128, 48]); din("swT", [128, 144]); din("sbT", [128, 48])
    din("fbias", [128, 16])
    din("WoutT", [16, 128, 2048]); din("bout", [128, 16])
    din("w0T", [3, 64]); din("w1T", [64, 64]); din("w2T", [64, 64]); din("w3T", [64, D])
    din("b0", [64, 1]); din("b1", [64, 1]); din("b2", [64, 1]); din("freq", [64, 1])
    din("zT", [3, L], BF16)
    din("dect", [128, NB * 4096], BF16)
    for n in ("W1R", "W1I", "W4R", "W4IN"):
        din(n, [128, 2048], BF16)
    for n in ("W2RE", "W2IM", "W2IMN", "W3RE", "W3IM", "W3IMN"):
        din(n, [128, 32], BF16)
    din("wgT", [44, 128, 2048], U8); din("wuT", [44, 128, 2048], U8)
    din("wdT", [16, 128, HID], U8)
    din("sg", [128, 44]); din("su", [128, 44]); din("sd", [128, 16])
    din("negc", [1, 128], BF16)

    out_t = nc.dram_tensor("outT", [D, L], F32, kind="ExternalOutput").ap()
    y_dram = nc.dram_tensor("Yscr", [D, L], BF16, kind="Internal").ap()
    x2_dram = nc.dram_tensor("X2scr", [D, L], F32, kind="Internal").ap()

    def sbn(name, shape, dtype=BF16):
        return nc.alloc_sbuf_tensor(name, list(shape), dtype)

    C = {}
    for n in ("W1R", "W1I", "W4R", "W4IN"):
        C[n] = sbn("C" + n, [128, 2048])
    for n in ("W2RE", "W2IM", "W2IMN", "W3RE", "W3IM", "W3IMN"):
        C[n] = sbn("C" + n, [128, 32])
    bmix_t = sbn("bmix_t", [128, 48], F32)
    swT_t = sbn("swT_t", [128, 144], F32)
    sbT_t = sbn("sbT_t", [128, 48], F32)
    fbias_t = sbn("fbias_t", [128, 16], F32)
    bout_t = sbn("bout_t", [128, 16], F32)
    sg_t = sbn("sg_t", [128, 44], F32); su_t = sbn("su_t", [128, 44], F32)
    sd_t = sbn("sd_t", [128, 16], F32)
    w0T_t = sbn("w0T_t", [3, 64]); w1T_t = sbn("w1T_t", [64, 64]); w2T_t = sbn("w2T_t", [64, 64])
    w3T_t = sbn("w3T_t", [64, D])
    b0_t = sbn("b0_t", [64, 1], F32); b1_t = sbn("b1_t", [64, 1], F32)
    b2_t = sbn("b2_t", [64, 1], F32); fq_t = sbn("fq_t", [64, 1], F32)
    hhp_t = sbn("hhp_t", [64, L])
    ones_t = sbn("ones_t", [128, 1])
    onesr_t = sbn("onesr_t", [1, 128], F32)
    negc_t = sbn("negc_t", [1, 128])
    rows_t = sbn("rows_t", [1, 4096], F32)
    sxzb_t = sbn("sxzb_t", [1, 2048])

    def srow(a, b_):
        return rows_t[0:1, a:b_]
    def srow2(a, b_):
        return rows_t[0:1, 1024 + a:1024 + b_]
    def mrow(a, b_):
        return rows_t[0:1, 2048 + a:2048 + b_]
    def vrow(a, b_):
        return rows_t[0:1, 3072 + a:3072 + b_]
    def sxb(a, b_):
        return sxzb_t[0:1, a:b_]
    def szb(a, b_):
        return sxzb_t[0:1, 1024 + a:1024 + b_]
    assert nc.sbuf_bytes_remaining > (229344 - MB), nc.sbuf_bytes_remaining

    def sbm(name, offset, shape, dtype=BF16):
        return nc.alloc_sbuf_tensor_at(name, list(shape), dtype, offset=offset, align_bytes=64)

    P = [sbm(f"P{i+1}", OFF_P[i], [128, 8192]) for i in range(7)]
    T0 = sbm("T0", OFF_T[0], [128, 4096]); T1 = sbm("T1", OFF_T[1], [128, 4096])
    T2 = sbm("T2", OFF_T[2], [128, 4096]); T3 = sbm("T3", OFF_T[3], [128, 4096])
    KVT = sbm("KVT", OFF_KVT, [128, 4096])
    arg_t = sbm("arg_t", OFF_P[0], [64, 2048], F32)
    un_t = sbm("un_t", OFF_P[1], [64, 2048], F32)
    ni_t = sbm("ni_t", OFF_P[2], [64, 2048], I32)
    nf_t = sbm("nf_t", OFF_P[3], [64, 2048], F32)
    hh_t = sbm("hh_t", OFF_P[4], [64, 2048])
    zT_t = sbm("zT_t", OFF_P[5], [3, L])
    CtR, CtI = P[0], P[1]
    TCR, TCI = P[2], P[3]
    XRE, XIM = P[0], P[1]
    KRE, KIM = P[4], P[5]
    S1 = P[6]
    TER, TEI = P[4], P[5]
    ype = sbm("ype", OFF_P[6], [128, 4096])
    yperm = sbm("yperm", OFF_P[6] + 8192, [128, 4096])
    dect_b = sbm("dect_b", OFF_P[6], [128, 4096])
    xtile = P[0]
    wmixb = sbm("wmixb", OFF_P[1], [128, 12288])
    wo_sl = sbm("wo_sl", OFF_P[0], [128, 4096])
    ytile = sbm("ytile", OFF_P[4], [128, 8192])
    r1_t = sbm("r1_t", OFF_P[5], [128, 8192], F32)
    r1b_t = sbm("r1b_t", OFF_P[1], [128, 8192])
    sq_t = sbm("sq_t", OFF_P[2], [128, 8192])
    z_t = sbm("z_t", MB, [128, 45056])
    x2b_t = sbm("x2b_t", MB + 90112, [128, 16384])
    wd_sl = sbm("wd_sl", MB + 122880, [128, 5632])
    wg_sl = sbm("wg_sl", MB + 134144, [128, 4096])
    wu_sl = sbm("wu_sl", MB + 142336, [128, 4096])
    xr2_t = sbm("xr2_t", MB + 150528, [128, 1024], F32)
    r2f_t = sbm("r2f_t", MB + 154624, [128, 1024], F32)
    mb2_t = sbm("mb2_t", MB + 134144, [128, 1024], F32)
    ib2_t = sbm("ib2_t", MB + 138240, [128, 1024], F32)
    r2b_t = sbm("r2b_t", MB + 158720, [128, 1024])
    sqb_t = sbm("sqb_t", MB + 160768, [128, 1024])

    mb_t = mb2_t
    ib_t = ib2_t
    xr_t = xr2_t

    PS = []
    def psum(name, shape):
        g = nc.psum_tensor(name, list(shape), F32)
        t = g.__enter__(); PS.append(g)
        return t
    PA = psum("PA", [128, 512])
    PB = psum("PB", [128, 512])
    PCe = psum("PCe", [128, 256])
    PCo = psum("PCo", [128, 256])
    PXR = psum("PXR", [128, 512])
    PXI = psum("PXI", [128, 512])
    P2m = psum("P2m", [128, 256])
    PR = psum("PR", [128, 512])

    sems = {}
    for nm in ("ld", "mm", "ev", "tp", "st", "bar"):
        g = nc.semaphore("s_" + nm)
        sems[nm] = g.__enter__()
    cnt = {"ld": 0, "mm": 0, "ev": 0, "tp": 0, "st": 0, "bar": 0}
    ENG = ("sync", "scalar", "vector", "tensor", "gpsimd")

    def barrier():
        cnt["bar"] += 1
        tgt = 5 * cnt["bar"]
        with nc.Block() as b:
            for e in ENG:
                def mk(e=e):
                    def f(eng):
                        eng.sem_inc(sems["bar"], 1)
                        eng.wait_ge(sems["bar"], tgt)
                    return f
                getattr(b, e)(mk())

    with nc.Block() as b:
        @b.gpsimd
        def _(g):
            for nm, t in (("bmix", bmix_t), ("swT", swT_t), ("sbT", sbT_t), ("fbias", fbias_t),
                          ("bout", bout_t), ("sg", sg_t), ("su", su_t), ("sd", sd_t),
                          ("b0", b0_t), ("b1", b1_t), ("b2", b2_t), ("freq", fq_t),
                          ("w0T", w0T_t), ("w1T", w1T_t), ("w2T", w2T_t), ("w3T", w3T_t)):
                g.dma_start(out=t[:], in_=dr[nm][:]).then_inc(sems["ld"], 16)
                cnt["ld"] += 16
            g.memset(ones_t[:], 1.0)
            g.memset(onesr_t[:], 1.0)
        @b.sync
        def _(s):
            for nm, t in (("zT", zT_t), ("negc", negc_t),
                          ("W1R", C["W1R"]), ("W1I", C["W1I"]), ("W4R", C["W4R"]), ("W4IN", C["W4IN"]),
                          ("W2RE", C["W2RE"]), ("W2IM", C["W2IM"]), ("W2IMN", C["W2IMN"]),
                          ("W3RE", C["W3RE"]), ("W3IM", C["W3IM"]), ("W3IMN", C["W3IMN"])):
                s.dma_start(out=t[:], in_=dr[nm][:]).then_inc(sems["ld"], 16)
                cnt["ld"] += 16
            s.wait_ge(sems["ld"], cnt["ld"])
    barrier()

    def mlp_layer(lhsT, rhs_q, bias):
        with nc.Block() as b:
            m0 = cnt["mm"]; e0 = cnt["ev"]
            @b.tensor
            def _(t):
                for j in range(4):
                    if j > 0:
                        t.wait_ge(sems["ev"], e0 + j)
                    ins = t.matmul(PA[:64, :], lhsT, rhs_q[j], start=True, stop=True)
                    ins.then_inc(sems["mm"], 1)
            @b.vector
            def _(v):
                for j in range(4):
                    v.wait_ge(sems["mm"], m0 + j + 1)
                    cols = slice(j * 512, (j + 1) * 512)
                    v.tensor_scalar(arg_t[:, cols], PA[:64, :], bias[:], fq_t[:], AL.add, AL.mult)
                    v.tensor_scalar(un_t[:, cols], arg_t[:, cols], 1.0 / TWO_PI, None, AL.mult)
                    v.tensor_copy(ni_t[:, cols], un_t[:, cols])
                    v.tensor_copy(nf_t[:, cols], ni_t[:, cols])
                    ins = v.scalar_tensor_tensor(arg_t[:, cols], nf_t[:, cols], -TWO_PI,
                                                 arg_t[:, cols], AL.mult, AL.add)
                    ins.then_inc(sems["ev"], 1)
            @b.scalar
            def _(sc):
                for j in range(4):
                    sc.wait_ge(sems["ev"], e0 + j + 1)
                    cols = slice(j * 512, (j + 1) * 512)
                    sc.activation(hh_t[:, cols], arg_t[:, cols], ACT.Sin)
            cnt["mm"] = m0 + 4; cnt["ev"] = e0 + 4
        barrier()

    mlp_layer(w0T_t[:], [zT_t[:, j * 512:(j + 1) * 512] for j in range(4)], b0_t)
    mlp_layer(w1T_t[:], [hh_t[:, j * 512:(j + 1) * 512] for j in range(4)], b1_t)
    mlp_layer(w2T_t[:], [hh_t[:, j * 512:(j + 1) * 512] for j in range(4)], b2_t)
    with nc.Block() as b:
        @b.vector
        def _(v):
            ap_in = bass.AP(hh_t, 0, [[L, 64], [1, 32], [32, 64]])
            v.tensor_copy(hhp_t[:].rearrange("a (s p) -> a s p", s=32), ap_in)
    barrier()

    def F1(srcT):
        with nc.Block() as b:
            m0 = cnt["mm"]; e0 = cnt["ev"]
            @b.tensor
            def _(t):
                nm = 0
                for h in range(2):
                    for u in range(16):
                        if nm >= 1:
                            t.wait_ge(sems["ev"], e0 + 4 * nm)
                        rhsE = srcT[0:64, h * 2048 + u * 128: h * 2048 + (u + 1) * 128]
                        rhsO = srcT[64:128, h * 2048 + u * 128: h * 2048 + (u + 1) * 128]
                        wre = C["W1R"][:, u * 128:(u + 1) * 128]
                        wim = C["W1I"][:, u * 128:(u + 1) * 128]
                        t.matmul(PCe[:, 0:128], wre[0:64, :], rhsE, start=True, stop=True, tile_position=(0, 0))
                        t.matmul(PCe[:, 128:256], wim[0:64, :], rhsE, start=True, stop=True, tile_position=(0, 0))
                        t.matmul(PCo[:, 0:128], wre[64:128, :], rhsO, start=True, stop=True, tile_position=(64, 0))
                        ins = t.matmul(PCo[:, 128:256], wim[64:128, :], rhsO, start=True, stop=True, tile_position=(64, 0))
                        nm += 1
                        ins.then_inc(sems["mm"], 1)
                cnt["mm"] = m0 + nm
            @b.vector
            def _(v):
                ne = 0
                for h in range(2):
                    for u in range(16):
                        v.wait_ge(sems["mm"], m0 + h * 16 + u + 1)
                        for pi, dst in ((0, CtR), (1, CtI)):
                            for so, pcsrc in ((0, PCe), (1, PCo)):
                                ap_in = bass.AP(pcsrc, pi * 128, [[256, 128], [1, 128]])
                                ap_out = bass.AP(dst, (h * 32) * 128 + 2 * u + so,
                                                 [[8192, 128], [128, 32], [32, 4]])
                                ins = v.tensor_copy(ap_out, ap_in)
                                ne += 1
                                ins.then_inc(sems["ev"], 1)
                cnt["ev"] = e0 + ne
        barrier()

    def turn(srcR, srcI, dstR, dstI):
        with nc.Block() as b:
            @b.sync
            def _(s):
                for sr, ds in ((srcR, dstR), (srcI, dstI)):
                    s.dma_start_transpose(
                        ds[:].rearrange("a (g r) -> a g r", g=64),
                        sr[:].rearrange("a (g f) -> a g f", g=64)).then_inc(sems["tp"], 16)
                    cnt["tp"] += 16
                s.wait_ge(sems["tp"], cnt["tp"])
        barrier()

    def F2(WRE, WIM, WIMN, srcR, srcI, dstRE, dstIM):
        with nc.Block() as b:
            m0 = cnt["mm"]; e0 = cnt["ev"]
            @b.tensor
            def _(t):
                nm = 0
                for blk in range(16):
                    if nm >= 1:
                        t.wait_ge(sems["ev"], e0 + 2 * nm)
                    cols = slice(blk * 512, blk * 512 + 512)
                    for c4 in range(4):
                        band = slice(32 * c4, 32 * c4 + 32)
                        tpos = (32 * c4, 32 * c4)
                        t.matmul(PXR[band, :], C[WRE][band, :], srcR[band, cols], start=True, stop=False, tile_position=tpos)
                        t.matmul(PXR[band, :], C[WIMN][band, :], srcI[band, cols], start=False, stop=True, tile_position=tpos)
                        t.matmul(PXI[band, :], C[WIM][band, :], srcR[band, cols], start=True, stop=False, tile_position=tpos)
                        ins = t.matmul(PXI[band, :], C[WRE][band, :], srcI[band, cols], start=False, stop=True, tile_position=tpos)
                    nm += 1
                    ins.then_inc(sems["mm"], 1)
                cnt["mm"] = m0 + nm
            @b.vector
            def _(v):
                ne = 0
                for blk in range(16):
                    v.wait_ge(sems["mm"], m0 + blk + 1)
                    cols = slice(blk * 512, blk * 512 + 512)
                    v.tensor_copy(dstRE[:, cols], PXR[:])
                    ins = v.tensor_copy(dstIM[:, cols], PXI[:])
                    ne += 2
                    ins.then_inc(sems["ev"], 2)
                cnt["ev"] = e0 + ne
        barrier()

    def conv_batch(bi):
        with nc.Block() as b:
            @b.sync
            def _(s):
                s.dma_start(out=dect_b[:], in_=dr["dect"][:, bi * 4096:(bi + 1) * 4096]).then_inc(sems["tp"], 16)
                cnt["tp"] += 16
                s.wait_ge(sems["tp"], cnt["tp"])
        barrier()
        with nc.Block() as b:
            m0 = cnt["mm"]; e0 = cnt["ev"]
            @b.tensor
            def _(t):
                for u in range(16):
                    if u > 0:
                        t.wait_ge(sems["ev"], e0 + u)
                    ins = t.matmul(PA[:, 0:256], hhp_t[:, u * 128:(u + 1) * 128],
                                   w3T_t[:, bi * 256:(bi + 1) * 256], start=True, stop=True)
                    ins.then_inc(sems["mm"], 1)
            @b.vector
            def _(v):
                for u in range(16):
                    v.wait_ge(sems["mm"], m0 + u + 1)
                    for h in range(2):
                        ins = v.tensor_tensor(
                            KVT[:, h * 2048 + u * 128: h * 2048 + u * 128 + 128],
                            PA[:, h * 128:(h + 1) * 128],
                            dect_b[:, u * 256 + h * 128: u * 256 + (h + 1) * 128], AL.mult)
                    ins.then_inc(sems["ev"], 1)
            cnt["mm"] = m0 + 16; cnt["ev"] = e0 + 16
        barrier()
        F1(KVT)
        turn(CtR, CtI, TCR, TCI)
        F2("W2RE", "W2IM", "W2IMN", TCR, TCI, KRE, KIM)

        with nc.Block() as b:
            l0 = cnt["ld"]
            @b.gpsimd
            def _(g):
                for dk in range(16):
                    g.dma_start(out=wmixb[:, dk * 768:(dk + 1) * 768],
                                in_=dr["WmixT"][dk * 128:(dk + 1) * 128, bi * 768:(bi + 1) * 768]).then_inc(sems["ld"], 16)
                    cnt["ld"] += 16
                g.wait_ge(sems["ld"], cnt["ld"])
        barrier()
        for tt in range(4):
            with nc.Block() as b:
                m0 = cnt["mm"]; e0 = cnt["ev"]; l0 = cnt["ld"]
                @b.gpsimd
                def _(g):
                    for dk in range(16):
                        g.dma_start(out=xtile[:, dk * 512:(dk + 1) * 512],
                                    in_=dr["xT"][dk * 128:(dk + 1) * 128, tt * 512:(tt + 1) * 512]).then_inc(sems["ld"], 16)
                        cnt["ld"] += 16
                @b.tensor
                def _(t):
                    t.wait_ge(sems["ld"], l0 + 256)
                    for rc in range(6):
                        if rc >= 2:
                            t.wait_ge(sems["ev"], e0 + rc - 1)
                        ps = PA if rc % 2 == 0 else PB
                        for dk in range(16):
                            ins = t.matmul(ps[:], wmixb[:, dk * 768 + rc * 128: dk * 768 + (rc + 1) * 128],
                                           xtile[:, dk * 512:(dk + 1) * 512],
                                           start=(dk == 0), stop=(dk == 15))
                        ins.then_inc(sems["mm"], 1)
                @b.scalar
                def _(sc):
                    for rc in range(6):
                        sc.wait_ge(sems["mm"], m0 + rc + 1)
                        ps = PA if rc % 2 == 0 else PB
                        role, half = rc // 2, rc % 2
                        dstT = (T0, T1, T2)[role]
                        ins = sc.activation(dstT[:, half * 2048 + tt * 512: half * 2048 + (tt + 1) * 512],
                                            ps[:], ACT.Identity, bias=bmix_t[:, bi * 6 + rc: bi * 6 + rc + 1])
                        ins.then_inc(sems["ev"], 1)
                cnt["mm"] = m0 + 6; cnt["ev"] = e0 + 6
            barrier()
        with nc.Block() as b:
            @b.vector
            def _(v):
                for (src, dst, role) in ((T0, T3, 0), (T1, T0, 1), (T2, T1, 2)):
                    for h in range(2):
                        j = bi * 6 + role * 2 + h
                        cs = h * 2048
                        w0s = swT_t[:, j * 3 + 0: j * 3 + 1]
                        w1s = swT_t[:, j * 3 + 1: j * 3 + 2]
                        w2s = swT_t[:, j * 3 + 2: j * 3 + 3]
                        v.tensor_scalar(dst[:, cs:cs + 2048], src[:, cs:cs + 2048],
                                        w2s, sbT_t[:, j:j + 1], AL.mult, AL.add)
                        v.scalar_tensor_tensor(dst[:, cs + 1:cs + 2048], src[:, cs:cs + 2047],
                                               w1s, dst[:, cs + 1:cs + 2048], AL.mult, AL.add)
                        v.scalar_tensor_tensor(dst[:, cs + 2:cs + 2048], src[:, cs:cs + 2046],
                                               w0s, dst[:, cs + 2:cs + 2048], AL.mult, AL.add)
                v.tensor_tensor(T1[:], T1[:], T0[:], AL.mult)
                for h in range(2):
                    ap_in = bass.AP(T1, h * 2048, [[4096, 128], [1, 32], [32, 64]])
                    out_ap = bass.AP(T2, h * 2048, [[4096, 128], [64, 32], [1, 64]])
                    v.tensor_copy(out_ap, ap_in)
        barrier()
        with nc.Block() as b:
            @b.sync
            def _(s):
                s.dma_start_transpose(
                    KVT[:].rearrange("a (t f) -> a t f", t=32),
                    T2[:].rearrange("a (t f) -> a t f", t=32)).then_inc(sems["tp"], 16)
                cnt["tp"] += 16
                s.wait_ge(sems["tp"], cnt["tp"])
        barrier()
        F1(KVT)
        turn(CtR, CtI, TCR, TCI)
        F2("W2RE", "W2IM", "W2IMN", TCR, TCI, XRE, XIM)
        with nc.Block() as b:
            @b.vector
            def _(v):
                v.tensor_tensor(S1[:], XRE[:], KIM[:], AL.mult)
                v.tensor_tensor(XRE[:], XRE[:], KRE[:], AL.mult)
                v.tensor_tensor(KRE[:], XIM[:], KRE[:], AL.mult)
                v.tensor_tensor(XIM[:], XIM[:], KIM[:], AL.mult)
                v.tensor_tensor(XRE[:], XRE[:], XIM[:], AL.subtract)
                v.tensor_tensor(S1[:], S1[:], KRE[:], AL.add)
        barrier()
        F2("W3RE", "W3IM", "W3IMN", XRE, S1, TCR, TCI)
        turn(TCR, TCI, TER, TEI)
        with nc.Block() as b:
            m0 = cnt["mm"]; e0 = cnt["ev"]
            @b.tensor
            def _(t):
                for u in range(16):
                    if u > 0:
                        t.wait_ge(sems["ev"], e0 + u)
                    for phi in range(2):
                        s = 2 * u + phi
                        rhsR = bass.AP(TER, s, [[8192, 128], [128, 64], [32, 4]])
                        rhsI = bass.AP(TEI, s, [[8192, 128], [128, 64], [32, 4]])
                        tpos = (0, 64 * phi)
                        outp = P2m[64 * phi:64 * phi + 64, :]
                        t.matmul(outp, C["W4R"][:, s * 64:s * 64 + 64], rhsR, start=True, stop=False, tile_position=tpos)
                        ins = t.matmul(outp, C["W4IN"][:, s * 64:s * 64 + 64], rhsI, start=False, stop=True, tile_position=tpos)
                    ins.then_inc(sems["mm"], 1)
            @b.vector
            def _(v):
                for u in range(16):
                    v.wait_ge(sems["mm"], m0 + u + 1)
                    v.tensor_copy(bass.AP(ype, u * 128, [[4096, 128], [1, 128]]), P2m[:, 0:128])
                    ins = v.tensor_copy(bass.AP(ype, 2048 + u * 128, [[4096, 128], [1, 128]]), P2m[:, 128:256])
                    ins.then_inc(sems["ev"], 1)
            cnt["mm"] = m0 + 16; cnt["ev"] = e0 + 16
        barrier()
        with nc.Block() as b:
            @b.sync
            def _(s):
                s.dma_start_transpose(
                    yperm[:].rearrange("a (t f) -> a t f", t=32),
                    ype[:].rearrange("a (t f) -> a t f", t=32)).then_inc(sems["tp"], 16)
                cnt["tp"] += 16
                s.wait_ge(sems["tp"], cnt["tp"])
        barrier()
        with nc.Block() as b:
            e0 = cnt["ev"]
            @b.vector
            def _(v):
                for h in range(2):
                    ap_y = bass.AP(yperm, h * 2048, [[4096, 128], [1, 64], [128, 16], [64, 2]])
                    ap_g1o = bass.AP(T2, h * 2048, [[4096, 128], [32, 64], [2, 16], [1, 2]])
                    fb = fbias_t[:, bi * 2 + h: bi * 2 + h + 1]
                    vt_ap = bass.AP(T1, h * 2048, [[4096, 128], [32, 64], [2, 16], [1, 2]])
                    v.scalar_tensor_tensor(ap_g1o, vt_ap, fb, ap_y, AL.mult, AL.add)
                    ins = v.tensor_tensor(T0[:, h * 2048:(h + 1) * 2048],
                                          T2[:, h * 2048:(h + 1) * 2048],
                                          T3[:, h * 2048:(h + 1) * 2048], AL.mult)
                    ins.then_inc(sems["ev"], 1)
            @b.sync
            def _(s):
                for h in range(2):
                    s.wait_ge(sems["ev"], e0 + h + 1)
                    s.dma_start(out=y_dram[(bi * 2 + h) * 128:(bi * 2 + h + 1) * 128, :],
                                in_=T0[:, h * 2048:(h + 1) * 2048]).then_inc(sems["st"], 16)
                    cnt["st"] += 16
                s.wait_ge(sems["st"], cnt["st"])
            cnt["ev"] = e0 + 2
        barrier()

    KPH = int(os.environ.get("KPH", "99"))
    KNB = int(os.environ.get("KNB", str(NB)))
    if KPH >= 1:
        for bi in range(KNB):
            conv_batch(bi)

    for tt in range(4 if KPH >= 2 else 0):
        with nc.Block() as b:
            @b.sync
            def _(s):
                for ck in range(16):
                    s.dma_start(out=ytile[:, ck * 512:(ck + 1) * 512],
                                in_=y_dram[ck * 128:(ck + 1) * 128, tt * 512:(tt + 1) * 512]).then_inc(sems["tp"], 16)
                    cnt["tp"] += 16
                s.wait_ge(sems["tp"], cnt["tp"])
        barrier()
        with nc.Block() as b:
            m0 = cnt["mm"]; e0 = cnt["ev"]; l0 = cnt["ld"]; t0_ = cnt["tp"]
            @b.gpsimd
            def _(g):
                for dk in range(16):
                    if dk >= 2:
                        g.wait_ge(sems["mm"], m0 + dk - 1)
                    buf = dk % 2
                    g.dma_start(out=wo_sl[:, buf * 2048:(buf + 1) * 2048],
                                in_=bass.AP(dr["WoutT"].tensor, dk * 128 * 2048, [[2048, 128], [1, 2048]])).then_inc(sems["ld"], 16)
                    cnt["ld"] += 16
            @b.sync
            def _(s):
                for dk in range(16):
                    if dk >= 2:
                        s.wait_ge(sems["ev"], e0 + 2 * (dk - 1))
                    s.dma_start(out=xr2_t[:, (dk % 2) * 512:(dk % 2) * 512 + 512],
                                in_=dr["xT"][dk * 128:(dk + 1) * 128, tt * 512:(tt + 1) * 512]).then_inc(sems["tp"], 16)
                    cnt["tp"] += 16
            @b.tensor
            def _(t):
                for dk in range(16):
                    buf = dk % 2
                    t.wait_ge(sems["ld"], l0 + 16 * (dk + 1))
                    if dk >= 2:
                        t.wait_ge(sems["ev"], e0 + 2 * (dk - 1))
                    ps = PA if dk % 2 == 0 else PB
                    for ck in range(16):
                        ins = t.matmul(ps[:], wo_sl[:, buf * 2048 + ck * 128: buf * 2048 + (ck + 1) * 128],
                                       ytile[:, ck * 512:(ck + 1) * 512],
                                       start=(ck == 0), stop=(ck == 15))
                    ins.then_inc(sems["mm"], 1)
            @b.scalar
            def _(sc):
                for dk in range(16):
                    sc.wait_ge(sems["mm"], m0 + dk + 1)
                    ps = PA if dk % 2 == 0 else PB
                    ins = sc.activation(bass.AP(r1_t, dk * 512, [[8192, 128], [1, 512]]),
                                        ps[:], ACT.Identity, bias=bout_t[:, dk:dk + 1])
                    ins.then_inc(sems["ev"], 1)
            @b.vector
            def _(v):
                for dk in range(16):
                    v.wait_ge(sems["ev"], e0 + 2 * dk + 1)
                    v.wait_ge(sems["tp"], t0_ + 16 * (dk + 1))
                    r1c = bass.AP(r1_t, dk * 512, [[8192, 128], [1, 512]])
                    v.tensor_tensor(r1c, r1c, xr2_t[:, (dk % 2) * 512:(dk % 2) * 512 + 512], AL.add)
                    ins = v.tensor_copy(bass.AP(r1b_t, dk * 512, [[8192, 128], [1, 512]]), r1c)
                    ins.then_inc(sems["ev"], 1)
            cnt["mm"] = m0 + 16; cnt["ev"] = e0 + 32
        barrier()
        with nc.Block() as b:
            @b.scalar
            def _(sc):
                for dk in range(16):
                    sc.activation(bass.AP(sq_t, dk * 512, [[8192, 128], [1, 512]]),
                                  bass.AP(r1_t, dk * 512, [[8192, 128], [1, 512]]), ACT.Square)
        barrier()
        with nc.Block() as b:
            m0 = cnt["mm"]; e0 = cnt["ev"]
            @b.tensor
            def _(t):
                for ck in range(16):
                    t.matmul(PXR[0:1, :], ones_t[:], bass.AP(r1b_t, ck * 512, [[8192, 128], [1, 512]]),
                             start=(ck == 0), stop=(ck == 15))
                for ck in range(16):
                    ins = t.matmul(PXI[0:1, :], ones_t[:], bass.AP(sq_t, ck * 512, [[8192, 128], [1, 512]]),
                                   start=(ck == 0), stop=(ck == 15))
                ins.then_inc(sems["mm"], 1)
                t.wait_ge(sems["ev"], e0 + 2)
                ins = t.matmul(PR[:], onesr_t[:], srow(0, 512), start=True, stop=True)
                ins.then_inc(sems["mm"], 1)
                t.wait_ge(sems["ev"], e0 + 3)
                ins = t.matmul(PR[:], onesr_t[:], vrow(0, 512), start=True, stop=True)
                ins.then_inc(sems["mm"], 1)
            @b.vector
            def _(v):
                v.wait_ge(sems["mm"], m0 + 1)
                v.tensor_scalar(srow(0, 512), PXR[0:1, :], 1.0 / D, None, AL.mult)
                v.tensor_scalar(srow2(0, 512), PXI[0:1, :], 1.0 / D, None, AL.mult)
                v.tensor_tensor(vrow(0, 512), srow(0, 512), srow(0, 512), AL.mult)
                v.tensor_tensor(vrow(0, 512), srow2(0, 512), vrow(0, 512), AL.subtract)
                v.tensor_scalar(vrow(0, 512), vrow(0, 512), 1e-5, None, AL.add)
                ins = v.reciprocal(mrow(0, 512), vrow(0, 512))
                ins.then_inc(sems["ev"], 1)
                v.wait_ge(sems["mm"], m0 + 2)
                ins = v.tensor_copy(mb_t[:, 0:512], PR[:])
                ins.then_inc(sems["ev"], 1)
                v.wait_ge(sems["mm"], m0 + 3)
                ins = v.tensor_copy(ib_t[:, 0:512], PR[:])
                ins.then_inc(sems["ev"], 1)
            @b.scalar
            def _(sc):
                sc.wait_ge(sems["ev"], e0 + 1)
                ins = sc.activation(vrow(0, 512), mrow(0, 512), ACT.Sqrt)
                ins.then_inc(sems["ev"], 1)
            cnt["mm"] = m0 + 3; cnt["ev"] = e0 + 4
        barrier()
        with nc.Block() as b:
            e0 = cnt["ev"]
            @b.vector
            def _(v):
                for dk in range(16):
                    r1c = bass.AP(r1_t, dk * 512, [[8192, 128], [1, 512]])
                    v.tensor_tensor(r1c, r1c, mb_t[:, 0:512], AL.subtract)
                    ins = v.tensor_tensor(r1c, r1c, ib_t[:, 0:512], AL.mult)
                    ins.then_inc(sems["ev"], 1)
            @b.sync
            def _(s):
                for dk in range(16):
                    s.wait_ge(sems["ev"], e0 + dk + 1)
                    s.dma_start(out=x2_dram[dk * 128:(dk + 1) * 128, tt * 512:(tt + 1) * 512],
                                in_=bass.AP(r1_t, dk * 512, [[8192, 128], [1, 512]])).then_inc(sems["st"], 16)
                    cnt["st"] += 16
                s.wait_ge(sems["st"], cnt["st"])
            cnt["ev"] = e0 + 16
        barrier()

    for t2 in range(2 if KPH >= 3 else 0):
        tb = t2 * 1024
        with nc.Block() as b:
            @b.gpsimd
            def _(g):
                for dk in range(16):
                    g.dma_start(out=x2b_t[:, dk * 1024:(dk + 1) * 1024],
                                in_=x2_dram[dk * 128:(dk + 1) * 128, tb:tb + 1024]).then_inc(sems["ld"], 16)
                    cnt["ld"] += 16
                g.wait_ge(sems["ld"], cnt["ld"])
        barrier()
        with nc.Block() as b:
            m0 = cnt["mm"]
            @b.tensor
            def _(t):
                for tq in range(2):
                    ps = PXR if tq == 0 else PXI
                    for dk in range(16):
                        ins = t.matmul(ps[0:1, :], ones_t[:],
                                       x2b_t[:, dk * 1024 + tq * 512: dk * 1024 + tq * 512 + 512],
                                       start=(dk == 0), stop=(dk == 15))
                ins.then_inc(sems["mm"], 1)
            @b.vector
            def _(v):
                v.wait_ge(sems["mm"], m0 + 1)
                v.tensor_copy(sxb(0, 512), PXR[0:1, :])
                v.tensor_copy(sxb(512, 1024), PXI[0:1, :])
            cnt["mm"] = m0 + 1
        barrier()
        with nc.Block() as b:
            m0 = cnt["mm"]; e0 = cnt["ev"]; l0 = cnt["ld"]
            @b.gpsimd
            def _(g):
                for hk in range(44):
                    if hk >= 2:
                        g.wait_ge(sems["mm"], m0 + 4 * (hk - 1))
                    buf = hk % 2
                    g.dma_start(out=wg_sl[:, buf * 2048:(buf + 1) * 2048],
                                in_=bass.AP(dr["wgT"].tensor, hk * 128 * 2048, [[2048, 128], [1, 2048]])).then_inc(sems["ld"], 16)
                    g.dma_start(out=wu_sl[:, buf * 2048:(buf + 1) * 2048],
                                in_=bass.AP(dr["wuT"].tensor, hk * 128 * 2048, [[2048, 128], [1, 2048]])).then_inc(sems["ld"], 16)
                    cnt["ld"] += 32
            @b.tensor
            def _(t):
                for hk in range(44):
                    buf = hk % 2
                    t.wait_ge(sems["ld"], l0 + 32 * (hk + 1))
                    if hk > 0:
                        t.wait_ge(sems["ev"], e0 + 4 * hk)
                    for tq in range(2):
                        psG = PA if tq == 0 else PXR
                        psU = PB if tq == 0 else PXI
                        for dk in range(16):
                            t.matmul(psG[:], wg_sl[:, buf * 2048 + dk * 128: buf * 2048 + (dk + 1) * 128],
                                     x2b_t[:, dk * 1024 + tq * 512: dk * 1024 + tq * 512 + 512],
                                     start=(dk == 0), stop=False)
                        ins = t.matmul(psG[:], negc_t[:], sxb(tq * 512, (tq + 1) * 512), start=False, stop=True)
                        ins.then_inc(sems["mm"], 1)
                        for dk in range(16):
                            t.matmul(psU[:], wu_sl[:, buf * 2048 + dk * 128: buf * 2048 + (dk + 1) * 128],
                                     x2b_t[:, dk * 1024 + tq * 512: dk * 1024 + tq * 512 + 512],
                                     start=(dk == 0), stop=False)
                        ins = t.matmul(psU[:], negc_t[:], sxb(tq * 512, (tq + 1) * 512), start=False, stop=True)
                        ins.then_inc(sems["mm"], 1)
            @b.scalar
            def _(sc):
                for hk in range(44):
                    for tq in range(2):
                        sc.wait_ge(sems["mm"], m0 + 4 * hk + 2 * tq + 1)
                        psG = PA if tq == 0 else PXR
                        ins = sc.activation(r2b_t[:, tq * 512:(tq + 1) * 512], psG[:],
                                            ACT.Silu, scale=sg_t[:, hk:hk + 1])
                        ins.then_inc(sems["ev"], 1)
            @b.vector
            def _(v):
                for hk in range(44):
                    for tq in range(2):
                        v.wait_ge(sems["mm"], m0 + 4 * hk + 2 * tq + 2)
                        psU = PB if tq == 0 else PXI
                        v.tensor_scalar(sqb_t[:, tq * 512:(tq + 1) * 512], psU[:],
                                        su_t[:, hk:hk + 1], None, AL.mult)
                        v.wait_ge(sems["ev"], e0 + 4 * hk + 2 * tq + 1)
                        ins = v.tensor_tensor(z_t[:, hk * 1024 + tq * 512: hk * 1024 + (tq + 1) * 512],
                                              r2b_t[:, tq * 512:(tq + 1) * 512],
                                              sqb_t[:, tq * 512:(tq + 1) * 512], AL.mult)
                        ins.then_inc(sems["ev"], 1)
            cnt["mm"] = m0 + 176; cnt["ev"] = e0 + 176
        barrier()
        with nc.Block() as b:
            m0 = cnt["mm"]
            @b.tensor
            def _(t):
                for tq in range(2):
                    ps = PXR if tq == 0 else PXI
                    for hk in range(44):
                        ins = t.matmul(ps[0:1, :], ones_t[:],
                                       z_t[:, hk * 1024 + tq * 512: hk * 1024 + tq * 512 + 512],
                                       start=(hk == 0), stop=(hk == 43))
                ins.then_inc(sems["mm"], 1)
            @b.vector
            def _(v):
                v.wait_ge(sems["mm"], m0 + 1)
                v.tensor_copy(szb(0, 512), PXR[0:1, :])
                v.tensor_copy(szb(512, 1024), PXI[0:1, :])
            cnt["mm"] = m0 + 1
        barrier()
        with nc.Block() as b:
            m0 = cnt["mm"]; e0 = cnt["ev"]; l0 = cnt["ld"]; t0_ = cnt["tp"]
            @b.gpsimd
            def _(g):
                for dk in range(16):
                    if dk >= 1:
                        g.wait_ge(sems["mm"], m0 + 2 * dk)
                    g.dma_start(out=wd_sl[:, 0:5632],
                                in_=bass.AP(dr["wdT"].tensor, dk * 128 * HID, [[HID, 128], [1, HID]])).then_inc(sems["ld"], 16)
                    cnt["ld"] += 16
            @b.sync
            def _(s):
                for dk in range(16):
                    if dk >= 1:
                        s.wait_ge(sems["ev"], e0 + 2 * dk)
                    s.dma_start(out=xr2_t[:], in_=x2_dram[dk * 128:(dk + 1) * 128, tb:tb + 1024]).then_inc(sems["tp"], 16)
                    cnt["tp"] += 16
                    for tq in range(2):
                        s.wait_ge(sems["ev"], e0 + 2 * dk + tq + 1)
                        s.dma_start(out=x2_dram[dk * 128:(dk + 1) * 128, tb + tq * 512: tb + (tq + 1) * 512],
                                    in_=r2f_t[:, tq * 512:(tq + 1) * 512]).then_inc(sems["st"], 16)
                        cnt["st"] += 16
                s.wait_ge(sems["st"], cnt["st"])
            @b.tensor
            def _(t):
                for dk in range(16):
                    t.wait_ge(sems["ld"], l0 + 16 * (dk + 1))
                    if dk >= 1:
                        t.wait_ge(sems["ev"], e0 + 2 * dk)
                    for tq in range(2):
                        ps = PA if tq == 0 else PB
                        for hk in range(44):
                            t.matmul(ps[:], wd_sl[:, hk * 128:(hk + 1) * 128],
                                     z_t[:, hk * 1024 + tq * 512: hk * 1024 + tq * 512 + 512],
                                     start=(hk == 0), stop=False)
                        ins = t.matmul(ps[:], negc_t[:], szb(tq * 512, (tq + 1) * 512), start=False, stop=True)
                        ins.then_inc(sems["mm"], 1)
            @b.vector
            def _(v):
                s0_ = cnt["st"] - 32 * 16
                for dk in range(16):
                    for tq in range(2):
                        v.wait_ge(sems["mm"], m0 + 2 * dk + tq + 1)
                        if tq == 0:
                            v.wait_ge(sems["tp"], t0_ + 16 * (dk + 1))
                        if dk >= 1:
                            v.wait_ge(sems["st"], s0_ + 16 * (2 * (dk - 1) + tq + 1))
                        ps = PA if tq == 0 else PB
                        v.tensor_scalar(r2f_t[:, tq * 512:(tq + 1) * 512], ps[:],
                                        sd_t[:, dk:dk + 1], None, AL.mult)
                        ins = v.tensor_tensor(r2f_t[:, tq * 512:(tq + 1) * 512],
                                              r2f_t[:, tq * 512:(tq + 1) * 512],
                                              xr2_t[:, tq * 512:(tq + 1) * 512], AL.add)
                        ins.then_inc(sems["ev"], 1)
            cnt["mm"] = m0 + 32; cnt["ev"] = e0 + 32
        barrier()
        with nc.Block() as b:
            m0 = cnt["mm"]; e0 = cnt["ev"]; l0 = cnt["ld"]
            @b.sync
            def _(s):
                for dk in range(16):
                    if dk > 0:
                        s.wait_ge(sems["mm"], m0 + dk)
                    s.dma_start(out=xr2_t[:], in_=x2_dram[dk * 128:(dk + 1) * 128, tb:tb + 1024]).then_inc(sems["ld"], 16)
                    cnt["ld"] += 16
            @b.vector
            def _(v):
                for dk in range(16):
                    v.wait_ge(sems["ld"], l0 + 16 * (dk + 1))
                    ins = v.tensor_copy(r2b_t[:], xr2_t[:])
                    ins.then_inc(sems["ev"], 1)
            @b.scalar
            def _(sc):
                for dk in range(16):
                    sc.wait_ge(sems["ev"], e0 + 2 * dk + 1)
                    ins = sc.activation(sqb_t[:], r2b_t[:], ACT.Square)
                    ins.then_inc(sems["ev"], 1)
            @b.tensor
            def _(t):
                for dk in range(16):
                    t.wait_ge(sems["ev"], e0 + 2 * (dk + 1))
                    t.matmul(PXR[0:1, :], ones_t[:], r2b_t[:, 0:512], start=(dk == 0), stop=(dk == 15))
                    t.matmul(PXI[0:1, :], ones_t[:], r2b_t[:, 512:1024], start=(dk == 0), stop=(dk == 15))
                    t.matmul(PA[0:1, :], ones_t[:], sqb_t[:, 0:512], start=(dk == 0), stop=(dk == 15))
                    ins = t.matmul(PB[0:1, :], ones_t[:], sqb_t[:, 512:1024], start=(dk == 0), stop=(dk == 15))
                    ins.then_inc(sems["mm"], 1)
            cnt["mm"] = m0 + 16; cnt["ev"] = e0 + 32
        barrier()
        with nc.Block() as b:
            m0 = cnt["mm"]; e0 = cnt["ev"]
            @b.vector
            def _(v):
                v.tensor_scalar(srow(0, 512), PXR[0:1, :], 1.0 / D, None, AL.mult)
                v.tensor_scalar(srow(512, 1024), PXI[0:1, :], 1.0 / D, None, AL.mult)
                v.tensor_scalar(srow2(0, 512), PA[0:1, :], 1.0 / D, None, AL.mult)
                v.tensor_scalar(srow2(512, 1024), PB[0:1, :], 1.0 / D, None, AL.mult)
                v.tensor_tensor(vrow(0, 1024), srow(0, 1024), srow(0, 1024), AL.mult)
                v.tensor_tensor(vrow(0, 1024), srow2(0, 1024), vrow(0, 1024), AL.subtract)
                v.tensor_scalar(vrow(0, 1024), vrow(0, 1024), 1e-5, None, AL.add)
                ins = v.reciprocal(mrow(0, 1024), vrow(0, 1024))
                ins.then_inc(sems["ev"], 1)
                for j in range(4):
                    v.wait_ge(sems["mm"], m0 + j + 1)
                    dst = (mb2_t, mb2_t, ib2_t, ib2_t)[j]
                    ins = v.tensor_copy(dst[:, (j % 2) * 512:(j % 2) * 512 + 512], PR[:])
                    ins.then_inc(sems["ev"], 1)
            @b.scalar
            def _(sc):
                sc.wait_ge(sems["ev"], e0 + 1)
                ins = sc.activation(vrow(0, 1024), mrow(0, 1024), ACT.Sqrt)
                ins.then_inc(sems["ev"], 1)
            @b.tensor
            def _(t):
                srcs = (srow(0, 512), srow(512, 1024), vrow(0, 512), vrow(512, 1024))
                for j in range(4):
                    t.wait_ge(sems["ev"], e0 + 2 + j)
                    ins = t.matmul(PR[:], onesr_t[:], srcs[j], start=True, stop=True)
                    ins.then_inc(sems["mm"], 1)
            cnt["mm"] = m0 + 4; cnt["ev"] = e0 + 6
        barrier()
        with nc.Block() as b:
            e0 = cnt["ev"]; l0 = cnt["ld"]; s0 = cnt["st"]
            @b.sync
            def _(s):
                for dk in range(16):
                    if dk > 0:
                        s.wait_ge(sems["ev"], e0 + dk)
                    s.dma_start(out=xr2_t[:], in_=x2_dram[dk * 128:(dk + 1) * 128, tb:tb + 1024]).then_inc(sems["ld"], 16)
                    cnt["ld"] += 16
            @b.vector
            def _(v):
                for dk in range(16):
                    v.wait_ge(sems["ld"], l0 + 16 * (dk + 1))
                    if dk > 0:
                        v.wait_ge(sems["st"], s0 + 16 * dk)
                    v.tensor_tensor(r2f_t[:], xr2_t[:], mb2_t[:], AL.subtract)
                    ins = v.tensor_tensor(r2f_t[:], r2f_t[:], ib2_t[:], AL.mult)
                    ins.then_inc(sems["ev"], 1)
            @b.scalar
            def _(sc):
                for dk in range(16):
                    sc.wait_ge(sems["ev"], e0 + dk + 1)
                    sc.dma_start(out=out_t[dk * 128:(dk + 1) * 128, tb:tb + 1024],
                                 in_=r2f_t[:]).then_inc(sems["st"], 16)
                    cnt["st"] += 16
                sc.wait_ge(sems["st"], cnt["st"])
            cnt["ev"] = e0 + 16
        barrier()

    barrier()

    _NC_CACHE["nc"] = nc
    return nc


def _prep_shared(inputs):
    f32 = np.float32
    w = np.asarray(inputs["L0_attn_in_proj_w"], f32)
    WmixT = np.ascontiguousarray(
        w.reshape(3, NB, 256, D).transpose(1, 0, 2, 3).reshape(NB * 3 * 256, D).T)
    bmix = np.ascontiguousarray(
        np.asarray(inputs["L0_attn_in_proj_b"], f32).reshape(3, NB, 256)
        .transpose(1, 0, 2).reshape(48, 128).T)
    sw = np.asarray(inputs["L0_attn_short_w"], f32)[:, 0, :]
    swT = np.ascontiguousarray(
        sw.reshape(3, NB, 256, 3).transpose(1, 0, 2, 3).reshape(48, 128, 3)
        .transpose(1, 0, 2).reshape(128, 144))
    sbT = np.ascontiguousarray(
        np.asarray(inputs["L0_attn_short_b"], f32).reshape(3, NB, 256)
        .transpose(1, 0, 2).reshape(48, 128).T)
    fbias = np.ascontiguousarray(np.asarray(inputs["L0_attn_filter_bias"], f32).reshape(16, 128).T)
    wo = np.asarray(inputs["L0_attn_out_proj_w"], f32)
    WoutT = np.ascontiguousarray(
        wo.reshape(16, 128, 16, 128).transpose(0, 3, 2, 1).reshape(16, 128, 2048))
    bout = np.ascontiguousarray(np.asarray(inputs["L0_attn_out_proj_b"], f32).reshape(16, 128).T)
    zT, dect = _consts_filter()
    dft = _consts_dft()

    def u8view(a):
        a = np.ascontiguousarray(np.asarray(a, np.int32))
        return a.view(np.uint8).reshape(a.shape[0], a.shape[1], 4)[..., 0]

    def pack_gu(a):
        t = a.reshape(44, 128, 16, 128)
        return np.ascontiguousarray(t.transpose(0, 3, 2, 1).reshape(44, 128, 2048))
    def pack_d(a):
        t = a.reshape(16, 128, 44, 128)
        return np.ascontiguousarray(t.transpose(0, 3, 2, 1).reshape(16, 128, 5632))
    wgT = pack_gu(u8view(inputs["L0_ffn_wq_gate"]))
    wuT = pack_gu(u8view(inputs["L0_ffn_wq_up"]))
    wdT = pack_d(u8view(inputs["L0_ffn_wq_down"]))
    sg = np.ascontiguousarray(np.asarray(inputs["L0_ffn_s_gate"], f32).reshape(44, 128).T)
    su = np.ascontiguousarray(np.asarray(inputs["L0_ffn_s_up"], f32).reshape(44, 128).T)
    sd = np.ascontiguousarray(np.asarray(inputs["L0_ffn_s_down"], f32).reshape(16, 128).T)
    shared = {
        "WmixT": WmixT, "bmix": bmix, "swT": swT, "sbT": sbT, "fbias": fbias,
        "WoutT": WoutT, "bout": bout,
        "w0T": np.ascontiguousarray(np.asarray(inputs["L0_attn_mlp_w0"], f32).T),
        "w1T": np.ascontiguousarray(np.asarray(inputs["L0_attn_mlp_w1"], f32).T),
        "w2T": np.ascontiguousarray(np.asarray(inputs["L0_attn_mlp_w2"], f32).T),
        "w3T": np.ascontiguousarray(np.asarray(inputs["L0_attn_mlp_w3"], f32).T),
        "b0": np.asarray(inputs["L0_attn_mlp_b0"], f32).reshape(64, 1),
        "b1": np.asarray(inputs["L0_attn_mlp_b1"], f32).reshape(64, 1),
        "b2": np.asarray(inputs["L0_attn_mlp_b2"], f32).reshape(64, 1),
        "freq": np.ascontiguousarray(np.asarray(inputs["L0_attn_sin_freq"], f32).reshape(64, 1)),
        "zT": zT, "dect": dect,
        "wgT": wgT, "wuT": wuT, "wdT": wdT, "sg": sg, "su": su, "sd": sd,
        "negc": np.full((1, 128), -128.0, bf16),
    }
    shared.update(dft)
    return shared


def kernel(**inputs):
    x = np.asarray(inputs["L0_attn_x"], np.float32)
    shared = _prep_shared(inputs)
    nc = build_program()
    in_maps = []
    for b in range(8):
        m = dict(shared)
        m["xT"] = np.ascontiguousarray(x[b].T)
        in_maps.append(m)
    res = run_bass_kernel_spmd(nc, in_maps, core_ids=list(range(8)))
    out = np.empty((8, L, D), np.float32)
    for b in range(8):
        out[b] = res.results[b]["outT"].T
    return out


# version 08
# speedup vs baseline: 1.2823x; 1.0689x over previous
import math
import os
import numpy as np
import ml_dtypes
import concourse.bass as bass
import concourse.mybir as mybir
from concourse.bass_utils import run_bass_kernel_spmd

F32 = mybir.dt.float32
BF16 = mybir.dt.bfloat16
U8 = mybir.dt.uint8
I32 = mybir.dt.int32
AL = mybir.AluOpType
ACT = mybir.ActivationFunctionType
bf16 = ml_dtypes.bfloat16

L = 2048
D = 2048
HID = 5632
NB = 8
TWO_PI = 2.0 * math.pi

MB = 65536
SEG = 16384
OFF_P = [MB + i * SEG for i in range(7)]
OFF_B = MB + 7 * SEG
OFF_T = [OFF_B + i * 8192 for i in range(4)]
OFF_KVT = OFF_B + 4 * 8192


def _consts_dft():
    s_ = np.arange(32); p_ = np.arange(64); r_ = np.arange(128); q_ = np.arange(32)
    th1 = 2 * np.pi * (np.einsum('p,r->pr', p_, r_)[None] / 128
                       + np.einsum('s,r->sr', s_, r_)[:, None, :] / 4096)
    W1c, W1s = np.cos(th1), -np.sin(th1)
    W1R = np.zeros((128, 2048), np.float32); W1I = np.zeros((128, 2048), np.float32)
    for u in range(16):
        for phi in range(2):
            W1R[phi * 64:phi * 64 + 64, u * 128:u * 128 + 128] = W1c[2 * u + phi]
            W1I[phi * 64:phi * 64 + 64, u * 128:u * 128 + 128] = W1s[2 * u + phi]
    th2 = 2 * np.pi * np.einsum('s,q->sq', s_, q_) / 32
    W2re, W2im = np.cos(th2), -np.sin(th2)
    th3 = 2 * np.pi * np.einsum('q,s->qs', q_, s_) / 32
    W3re, W3im = np.cos(th3) / 4096.0, np.sin(th3) / 4096.0
    th4 = 2 * np.pi * (np.einsum('p,r->pr', p_, r_)[None] / 128
                       + np.einsum('s,r->sr', s_, r_)[:, None, :] / 4096)
    W4re, W4im = np.cos(th4), np.sin(th4)
    W4R = np.zeros((128, 2048), np.float32); W4IN = np.zeros((128, 2048), np.float32)
    for s in range(32):
        W4R[:, s * 64:s * 64 + 64] = W4re[s].T
        W4IN[:, s * 64:s * 64 + 64] = -W4im[s].T
    rep4 = lambda m: np.tile(m, (4, 1)).astype(np.float32)
    c = {"W1R": W1R, "W1I": W1I, "W4R": W4R, "W4IN": W4IN,
         "W2RE": rep4(W2re), "W2IM": rep4(W2im), "W2IMN": rep4(-W2im),
         "W3RE": rep4(W3re), "W3IM": rep4(W3im), "W3IMN": rep4(-W3im)}
    return {k: v.astype(bf16) for k, v in c.items()}


def _consts_filter():
    t = np.linspace(0.0, 1.0, L)[:, None]
    t_resc = np.linspace(0.0, L - 1.0, L)
    w = 2.0 * np.pi * t_resc / L
    f = np.linspace(0.0001, 0.0, 1)
    ang = f[None, :] * w[:, None]
    z = np.concatenate([t, np.cos(ang), -np.sin(ang)], axis=-1)
    zT = np.ascontiguousarray(z.T).astype(bf16)
    deltas = np.abs(np.linspace(np.log(0.01) / 1.5, np.log(0.01) / 0.3, D))
    decay = np.exp(-t * deltas[None, :]).astype(np.float32)
    dect = np.zeros((128, NB * 4096), np.float32)
    pp = np.arange(64)
    for b in range(NB):
        for u in range(16):
            for phi in range(2):
                tv = 2 * u + phi + 32 * pp
                dect[phi * 64:phi * 64 + 64, b * 4096 + u * 256:b * 4096 + (u + 1) * 256] = \
                    decay[tv][:, b * 256:(b + 1) * 256]
    return zT, dect.astype(bf16)


_NC_CACHE = {}


def build_program():
    if "nc" in _NC_CACHE:
        return _NC_CACHE["nc"]
    nc = bass.Bass("TRN2", target_bir_lowering=False)

    dr = {}
    def din(name, shape, dtype=F32):
        dr[name] = nc.dram_tensor(name, list(shape), dtype, kind="ExternalInput").ap()

    din("xT", [D, L])
    din("WmixT", [D, 3 * D])
    din("bmix", [128, 48]); din("swT", [128, 144]); din("sbT", [128, 48])
    din("fbias", [128, 16])
    din("WoutT", [16, 128, 2048]); din("bout", [128, 16])
    din("w0T", [3, 64]); din("w1T", [64, 64]); din("w2T", [64, 64]); din("w3T", [64, D])
    din("b0", [64, 1]); din("b1", [64, 1]); din("b2", [64, 1]); din("freq", [64, 1])
    din("zT", [3, L], BF16)
    din("dect", [128, NB * 4096], BF16)
    for n in ("W1R", "W1I", "W4R", "W4IN"):
        din(n, [128, 2048], BF16)
    for n in ("W2RE", "W2IM", "W2IMN", "W3RE", "W3IM", "W3IMN"):
        din(n, [128, 32], BF16)
    din("wgT", [44, 128, 2048], U8); din("wuT", [44, 128, 2048], U8)
    din("wdT", [16, 128, HID], U8)
    din("sg", [128, 44]); din("su", [128, 44]); din("sd", [128, 16])
    din("negc", [1, 128], BF16)

    out_t = nc.dram_tensor("outT", [D, L], F32, kind="ExternalOutput").ap()
    y_dram = nc.dram_tensor("Yscr", [D, L], BF16, kind="Internal").ap()
    x2_dram = nc.dram_tensor("X2scr", [D, L], F32, kind="Internal").ap()

    def sbn(name, shape, dtype=BF16):
        return nc.alloc_sbuf_tensor(name, list(shape), dtype)

    C = {}
    for n in ("W1R", "W1I", "W4R", "W4IN"):
        C[n] = sbn("C" + n, [128, 2048])
    for n in ("W2RE", "W2IM", "W2IMN", "W3RE", "W3IM", "W3IMN"):
        C[n] = sbn("C" + n, [128, 32])
    bmix_t = sbn("bmix_t", [128, 48], F32)
    swT_t = sbn("swT_t", [128, 144], F32)
    sbT_t = sbn("sbT_t", [128, 48], F32)
    fbias_t = sbn("fbias_t", [128, 16], F32)
    bout_t = sbn("bout_t", [128, 16], F32)
    sg_t = sbn("sg_t", [128, 44], F32); su_t = sbn("su_t", [128, 44], F32)
    sd_t = sbn("sd_t", [128, 16], F32)
    w0T_t = sbn("w0T_t", [3, 64]); w1T_t = sbn("w1T_t", [64, 64]); w2T_t = sbn("w2T_t", [64, 64])
    w3T_t = sbn("w3T_t", [64, D])
    b0_t = sbn("b0_t", [64, 1], F32); b1_t = sbn("b1_t", [64, 1], F32)
    b2_t = sbn("b2_t", [64, 1], F32); fq_t = sbn("fq_t", [64, 1], F32)
    hhp_t = sbn("hhp_t", [64, L])
    ones_t = sbn("ones_t", [128, 1])
    onesr_t = sbn("onesr_t", [1, 128], F32)
    negc_t = sbn("negc_t", [1, 128])
    rows_t = sbn("rows_t", [1, 4096], F32)
    sxzb_t = sbn("sxzb_t", [1, 2048])

    def srow(a, b_):
        return rows_t[0:1, a:b_]
    def srow2(a, b_):
        return rows_t[0:1, 1024 + a:1024 + b_]
    def mrow(a, b_):
        return rows_t[0:1, 2048 + a:2048 + b_]
    def vrow(a, b_):
        return rows_t[0:1, 3072 + a:3072 + b_]
    def sxb(a, b_):
        return sxzb_t[0:1, a:b_]
    def szb(a, b_):
        return sxzb_t[0:1, 1024 + a:1024 + b_]
    assert nc.sbuf_bytes_remaining > (229344 - MB), nc.sbuf_bytes_remaining

    def sbm(name, offset, shape, dtype=BF16):
        return nc.alloc_sbuf_tensor_at(name, list(shape), dtype, offset=offset, align_bytes=64)

    P = [sbm(f"P{i+1}", OFF_P[i], [128, 8192]) for i in range(7)]
    T0 = sbm("T0", OFF_T[0], [128, 4096]); T1 = sbm("T1", OFF_T[1], [128, 4096])
    T2 = sbm("T2", OFF_T[2], [128, 4096]); T3 = sbm("T3", OFF_T[3], [128, 4096])
    KVT = sbm("KVT", OFF_KVT, [128, 4096])
    arg_t = sbm("arg_t", OFF_P[0], [64, 2048], F32)
    un_t = sbm("un_t", OFF_P[1], [64, 2048], F32)
    ni_t = sbm("ni_t", OFF_P[2], [64, 2048], I32)
    nf_t = sbm("nf_t", OFF_P[3], [64, 2048], F32)
    hh_t = sbm("hh_t", OFF_P[4], [64, 2048])
    zT_t = sbm("zT_t", OFF_P[5], [3, L])
    CtkR, CtkI = P[0], P[1]
    TCR, TCI = P[2], P[3]
    KRE, KIM = P[0], P[1]
    CtvR, CtvI = P[4], P[5]
    XRE, XIM = P[4], P[5]
    S1 = P[6]
    TER, TEI = P[0], P[1]
    ype = sbm("ype", OFF_P[6], [128, 4096])
    yperm = sbm("yperm", OFF_P[6] + 8192, [128, 4096])
    dect_b = sbm("dect_b", OFF_P[6], [128, 4096])
    xtile = P[6]
    wmixb = sbm("wmixb", OFF_P[4], [128, 12288])
    wo_sl = sbm("wo_sl", OFF_P[0], [128, 4096])
    ytile = sbm("ytile", OFF_P[4], [128, 8192])
    r1_t = sbm("r1_t", OFF_P[5], [128, 8192], F32)
    r1b_t = sbm("r1b_t", OFF_P[1], [128, 8192])
    sq_t = sbm("sq_t", OFF_P[2], [128, 8192])
    z_t = sbm("z_t", MB, [128, 45056])
    x2b_t = sbm("x2b_t", MB + 90112, [128, 16384])
    wd_sl = sbm("wd_sl", MB + 122880, [128, 5632])
    wg_sl = sbm("wg_sl", MB + 134144, [128, 4096])
    wu_sl = sbm("wu_sl", MB + 142336, [128, 4096])
    xr2_t = sbm("xr2_t", MB + 150528, [128, 1024], F32)
    r2f_t = sbm("r2f_t", MB + 154624, [128, 1024], F32)
    mb2_t = sbm("mb2_t", MB + 134144, [128, 1024], F32)
    ib2_t = sbm("ib2_t", MB + 138240, [128, 1024], F32)
    r2b_t = sbm("r2b_t", MB + 158720, [128, 1024])
    sqb_t = sbm("sqb_t", MB + 160768, [128, 1024])

    mb_t = mb2_t
    ib_t = ib2_t
    xr_t = xr2_t

    PS = []
    def psum(name, shape):
        g = nc.psum_tensor(name, list(shape), F32)
        t = g.__enter__(); PS.append(g)
        return t
    PA = psum("PA", [128, 512])
    PB = psum("PB", [128, 512])
    PCe = psum("PCe", [128, 256])
    PCo = psum("PCo", [128, 256])
    PXR = psum("PXR", [128, 512])
    PXI = psum("PXI", [128, 512])
    P2m = psum("P2m", [128, 256])
    PR = psum("PR", [128, 512])

    sems = {}
    for nm in ("ld", "mm", "ev", "tp", "st", "bar"):
        g = nc.semaphore("s_" + nm)
        sems[nm] = g.__enter__()
    cnt = {"ld": 0, "mm": 0, "ev": 0, "tp": 0, "st": 0, "bar": 0}
    ENG = ("sync", "scalar", "vector", "tensor", "gpsimd")

    def barrier():
        cnt["bar"] += 1
        tgt = 5 * cnt["bar"]
        with nc.Block() as b:
            for e in ENG:
                def mk(e=e):
                    def f(eng):
                        eng.sem_inc(sems["bar"], 1)
                        eng.wait_ge(sems["bar"], tgt)
                    return f
                getattr(b, e)(mk())

    with nc.Block() as b:
        @b.gpsimd
        def _(g):
            for nm, t in (("bmix", bmix_t), ("swT", swT_t), ("sbT", sbT_t), ("fbias", fbias_t),
                          ("bout", bout_t), ("sg", sg_t), ("su", su_t), ("sd", sd_t),
                          ("b0", b0_t), ("b1", b1_t), ("b2", b2_t), ("freq", fq_t),
                          ("w0T", w0T_t), ("w1T", w1T_t), ("w2T", w2T_t), ("w3T", w3T_t)):
                g.dma_start(out=t[:], in_=dr[nm][:]).then_inc(sems["ld"], 16)
                cnt["ld"] += 16
            g.memset(ones_t[:], 1.0)
            g.memset(onesr_t[:], 1.0)
        @b.sync
        def _(s):
            for nm, t in (("zT", zT_t), ("negc", negc_t),
                          ("W1R", C["W1R"]), ("W1I", C["W1I"]), ("W4R", C["W4R"]), ("W4IN", C["W4IN"]),
                          ("W2RE", C["W2RE"]), ("W2IM", C["W2IM"]), ("W2IMN", C["W2IMN"]),
                          ("W3RE", C["W3RE"]), ("W3IM", C["W3IM"]), ("W3IMN", C["W3IMN"])):
                s.dma_start(out=t[:], in_=dr[nm][:]).then_inc(sems["ld"], 16)
                cnt["ld"] += 16
            s.wait_ge(sems["ld"], cnt["ld"])
    barrier()

    def mlp_layer(lhsT, rhs_q, bias):
        with nc.Block() as b:
            m0 = cnt["mm"]; e0 = cnt["ev"]
            @b.tensor
            def _(t):
                for j in range(4):
                    if j > 0:
                        t.wait_ge(sems["ev"], e0 + j)
                    ins = t.matmul(PA[:64, :], lhsT, rhs_q[j], start=True, stop=True)
                    ins.then_inc(sems["mm"], 1)
            @b.vector
            def _(v):
                for j in range(4):
                    v.wait_ge(sems["mm"], m0 + j + 1)
                    cols = slice(j * 512, (j + 1) * 512)
                    v.tensor_scalar(arg_t[:, cols], PA[:64, :], bias[:], fq_t[:], AL.add, AL.mult)
                    v.tensor_scalar(un_t[:, cols], arg_t[:, cols], 1.0 / TWO_PI, None, AL.mult)
                    v.tensor_copy(ni_t[:, cols], un_t[:, cols])
                    v.tensor_copy(nf_t[:, cols], ni_t[:, cols])
                    ins = v.scalar_tensor_tensor(arg_t[:, cols], nf_t[:, cols], -TWO_PI,
                                                 arg_t[:, cols], AL.mult, AL.add)
                    ins.then_inc(sems["ev"], 1)
            @b.scalar
            def _(sc):
                for j in range(4):
                    sc.wait_ge(sems["ev"], e0 + j + 1)
                    cols = slice(j * 512, (j + 1) * 512)
                    sc.activation(hh_t[:, cols], arg_t[:, cols], ACT.Sin)
            cnt["mm"] = m0 + 4; cnt["ev"] = e0 + 4
        barrier()

    mlp_layer(w0T_t[:], [zT_t[:, j * 512:(j + 1) * 512] for j in range(4)], b0_t)
    mlp_layer(w1T_t[:], [hh_t[:, j * 512:(j + 1) * 512] for j in range(4)], b1_t)
    mlp_layer(w2T_t[:], [hh_t[:, j * 512:(j + 1) * 512] for j in range(4)], b2_t)
    with nc.Block() as b:
        @b.vector
        def _(v):
            ap_in = bass.AP(hh_t, 0, [[L, 64], [1, 32], [32, 64]])
            v.tensor_copy(hhp_t[:].rearrange("a (s p) -> a s p", s=32), ap_in)
    barrier()

    def F1(srcT):
        with nc.Block() as b:
            m0 = cnt["mm"]; e0 = cnt["ev"]
            @b.tensor
            def _(t):
                nm = 0
                for h in range(2):
                    for u in range(16):
                        if nm >= 1:
                            t.wait_ge(sems["ev"], e0 + 4 * nm)
                        rhsE = srcT[0:64, h * 2048 + u * 128: h * 2048 + (u + 1) * 128]
                        rhsO = srcT[64:128, h * 2048 + u * 128: h * 2048 + (u + 1) * 128]
                        wre = C["W1R"][:, u * 128:(u + 1) * 128]
                        wim = C["W1I"][:, u * 128:(u + 1) * 128]
                        t.matmul(PCe[:, 0:128], wre[0:64, :], rhsE, start=True, stop=True, tile_position=(0, 0))
                        t.matmul(PCe[:, 128:256], wim[0:64, :], rhsE, start=True, stop=True, tile_position=(0, 0))
                        t.matmul(PCo[:, 0:128], wre[64:128, :], rhsO, start=True, stop=True, tile_position=(64, 0))
                        ins = t.matmul(PCo[:, 128:256], wim[64:128, :], rhsO, start=True, stop=True, tile_position=(64, 0))
                        nm += 1
                        ins.then_inc(sems["mm"], 1)
                cnt["mm"] = m0 + nm
            @b.vector
            def _(v):
                ne = 0
                for h in range(2):
                    for u in range(16):
                        v.wait_ge(sems["mm"], m0 + h * 16 + u + 1)
                        for pi, dst in ((0, CtR), (1, CtI)):
                            for so, pcsrc in ((0, PCe), (1, PCo)):
                                ap_in = bass.AP(pcsrc, pi * 128, [[256, 128], [1, 128]])
                                ap_out = bass.AP(dst, (h * 32) * 128 + 2 * u + so,
                                                 [[8192, 128], [128, 32], [32, 4]])
                                ins = v.tensor_copy(ap_out, ap_in)
                                ne += 1
                                ins.then_inc(sems["ev"], 1)
                cnt["ev"] = e0 + ne
        barrier()

    def turn(srcR, srcI, dstR, dstI):
        with nc.Block() as b:
            @b.sync
            def _(s):
                for sr, ds in ((srcR, dstR), (srcI, dstI)):
                    s.dma_start_transpose(
                        ds[:].rearrange("a (g r) -> a g r", g=64),
                        sr[:].rearrange("a (g f) -> a g f", g=64)).then_inc(sems["tp"], 16)
                    cnt["tp"] += 16
                s.wait_ge(sems["tp"], cnt["tp"])
        barrier()

    def F2(WRE, WIM, WIMN, srcR, srcI, dstRE, dstIM):
        with nc.Block() as b:
            m0 = cnt["mm"]; e0 = cnt["ev"]
            @b.tensor
            def _(t):
                nm = 0
                for blk in range(16):
                    if nm >= 1:
                        t.wait_ge(sems["ev"], e0 + 2 * nm)
                    cols = slice(blk * 512, blk * 512 + 512)
                    for c4 in range(4):
                        band = slice(32 * c4, 32 * c4 + 32)
                        tpos = (32 * c4, 32 * c4)
                        t.matmul(PXR[band, :], C[WRE][band, :], srcR[band, cols], start=True, stop=False, tile_position=tpos)
                        t.matmul(PXR[band, :], C[WIMN][band, :], srcI[band, cols], start=False, stop=True, tile_position=tpos)
                        t.matmul(PXI[band, :], C[WIM][band, :], srcR[band, cols], start=True, stop=False, tile_position=tpos)
                        ins = t.matmul(PXI[band, :], C[WRE][band, :], srcI[band, cols], start=False, stop=True, tile_position=tpos)
                    nm += 1
                    ins.then_inc(sems["mm"], 1)
                cnt["mm"] = m0 + nm
            @b.vector
            def _(v):
                ne = 0
                for blk in range(16):
                    v.wait_ge(sems["mm"], m0 + blk + 1)
                    cols = slice(blk * 512, blk * 512 + 512)
                    v.tensor_copy(dstRE[:, cols], PXR[:])
                    ins = v.tensor_copy(dstIM[:, cols], PXI[:])
                    ne += 2
                    ins.then_inc(sems["ev"], 2)
                cnt["ev"] = e0 + ne
        barrier()

    def F1(srcT, dR, dI):
        with nc.Block() as b:
            m0 = cnt["mm"]; e0 = cnt["ev"]
            @b.tensor
            def _(t):
                nm = 0
                for h in range(2):
                    for u in range(16):
                        if nm >= 1:
                            t.wait_ge(sems["ev"], e0 + 4 * nm)
                        rhsE = srcT[0:64, h * 2048 + u * 128: h * 2048 + (u + 1) * 128]
                        rhsO = srcT[64:128, h * 2048 + u * 128: h * 2048 + (u + 1) * 128]
                        wre = C["W1R"][:, u * 128:(u + 1) * 128]
                        wim = C["W1I"][:, u * 128:(u + 1) * 128]
                        t.matmul(PCe[:, 0:128], wre[0:64, :], rhsE, start=True, stop=True, tile_position=(0, 0))
                        t.matmul(PCe[:, 128:256], wim[0:64, :], rhsE, start=True, stop=True, tile_position=(0, 0))
                        t.matmul(PCo[:, 0:128], wre[64:128, :], rhsO, start=True, stop=True, tile_position=(64, 0))
                        ins = t.matmul(PCo[:, 128:256], wim[64:128, :], rhsO, start=True, stop=True, tile_position=(64, 0))
                        nm += 1
                        ins.then_inc(sems["mm"], 1)
                cnt["mm"] = m0 + nm
            def evac_body(eng, so, pcsrc, cpy):
                for h in range(2):
                    for u in range(16):
                        eng.wait_ge(sems["mm"], m0 + h * 16 + u + 1)
                        for plane, dst in ((0, dR), (1, dI)):
                            ap_in = bass.AP(pcsrc, plane * 128, [[256, 128], [1, 128]])
                            ap_out = bass.AP(dst, (h * 32) * 128 + 2 * u + so,
                                             [[8192, 128], [128, 32], [32, 4]])
                            ins = cpy(ap_out, ap_in)
                            ins.then_inc(sems["ev"], 1)
            @b.vector
            def _(v):
                evac_body(v, 0, PCe, v.tensor_copy)
            @b.scalar
            def _(sc):
                evac_body(sc, 1, PCo, sc.copy)
            cnt["ev"] = e0 + 128
        barrier()

    def turn(srcR, srcI, dstR, dstI):
        with nc.Block() as b:
            @b.sync
            def _(s):
                for sr, ds in ((srcR, dstR), (srcI, dstI)):
                    s.dma_start_transpose(
                        ds[:].rearrange("a (g r) -> a g r", g=64),
                        sr[:].rearrange("a (g f) -> a g f", g=64)).then_inc(sems["tp"], 16)
                    cnt["tp"] += 16
                s.wait_ge(sems["tp"], cnt["tp"])
        barrier()

    def F2(WRE, WIM, WIMN, srcR, srcI, dstRE, dstIM):
        with nc.Block() as b:
            m0 = cnt["mm"]; e0 = cnt["ev"]
            @b.tensor
            def _(t):
                nm = 0
                for blk in range(16):
                    if nm >= 1:
                        t.wait_ge(sems["ev"], e0 + 2 * nm)
                    cols = slice(blk * 512, blk * 512 + 512)
                    for c4 in range(4):
                        band = slice(32 * c4, 32 * c4 + 32)
                        tpos = (32 * c4, 32 * c4)
                        t.matmul(PXR[band, :], C[WRE][band, :], srcR[band, cols], start=True, stop=False, tile_position=tpos)
                        t.matmul(PXR[band, :], C[WIMN][band, :], srcI[band, cols], start=False, stop=True, tile_position=tpos)
                        t.matmul(PXI[band, :], C[WIM][band, :], srcR[band, cols], start=True, stop=False, tile_position=tpos)
                        ins = t.matmul(PXI[band, :], C[WRE][band, :], srcI[band, cols], start=False, stop=True, tile_position=tpos)
                    nm += 1
                    ins.then_inc(sems["mm"], 1)
                cnt["mm"] = m0 + nm
            @b.vector
            def _(v):
                for blk in range(16):
                    v.wait_ge(sems["mm"], m0 + blk + 1)
                    cols = slice(blk * 512, blk * 512 + 512)
                    ins = v.tensor_copy(dstRE[:, cols], PXR[:])
                    ins.then_inc(sems["ev"], 1)
            @b.scalar
            def _(sc):
                for blk in range(16):
                    sc.wait_ge(sems["mm"], m0 + blk + 1)
                    cols = slice(blk * 512, blk * 512 + 512)
                    ins = sc.copy(dstIM[:, cols], PXI[:])
                    ins.then_inc(sems["ev"], 1)
            cnt["ev"] = e0 + 32
        barrier()

    def conv_batch(bi):
        with nc.Block() as b:
            m0 = cnt["mm"]; e0 = cnt["ev"]; t0_ = cnt["tp"]; l0 = cnt["ld"]
            @b.sync
            def _(s):
                s.dma_start(out=dect_b[:], in_=dr["dect"][:, bi * 4096:(bi + 1) * 4096]).then_inc(sems["tp"], 16)
                cnt["tp"] += 16
            @b.tensor
            def _(t):
                for u in range(16):
                    if u > 0:
                        t.wait_ge(sems["ev"], e0 + u)
                    ins = t.matmul(PA[:, 0:256], hhp_t[:, u * 128:(u + 1) * 128],
                                   w3T_t[:, bi * 256:(bi + 1) * 256], start=True, stop=True)
                    ins.then_inc(sems["mm"], 1)
            @b.vector
            def _(v):
                v.wait_ge(sems["tp"], t0_ + 16)
                for u in range(16):
                    v.wait_ge(sems["mm"], m0 + u + 1)
                    for h in range(2):
                        ins = v.tensor_tensor(
                            KVT[:, h * 2048 + u * 128: h * 2048 + u * 128 + 128],
                            PA[:, h * 128:(h + 1) * 128],
                            dect_b[:, u * 256 + h * 128: u * 256 + (h + 1) * 128], AL.mult)
                    ins.then_inc(sems["ev"], 1)
            @b.gpsimd
            def _(g):
                for dk in range(16):
                    g.dma_start(out=wmixb[:, dk * 768:(dk + 1) * 768],
                                in_=dr["WmixT"][dk * 128:(dk + 1) * 128, bi * 768:(bi + 1) * 768]).then_inc(sems["ld"], 16)
                    cnt["ld"] += 16
                g.wait_ge(sems["ev"], e0 + 16)
                for dk in range(16):
                    g.dma_start(out=xtile[:, dk * 512:(dk + 1) * 512],
                                in_=dr["xT"][dk * 128:(dk + 1) * 128, 0:512]).then_inc(sems["ld"], 16)
                    cnt["ld"] += 16
            cnt["mm"] = m0 + 16; cnt["ev"] = e0 + 16
        barrier()
        F1(KVT, CtkR, CtkI)
        turn(CtkR, CtkI, TCR, TCI)
        F2("W2RE", "W2IM", "W2IMN", TCR, TCI, KRE, KIM)

        SW1 = (PA, PB, PXR, PXI)
        for tt in range(4):
            with nc.Block() as b:
                m0 = cnt["mm"]; e0 = cnt["ev"]; l0 = cnt["ld"] - 256
                @b.tensor
                def _(t):
                    for dk in range(16):
                        t.wait_ge(sems["ld"], l0 + 16 * (dk + 1))
                        for rc in range(4):
                            ins = t.matmul(SW1[rc][:], wmixb[:, dk * 768 + rc * 128: dk * 768 + (rc + 1) * 128],
                                           xtile[:, dk * 512:(dk + 1) * 512],
                                           start=(dk == 0), stop=(dk == 15))
                    ins.then_inc(sems["mm"], 1)
                    t.wait_ge(sems["ev"], e0 + 4)
                    for dk in range(16):
                        for rc in range(4, 6):
                            ins = t.matmul((PA, PB)[rc - 4][:], wmixb[:, dk * 768 + rc * 128: dk * 768 + (rc + 1) * 128],
                                           xtile[:, dk * 512:(dk + 1) * 512],
                                           start=(dk == 0), stop=(dk == 15))
                    ins.then_inc(sems["mm"], 1)
                @b.scalar
                def _(sc):
                    for rc in range(6):
                        sc.wait_ge(sems["mm"], m0 + (1 if rc < 4 else 2))
                        ps = SW1[rc] if rc < 4 else (PA, PB)[rc - 4]
                        role, half = rc // 2, rc % 2
                        dstT = (T0, T1, T2)[role]
                        ins = sc.activation(dstT[:, half * 2048 + tt * 512: half * 2048 + (tt + 1) * 512],
                                            ps[:], ACT.Identity, bias=bmix_t[:, bi * 6 + rc: bi * 6 + rc + 1])
                        ins.then_inc(sems["ev"], 1)
                @b.gpsimd
                def _(g):
                    if tt < 3:
                        g.wait_ge(sems["mm"], m0 + 2)
                        for dk in range(16):
                            g.dma_start(out=xtile[:, dk * 512:(dk + 1) * 512],
                                        in_=dr["xT"][dk * 128:(dk + 1) * 128, (tt + 1) * 512:(tt + 2) * 512]).then_inc(sems["ld"], 16)
                            cnt["ld"] += 16
                cnt["mm"] = m0 + 2; cnt["ev"] = e0 + 6
            barrier()

        with nc.Block() as b:
            def taps(eng, h):
                for (srcT, dstT, role) in ((T0, T3, 0), (T1, T0, 1), (T2, T1, 2)):
                    j = bi * 6 + role * 2 + h
                    cs = h * 2048
                    w0s = swT_t[:, j * 3 + 0: j * 3 + 1]
                    w1s = swT_t[:, j * 3 + 1: j * 3 + 2]
                    w2s = swT_t[:, j * 3 + 2: j * 3 + 3]
                    eng.tensor_scalar(dstT[:, cs:cs + 2048], srcT[:, cs:cs + 2048],
                                      w2s, sbT_t[:, j:j + 1], AL.mult, AL.add)
                    eng.scalar_tensor_tensor(dstT[:, cs + 1:cs + 2048], srcT[:, cs:cs + 2047],
                                             w1s, dstT[:, cs + 1:cs + 2048], AL.mult, AL.add)
                    eng.scalar_tensor_tensor(dstT[:, cs + 2:cs + 2048], srcT[:, cs:cs + 2046],
                                             w0s, dstT[:, cs + 2:cs + 2048], AL.mult, AL.add)
                eng.tensor_tensor(T1[:, h * 2048:(h + 1) * 2048], T1[:, h * 2048:(h + 1) * 2048],
                                  T0[:, h * 2048:(h + 1) * 2048], AL.mult)
                ap_in = bass.AP(T1, h * 2048, [[4096, 128], [1, 32], [32, 64]])
                out_ap = bass.AP(T2, h * 2048, [[4096, 128], [64, 32], [1, 64]])
                ins = eng.tensor_copy(out_ap, ap_in)
                ins.then_inc(sems["ev"], 1)
            @b.vector
            def _(v):
                taps(v, 0)
                taps(v, 1)
            cnt["ev"] += 2
        barrier()
        with nc.Block() as b:
            @b.sync
            def _(s):
                s.dma_start_transpose(
                    KVT[:].rearrange("a (t f) -> a t f", t=32),
                    T2[:].rearrange("a (t f) -> a t f", t=32)).then_inc(sems["tp"], 16)
                cnt["tp"] += 16
                s.wait_ge(sems["tp"], cnt["tp"])
        barrier()
        F1(KVT, CtvR, CtvI)
        turn(CtvR, CtvI, TCR, TCI)
        F2("W2RE", "W2IM", "W2IMN", TCR, TCI, XRE, XIM)

        with nc.Block() as b:
            def prod(eng, c0, c1):
                sl = slice(c0, c1)
                eng.tensor_tensor(S1[:, sl], XRE[:, sl], KIM[:, sl], AL.mult)
                eng.tensor_tensor(XRE[:, sl], XRE[:, sl], KRE[:, sl], AL.mult)
                eng.tensor_tensor(KRE[:, sl], XIM[:, sl], KRE[:, sl], AL.mult)
                eng.tensor_tensor(XIM[:, sl], XIM[:, sl], KIM[:, sl], AL.mult)
                eng.tensor_tensor(XRE[:, sl], XRE[:, sl], XIM[:, sl], AL.subtract)
                eng.tensor_tensor(S1[:, sl], S1[:, sl], KRE[:, sl], AL.add)
            @b.vector
            def _(v):
                prod(v, 0, 5120)
            @b.gpsimd
            def _(g):
                prod(g, 5120, 8192)
        barrier()
        F2("W3RE", "W3IM", "W3IMN", XRE, S1, TCR, TCI)
        turn(TCR, TCI, TER, TEI)
        with nc.Block() as b:
            m0 = cnt["mm"]; e0 = cnt["ev"]
            @b.tensor
            def _(t):
                for u in range(16):
                    if u > 0:
                        t.wait_ge(sems["ev"], e0 + u)
                    for phi in range(2):
                        s = 2 * u + phi
                        rhsR = bass.AP(TER, s, [[8192, 128], [128, 64], [32, 4]])
                        rhsI = bass.AP(TEI, s, [[8192, 128], [128, 64], [32, 4]])
                        tpos = (0, 64 * phi)
                        outp = P2m[64 * phi:64 * phi + 64, :]
                        t.matmul(outp, C["W4R"][:, s * 64:s * 64 + 64], rhsR, start=True, stop=False, tile_position=tpos)
                        ins = t.matmul(outp, C["W4IN"][:, s * 64:s * 64 + 64], rhsI, start=False, stop=True, tile_position=tpos)
                    ins.then_inc(sems["mm"], 1)
            @b.vector
            def _(v):
                for u in range(16):
                    v.wait_ge(sems["mm"], m0 + u + 1)
                    v.tensor_copy(bass.AP(ype, u * 128, [[4096, 128], [1, 128]]), P2m[:, 0:128])
                    ins = v.tensor_copy(bass.AP(ype, 2048 + u * 128, [[4096, 128], [1, 128]]), P2m[:, 128:256])
                    ins.then_inc(sems["ev"], 1)
            cnt["mm"] = m0 + 16; cnt["ev"] = e0 + 16
        barrier()
        with nc.Block() as b:
            @b.sync
            def _(s):
                s.dma_start_transpose(
                    yperm[:].rearrange("a (t f) -> a t f", t=32),
                    ype[:].rearrange("a (t f) -> a t f", t=32)).then_inc(sems["tp"], 16)
                cnt["tp"] += 16
                s.wait_ge(sems["tp"], cnt["tp"])
        barrier()
        with nc.Block() as b:
            e0 = cnt["ev"]
            @b.vector
            def _(v):
                for h in range(2):
                    ap_y = bass.AP(yperm, h * 2048, [[4096, 128], [1, 64], [128, 16], [64, 2]])
                    ap_g1o = bass.AP(T2, h * 2048, [[4096, 128], [32, 64], [2, 16], [1, 2]])
                    fb = fbias_t[:, bi * 2 + h: bi * 2 + h + 1]
                    vt_ap = bass.AP(T1, h * 2048, [[4096, 128], [32, 64], [2, 16], [1, 2]])
                    ins = v.scalar_tensor_tensor(ap_g1o, vt_ap, fb, ap_y, AL.mult, AL.add)
                    if h == 1:
                        ins.then_inc(sems["ev"], 1)
                ins = v.tensor_tensor(T0[:, 0:2048], T2[:, 0:2048], T3[:, 0:2048], AL.mult)
                ins.then_inc(sems["ev"], 1)
            @b.gpsimd
            def _(g):
                g.wait_ge(sems["ev"], e0 + 1)
                ins = g.tensor_tensor(T0[:, 2048:4096], T2[:, 2048:4096], T3[:, 2048:4096], AL.mult)
                ins.then_inc(sems["ev"], 1)
            @b.sync
            def _(s):
                s.wait_ge(sems["ev"], e0 + 3)
                for h in range(2):
                    s.dma_start(out=y_dram[(bi * 2 + h) * 128:(bi * 2 + h + 1) * 128, :],
                                in_=T0[:, h * 2048:(h + 1) * 2048]).then_inc(sems["st"], 16)
                    cnt["st"] += 16
                s.wait_ge(sems["st"], cnt["st"])
            cnt["ev"] = e0 + 3
        barrier()

    KPH = int(os.environ.get("KPH", "99"))
    KNB = int(os.environ.get("KNB", str(NB)))
    if KPH >= 1:
        for bi in range(KNB):
            conv_batch(bi)

    for tt in range(4 if KPH >= 2 else 0):
        with nc.Block() as b:
            @b.sync
            def _(s):
                for ck in range(16):
                    s.dma_start(out=ytile[:, ck * 512:(ck + 1) * 512],
                                in_=y_dram[ck * 128:(ck + 1) * 128, tt * 512:(tt + 1) * 512]).then_inc(sems["tp"], 16)
                    cnt["tp"] += 16
                s.wait_ge(sems["tp"], cnt["tp"])
        barrier()
        with nc.Block() as b:
            m0 = cnt["mm"]; e0 = cnt["ev"]; l0 = cnt["ld"]; t0_ = cnt["tp"]
            @b.gpsimd
            def _(g):
                for dk in range(16):
                    if dk >= 2:
                        g.wait_ge(sems["mm"], m0 + dk - 1)
                    buf = dk % 2
                    g.dma_start(out=wo_sl[:, buf * 2048:(buf + 1) * 2048],
                                in_=bass.AP(dr["WoutT"].tensor, dk * 128 * 2048, [[2048, 128], [1, 2048]])).then_inc(sems["ld"], 16)
                    cnt["ld"] += 16
            @b.sync
            def _(s):
                for dk in range(16):
                    if dk >= 2:
                        s.wait_ge(sems["ev"], e0 + 2 * (dk - 1))
                    s.dma_start(out=xr2_t[:, (dk % 2) * 512:(dk % 2) * 512 + 512],
                                in_=dr["xT"][dk * 128:(dk + 1) * 128, tt * 512:(tt + 1) * 512]).then_inc(sems["tp"], 16)
                    cnt["tp"] += 16
            @b.tensor
            def _(t):
                for dk in range(16):
                    buf = dk % 2
                    t.wait_ge(sems["ld"], l0 + 16 * (dk + 1))
                    if dk >= 2:
                        t.wait_ge(sems["ev"], e0 + 2 * (dk - 1))
                    ps = PA if dk % 2 == 0 else PB
                    for ck in range(16):
                        ins = t.matmul(ps[:], wo_sl[:, buf * 2048 + ck * 128: buf * 2048 + (ck + 1) * 128],
                                       ytile[:, ck * 512:(ck + 1) * 512],
                                       start=(ck == 0), stop=(ck == 15))
                    ins.then_inc(sems["mm"], 1)
            @b.scalar
            def _(sc):
                for dk in range(16):
                    sc.wait_ge(sems["mm"], m0 + dk + 1)
                    ps = PA if dk % 2 == 0 else PB
                    ins = sc.activation(bass.AP(r1_t, dk * 512, [[8192, 128], [1, 512]]),
                                        ps[:], ACT.Identity, bias=bout_t[:, dk:dk + 1])
                    ins.then_inc(sems["ev"], 1)
            @b.vector
            def _(v):
                for dk in range(16):
                    v.wait_ge(sems["ev"], e0 + 2 * dk + 1)
                    v.wait_ge(sems["tp"], t0_ + 16 * (dk + 1))
                    r1c = bass.AP(r1_t, dk * 512, [[8192, 128], [1, 512]])
                    v.tensor_tensor(r1c, r1c, xr2_t[:, (dk % 2) * 512:(dk % 2) * 512 + 512], AL.add)
                    ins = v.tensor_copy(bass.AP(r1b_t, dk * 512, [[8192, 128], [1, 512]]), r1c)
                    ins.then_inc(sems["ev"], 1)
            cnt["mm"] = m0 + 16; cnt["ev"] = e0 + 32
        barrier()
        with nc.Block() as b:
            @b.scalar
            def _(sc):
                for dk in range(16):
                    sc.activation(bass.AP(sq_t, dk * 512, [[8192, 128], [1, 512]]),
                                  bass.AP(r1_t, dk * 512, [[8192, 128], [1, 512]]), ACT.Square)
        barrier()
        with nc.Block() as b:
            m0 = cnt["mm"]; e0 = cnt["ev"]
            @b.tensor
            def _(t):
                for ck in range(16):
                    t.matmul(PXR[0:1, :], ones_t[:], bass.AP(r1b_t, ck * 512, [[8192, 128], [1, 512]]),
                             start=(ck == 0), stop=(ck == 15))
                for ck in range(16):
                    ins = t.matmul(PXI[0:1, :], ones_t[:], bass.AP(sq_t, ck * 512, [[8192, 128], [1, 512]]),
                                   start=(ck == 0), stop=(ck == 15))
                ins.then_inc(sems["mm"], 1)
                t.wait_ge(sems["ev"], e0 + 2)
                ins = t.matmul(PR[:], onesr_t[:], srow(0, 512), start=True, stop=True)
                ins.then_inc(sems["mm"], 1)
                t.wait_ge(sems["ev"], e0 + 3)
                ins = t.matmul(PR[:], onesr_t[:], vrow(0, 512), start=True, stop=True)
                ins.then_inc(sems["mm"], 1)
            @b.vector
            def _(v):
                v.wait_ge(sems["mm"], m0 + 1)
                v.tensor_scalar(srow(0, 512), PXR[0:1, :], 1.0 / D, None, AL.mult)
                v.tensor_scalar(srow2(0, 512), PXI[0:1, :], 1.0 / D, None, AL.mult)
                v.tensor_tensor(vrow(0, 512), srow(0, 512), srow(0, 512), AL.mult)
                v.tensor_tensor(vrow(0, 512), srow2(0, 512), vrow(0, 512), AL.subtract)
                v.tensor_scalar(vrow(0, 512), vrow(0, 512), 1e-5, None, AL.add)
                ins = v.reciprocal(mrow(0, 512), vrow(0, 512))
                ins.then_inc(sems["ev"], 1)
                v.wait_ge(sems["mm"], m0 + 2)
                ins = v.tensor_copy(mb_t[:, 0:512], PR[:])
                ins.then_inc(sems["ev"], 1)
                v.wait_ge(sems["mm"], m0 + 3)
                ins = v.tensor_copy(ib_t[:, 0:512], PR[:])
                ins.then_inc(sems["ev"], 1)
            @b.scalar
            def _(sc):
                sc.wait_ge(sems["ev"], e0 + 1)
                ins = sc.activation(vrow(0, 512), mrow(0, 512), ACT.Sqrt)
                ins.then_inc(sems["ev"], 1)
            cnt["mm"] = m0 + 3; cnt["ev"] = e0 + 4
        barrier()
        with nc.Block() as b:
            e0 = cnt["ev"]
            @b.vector
            def _(v):
                for dk in range(16):
                    r1c = bass.AP(r1_t, dk * 512, [[8192, 128], [1, 512]])
                    v.tensor_tensor(r1c, r1c, mb_t[:, 0:512], AL.subtract)
                    ins = v.tensor_tensor(r1c, r1c, ib_t[:, 0:512], AL.mult)
                    ins.then_inc(sems["ev"], 1)
            @b.sync
            def _(s):
                for dk in range(16):
                    s.wait_ge(sems["ev"], e0 + dk + 1)
                    s.dma_start(out=x2_dram[dk * 128:(dk + 1) * 128, tt * 512:(tt + 1) * 512],
                                in_=bass.AP(r1_t, dk * 512, [[8192, 128], [1, 512]])).then_inc(sems["st"], 16)
                    cnt["st"] += 16
                s.wait_ge(sems["st"], cnt["st"])
            cnt["ev"] = e0 + 16
        barrier()

    for t2 in range(2 if KPH >= 3 else 0):
        tb = t2 * 1024
        with nc.Block() as b:
            @b.gpsimd
            def _(g):
                for dk in range(16):
                    g.dma_start(out=x2b_t[:, dk * 1024:(dk + 1) * 1024],
                                in_=x2_dram[dk * 128:(dk + 1) * 128, tb:tb + 1024]).then_inc(sems["ld"], 16)
                    cnt["ld"] += 16
                g.wait_ge(sems["ld"], cnt["ld"])
        barrier()
        with nc.Block() as b:
            m0 = cnt["mm"]
            @b.tensor
            def _(t):
                for tq in range(2):
                    ps = PXR if tq == 0 else PXI
                    for dk in range(16):
                        ins = t.matmul(ps[0:1, :], ones_t[:],
                                       x2b_t[:, dk * 1024 + tq * 512: dk * 1024 + tq * 512 + 512],
                                       start=(dk == 0), stop=(dk == 15))
                ins.then_inc(sems["mm"], 1)
            @b.vector
            def _(v):
                v.wait_ge(sems["mm"], m0 + 1)
                v.tensor_copy(sxb(0, 512), PXR[0:1, :])
                v.tensor_copy(sxb(512, 1024), PXI[0:1, :])
            cnt["mm"] = m0 + 1
        barrier()
        with nc.Block() as b:
            m0 = cnt["mm"]; e0 = cnt["ev"]; l0 = cnt["ld"]
            @b.gpsimd
            def _(g):
                for hk in range(44):
                    if hk >= 2:
                        g.wait_ge(sems["mm"], m0 + 4 * (hk - 1))
                    buf = hk % 2
                    g.dma_start(out=wg_sl[:, buf * 2048:(buf + 1) * 2048],
                                in_=bass.AP(dr["wgT"].tensor, hk * 128 * 2048, [[2048, 128], [1, 2048]])).then_inc(sems["ld"], 16)
                    g.dma_start(out=wu_sl[:, buf * 2048:(buf + 1) * 2048],
                                in_=bass.AP(dr["wuT"].tensor, hk * 128 * 2048, [[2048, 128], [1, 2048]])).then_inc(sems["ld"], 16)
                    cnt["ld"] += 32
            @b.tensor
            def _(t):
                for hk in range(44):
                    buf = hk % 2
                    t.wait_ge(sems["ld"], l0 + 32 * (hk + 1))
                    if hk > 0:
                        t.wait_ge(sems["ev"], e0 + 4 * hk)
                    for tq in range(2):
                        psG = PA if tq == 0 else PXR
                        psU = PB if tq == 0 else PXI
                        for dk in range(16):
                            t.matmul(psG[:], wg_sl[:, buf * 2048 + dk * 128: buf * 2048 + (dk + 1) * 128],
                                     x2b_t[:, dk * 1024 + tq * 512: dk * 1024 + tq * 512 + 512],
                                     start=(dk == 0), stop=False)
                        ins = t.matmul(psG[:], negc_t[:], sxb(tq * 512, (tq + 1) * 512), start=False, stop=True)
                        ins.then_inc(sems["mm"], 1)
                        for dk in range(16):
                            t.matmul(psU[:], wu_sl[:, buf * 2048 + dk * 128: buf * 2048 + (dk + 1) * 128],
                                     x2b_t[:, dk * 1024 + tq * 512: dk * 1024 + tq * 512 + 512],
                                     start=(dk == 0), stop=False)
                        ins = t.matmul(psU[:], negc_t[:], sxb(tq * 512, (tq + 1) * 512), start=False, stop=True)
                        ins.then_inc(sems["mm"], 1)
            @b.scalar
            def _(sc):
                for hk in range(44):
                    for tq in range(2):
                        sc.wait_ge(sems["mm"], m0 + 4 * hk + 2 * tq + 1)
                        psG = PA if tq == 0 else PXR
                        ins = sc.activation(r2b_t[:, tq * 512:(tq + 1) * 512], psG[:],
                                            ACT.Silu, scale=sg_t[:, hk:hk + 1])
                        ins.then_inc(sems["ev"], 1)
            @b.vector
            def _(v):
                for hk in range(44):
                    for tq in range(2):
                        v.wait_ge(sems["mm"], m0 + 4 * hk + 2 * tq + 2)
                        psU = PB if tq == 0 else PXI
                        v.tensor_scalar(sqb_t[:, tq * 512:(tq + 1) * 512], psU[:],
                                        su_t[:, hk:hk + 1], None, AL.mult)
                        v.wait_ge(sems["ev"], e0 + 4 * hk + 2 * tq + 1)
                        ins = v.tensor_tensor(z_t[:, hk * 1024 + tq * 512: hk * 1024 + (tq + 1) * 512],
                                              r2b_t[:, tq * 512:(tq + 1) * 512],
                                              sqb_t[:, tq * 512:(tq + 1) * 512], AL.mult)
                        ins.then_inc(sems["ev"], 1)
            cnt["mm"] = m0 + 176; cnt["ev"] = e0 + 176
        barrier()
        with nc.Block() as b:
            m0 = cnt["mm"]
            @b.tensor
            def _(t):
                for tq in range(2):
                    ps = PXR if tq == 0 else PXI
                    for hk in range(44):
                        ins = t.matmul(ps[0:1, :], ones_t[:],
                                       z_t[:, hk * 1024 + tq * 512: hk * 1024 + tq * 512 + 512],
                                       start=(hk == 0), stop=(hk == 43))
                ins.then_inc(sems["mm"], 1)
            @b.vector
            def _(v):
                v.wait_ge(sems["mm"], m0 + 1)
                v.tensor_copy(szb(0, 512), PXR[0:1, :])
                v.tensor_copy(szb(512, 1024), PXI[0:1, :])
            cnt["mm"] = m0 + 1
        barrier()
        with nc.Block() as b:
            m0 = cnt["mm"]; e0 = cnt["ev"]; l0 = cnt["ld"]; t0_ = cnt["tp"]
            @b.gpsimd
            def _(g):
                for dk in range(16):
                    if dk >= 1:
                        g.wait_ge(sems["mm"], m0 + 2 * dk)
                    g.dma_start(out=wd_sl[:, 0:5632],
                                in_=bass.AP(dr["wdT"].tensor, dk * 128 * HID, [[HID, 128], [1, HID]])).then_inc(sems["ld"], 16)
                    cnt["ld"] += 16
            @b.sync
            def _(s):
                for dk in range(16):
                    if dk >= 1:
                        s.wait_ge(sems["ev"], e0 + 2 * dk)
                    s.dma_start(out=xr2_t[:], in_=x2_dram[dk * 128:(dk + 1) * 128, tb:tb + 1024]).then_inc(sems["tp"], 16)
                    cnt["tp"] += 16
                    for tq in range(2):
                        s.wait_ge(sems["ev"], e0 + 2 * dk + tq + 1)
                        s.dma_start(out=x2_dram[dk * 128:(dk + 1) * 128, tb + tq * 512: tb + (tq + 1) * 512],
                                    in_=r2f_t[:, tq * 512:(tq + 1) * 512]).then_inc(sems["st"], 16)
                        cnt["st"] += 16
                s.wait_ge(sems["st"], cnt["st"])
            @b.tensor
            def _(t):
                for dk in range(16):
                    t.wait_ge(sems["ld"], l0 + 16 * (dk + 1))
                    if dk >= 1:
                        t.wait_ge(sems["ev"], e0 + 2 * dk)
                    for tq in range(2):
                        ps = PA if tq == 0 else PB
                        for hk in range(44):
                            t.matmul(ps[:], wd_sl[:, hk * 128:(hk + 1) * 128],
                                     z_t[:, hk * 1024 + tq * 512: hk * 1024 + tq * 512 + 512],
                                     start=(hk == 0), stop=False)
                        ins = t.matmul(ps[:], negc_t[:], szb(tq * 512, (tq + 1) * 512), start=False, stop=True)
                        ins.then_inc(sems["mm"], 1)
            @b.vector
            def _(v):
                s0_ = cnt["st"] - 32 * 16
                for dk in range(16):
                    for tq in range(2):
                        v.wait_ge(sems["mm"], m0 + 2 * dk + tq + 1)
                        if tq == 0:
                            v.wait_ge(sems["tp"], t0_ + 16 * (dk + 1))
                        if dk >= 1:
                            v.wait_ge(sems["st"], s0_ + 16 * (2 * (dk - 1) + tq + 1))
                        ps = PA if tq == 0 else PB
                        v.tensor_scalar(r2f_t[:, tq * 512:(tq + 1) * 512], ps[:],
                                        sd_t[:, dk:dk + 1], None, AL.mult)
                        ins = v.tensor_tensor(r2f_t[:, tq * 512:(tq + 1) * 512],
                                              r2f_t[:, tq * 512:(tq + 1) * 512],
                                              xr2_t[:, tq * 512:(tq + 1) * 512], AL.add)
                        ins.then_inc(sems["ev"], 1)
            cnt["mm"] = m0 + 32; cnt["ev"] = e0 + 32
        barrier()
        with nc.Block() as b:
            m0 = cnt["mm"]; e0 = cnt["ev"]; l0 = cnt["ld"]
            @b.sync
            def _(s):
                for dk in range(16):
                    if dk > 0:
                        s.wait_ge(sems["mm"], m0 + dk)
                    s.dma_start(out=xr2_t[:], in_=x2_dram[dk * 128:(dk + 1) * 128, tb:tb + 1024]).then_inc(sems["ld"], 16)
                    cnt["ld"] += 16
            @b.vector
            def _(v):
                for dk in range(16):
                    v.wait_ge(sems["ld"], l0 + 16 * (dk + 1))
                    ins = v.tensor_copy(r2b_t[:], xr2_t[:])
                    ins.then_inc(sems["ev"], 1)
            @b.scalar
            def _(sc):
                for dk in range(16):
                    sc.wait_ge(sems["ev"], e0 + 2 * dk + 1)
                    ins = sc.activation(sqb_t[:], r2b_t[:], ACT.Square)
                    ins.then_inc(sems["ev"], 1)
            @b.tensor
            def _(t):
                for dk in range(16):
                    t.wait_ge(sems["ev"], e0 + 2 * (dk + 1))
                    t.matmul(PXR[0:1, :], ones_t[:], r2b_t[:, 0:512], start=(dk == 0), stop=(dk == 15))
                    t.matmul(PXI[0:1, :], ones_t[:], r2b_t[:, 512:1024], start=(dk == 0), stop=(dk == 15))
                    t.matmul(PA[0:1, :], ones_t[:], sqb_t[:, 0:512], start=(dk == 0), stop=(dk == 15))
                    ins = t.matmul(PB[0:1, :], ones_t[:], sqb_t[:, 512:1024], start=(dk == 0), stop=(dk == 15))
                    ins.then_inc(sems["mm"], 1)
            cnt["mm"] = m0 + 16; cnt["ev"] = e0 + 32
        barrier()
        with nc.Block() as b:
            m0 = cnt["mm"]; e0 = cnt["ev"]
            @b.vector
            def _(v):
                v.tensor_scalar(srow(0, 512), PXR[0:1, :], 1.0 / D, None, AL.mult)
                v.tensor_scalar(srow(512, 1024), PXI[0:1, :], 1.0 / D, None, AL.mult)
                v.tensor_scalar(srow2(0, 512), PA[0:1, :], 1.0 / D, None, AL.mult)
                v.tensor_scalar(srow2(512, 1024), PB[0:1, :], 1.0 / D, None, AL.mult)
                v.tensor_tensor(vrow(0, 1024), srow(0, 1024), srow(0, 1024), AL.mult)
                v.tensor_tensor(vrow(0, 1024), srow2(0, 1024), vrow(0, 1024), AL.subtract)
                v.tensor_scalar(vrow(0, 1024), vrow(0, 1024), 1e-5, None, AL.add)
                ins = v.reciprocal(mrow(0, 1024), vrow(0, 1024))
                ins.then_inc(sems["ev"], 1)
                for j in range(4):
                    v.wait_ge(sems["mm"], m0 + j + 1)
                    dst = (mb2_t, mb2_t, ib2_t, ib2_t)[j]
                    ins = v.tensor_copy(dst[:, (j % 2) * 512:(j % 2) * 512 + 512], PR[:])
                    ins.then_inc(sems["ev"], 1)
            @b.scalar
            def _(sc):
                sc.wait_ge(sems["ev"], e0 + 1)
                ins = sc.activation(vrow(0, 1024), mrow(0, 1024), ACT.Sqrt)
                ins.then_inc(sems["ev"], 1)
            @b.tensor
            def _(t):
                srcs = (srow(0, 512), srow(512, 1024), vrow(0, 512), vrow(512, 1024))
                for j in range(4):
                    t.wait_ge(sems["ev"], e0 + 2 + j)
                    ins = t.matmul(PR[:], onesr_t[:], srcs[j], start=True, stop=True)
                    ins.then_inc(sems["mm"], 1)
            cnt["mm"] = m0 + 4; cnt["ev"] = e0 + 6
        barrier()
        with nc.Block() as b:
            e0 = cnt["ev"]; l0 = cnt["ld"]; s0 = cnt["st"]
            @b.sync
            def _(s):
                for dk in range(16):
                    if dk > 0:
                        s.wait_ge(sems["ev"], e0 + dk)
                    s.dma_start(out=xr2_t[:], in_=x2_dram[dk * 128:(dk + 1) * 128, tb:tb + 1024]).then_inc(sems["ld"], 16)
                    cnt["ld"] += 16
            @b.vector
            def _(v):
                for dk in range(16):
                    v.wait_ge(sems["ld"], l0 + 16 * (dk + 1))
                    if dk > 0:
                        v.wait_ge(sems["st"], s0 + 16 * dk)
                    v.tensor_tensor(r2f_t[:], xr2_t[:], mb2_t[:], AL.subtract)
                    ins = v.tensor_tensor(r2f_t[:], r2f_t[:], ib2_t[:], AL.mult)
                    ins.then_inc(sems["ev"], 1)
            @b.scalar
            def _(sc):
                for dk in range(16):
                    sc.wait_ge(sems["ev"], e0 + dk + 1)
                    sc.dma_start(out=out_t[dk * 128:(dk + 1) * 128, tb:tb + 1024],
                                 in_=r2f_t[:]).then_inc(sems["st"], 16)
                    cnt["st"] += 16
                sc.wait_ge(sems["st"], cnt["st"])
            cnt["ev"] = e0 + 16
        barrier()

    barrier()

    _NC_CACHE["nc"] = nc
    return nc


def _prep_shared(inputs):
    f32 = np.float32
    w = np.asarray(inputs["L0_attn_in_proj_w"], f32)
    WmixT = np.ascontiguousarray(
        w.reshape(3, NB, 256, D).transpose(1, 0, 2, 3).reshape(NB * 3 * 256, D).T)
    bmix = np.ascontiguousarray(
        np.asarray(inputs["L0_attn_in_proj_b"], f32).reshape(3, NB, 256)
        .transpose(1, 0, 2).reshape(48, 128).T)
    sw = np.asarray(inputs["L0_attn_short_w"], f32)[:, 0, :]
    swT = np.ascontiguousarray(
        sw.reshape(3, NB, 256, 3).transpose(1, 0, 2, 3).reshape(48, 128, 3)
        .transpose(1, 0, 2).reshape(128, 144))
    sbT = np.ascontiguousarray(
        np.asarray(inputs["L0_attn_short_b"], f32).reshape(3, NB, 256)
        .transpose(1, 0, 2).reshape(48, 128).T)
    fbias = np.ascontiguousarray(np.asarray(inputs["L0_attn_filter_bias"], f32).reshape(16, 128).T)
    wo = np.asarray(inputs["L0_attn_out_proj_w"], f32)
    WoutT = np.ascontiguousarray(
        wo.reshape(16, 128, 16, 128).transpose(0, 3, 2, 1).reshape(16, 128, 2048))
    bout = np.ascontiguousarray(np.asarray(inputs["L0_attn_out_proj_b"], f32).reshape(16, 128).T)
    zT, dect = _consts_filter()
    dft = _consts_dft()

    def u8view(a):
        a = np.ascontiguousarray(np.asarray(a, np.int32))
        return a.view(np.uint8).reshape(a.shape[0], a.shape[1], 4)[..., 0]

    def pack_gu(a):
        t = a.reshape(44, 128, 16, 128)
        return np.ascontiguousarray(t.transpose(0, 3, 2, 1).reshape(44, 128, 2048))
    def pack_d(a):
        t = a.reshape(16, 128, 44, 128)
        return np.ascontiguousarray(t.transpose(0, 3, 2, 1).reshape(16, 128, 5632))
    wgT = pack_gu(u8view(inputs["L0_ffn_wq_gate"]))
    wuT = pack_gu(u8view(inputs["L0_ffn_wq_up"]))
    wdT = pack_d(u8view(inputs["L0_ffn_wq_down"]))
    sg = np.ascontiguousarray(np.asarray(inputs["L0_ffn_s_gate"], f32).reshape(44, 128).T)
    su = np.ascontiguousarray(np.asarray(inputs["L0_ffn_s_up"], f32).reshape(44, 128).T)
    sd = np.ascontiguousarray(np.asarray(inputs["L0_ffn_s_down"], f32).reshape(16, 128).T)
    shared = {
        "WmixT": WmixT, "bmix": bmix, "swT": swT, "sbT": sbT, "fbias": fbias,
        "WoutT": WoutT, "bout": bout,
        "w0T": np.ascontiguousarray(np.asarray(inputs["L0_attn_mlp_w0"], f32).T),
        "w1T": np.ascontiguousarray(np.asarray(inputs["L0_attn_mlp_w1"], f32).T),
        "w2T": np.ascontiguousarray(np.asarray(inputs["L0_attn_mlp_w2"], f32).T),
        "w3T": np.ascontiguousarray(np.asarray(inputs["L0_attn_mlp_w3"], f32).T),
        "b0": np.asarray(inputs["L0_attn_mlp_b0"], f32).reshape(64, 1),
        "b1": np.asarray(inputs["L0_attn_mlp_b1"], f32).reshape(64, 1),
        "b2": np.asarray(inputs["L0_attn_mlp_b2"], f32).reshape(64, 1),
        "freq": np.ascontiguousarray(np.asarray(inputs["L0_attn_sin_freq"], f32).reshape(64, 1)),
        "zT": zT, "dect": dect,
        "wgT": wgT, "wuT": wuT, "wdT": wdT, "sg": sg, "su": su, "sd": sd,
        "negc": np.full((1, 128), -128.0, bf16),
    }
    shared.update(dft)
    return shared


def kernel(**inputs):
    x = np.asarray(inputs["L0_attn_x"], np.float32)
    shared = _prep_shared(inputs)
    nc = build_program()
    in_maps = []
    for b in range(8):
        m = dict(shared)
        m["xT"] = np.ascontiguousarray(x[b].T)
        in_maps.append(m)
    res = run_bass_kernel_spmd(nc, in_maps, core_ids=list(range(8)))
    out = np.empty((8, L, D), np.float32)
    for b in range(8):
        out[b] = res.results[b]["outT"].T
    return out


# version 10
# speedup vs baseline: 1.3798x; 1.0760x over previous
import math
import os
import numpy as np
import ml_dtypes
import concourse.bass as bass
import concourse.mybir as mybir
from concourse.bass_utils import run_bass_kernel_spmd

F32 = mybir.dt.float32
BF16 = mybir.dt.bfloat16
U8 = mybir.dt.uint8
I32 = mybir.dt.int32
AL = mybir.AluOpType
ACT = mybir.ActivationFunctionType
bf16 = ml_dtypes.bfloat16

L = 2048
D = 2048
HID = 5632
NB = 8
TWO_PI = 2.0 * math.pi

MB = 66048
SEG = 16384
OFF_P = [MB + i * SEG for i in range(7)]
OFF_B = MB + 7 * SEG
OFF_T = [OFF_B + i * 8192 for i in range(4)]
OFF_KVT = OFF_B + 4 * 8192


def _consts_dft():
    s_ = np.arange(32); p_ = np.arange(64); r_ = np.arange(128); q_ = np.arange(32)
    th1 = 2 * np.pi * (np.einsum('p,r->pr', p_, r_)[None] / 128
                       + np.einsum('s,r->sr', s_, r_)[:, None, :] / 4096)
    W1c, W1s = np.cos(th1), -np.sin(th1)
    W1R = np.zeros((128, 2048), np.float32); W1I = np.zeros((128, 2048), np.float32)
    for u in range(16):
        for phi in range(2):
            W1R[phi * 64:phi * 64 + 64, u * 128:u * 128 + 128] = W1c[2 * u + phi]
            W1I[phi * 64:phi * 64 + 64, u * 128:u * 128 + 128] = W1s[2 * u + phi]
    def bdp(m):
        out = np.zeros((128, 128), np.float32)
        for c4 in range(4):
            idx = np.arange(32) * 4 + c4
            out[np.ix_(idx, idx)] = m
        return out
    th2 = 2 * np.pi * np.einsum('s,q->sq', s_, q_) / 32
    W2re, W2im = np.cos(th2), -np.sin(th2)
    th3 = 2 * np.pi * np.einsum('q,s->qs', q_, s_) / 32
    W3re, W3im = np.cos(th3) / 4096.0, np.sin(th3) / 4096.0
    th4 = 2 * np.pi * (np.einsum('p,r->pr', p_, r_)[None] / 128
                       + np.einsum('s,r->sr', s_, r_)[:, None, :] / 4096)
    W4re, W4im = np.cos(th4), np.sin(th4)
    W4R = np.zeros((128, 2048), np.float32); W4IN = np.zeros((128, 2048), np.float32)
    for s in range(32):
        W4R[:, s * 64:s * 64 + 64] = W4re[s].T
        W4IN[:, s * 64:s * 64 + 64] = -W4im[s].T
    c = {"W1R": W1R, "W1I": W1I, "W4R": W4R, "W4IN": W4IN,
         "W2RE": bdp(W2re), "W2IM": bdp(W2im), "W2IMN": bdp(-W2im),
         "W3RE": bdp(W3re), "W3IM": bdp(W3im), "W3IMN": bdp(-W3im)}
    return {k: v.astype(bf16) for k, v in c.items()}


def _consts_filter():
    t = np.linspace(0.0, 1.0, L)[:, None]
    t_resc = np.linspace(0.0, L - 1.0, L)
    w = 2.0 * np.pi * t_resc / L
    f = np.linspace(0.0001, 0.0, 1)
    ang = f[None, :] * w[:, None]
    z = np.concatenate([t, np.cos(ang), -np.sin(ang)], axis=-1)
    zT = np.ascontiguousarray(z.T).astype(bf16)
    deltas = np.abs(np.linspace(np.log(0.01) / 1.5, np.log(0.01) / 0.3, D))
    decay = np.exp(-t * deltas[None, :]).astype(np.float32)
    dect = np.zeros((128, NB * 4096), np.float32)
    pp = np.arange(64)
    for b in range(NB):
        for u in range(16):
            for phi in range(2):
                tv = 2 * u + phi + 32 * pp
                dect[phi * 64:phi * 64 + 64, b * 4096 + u * 256:b * 4096 + (u + 1) * 256] = \
                    decay[tv][:, b * 256:(b + 1) * 256]
    return zT, dect.astype(bf16)


_NC_CACHE = {}


def build_program():
    if "nc" in _NC_CACHE:
        return _NC_CACHE["nc"]
    nc = bass.Bass("TRN2", target_bir_lowering=False)

    dr = {}
    def din(name, shape, dtype=F32):
        dr[name] = nc.dram_tensor(name, list(shape), dtype, kind="ExternalInput").ap()

    din("xT", [D, L])
    din("WmixT", [D, 3 * D])
    din("bmix", [128, 48]); din("swT", [128, 144]); din("sbT", [128, 48])
    din("fbias", [128, 16])
    din("WoutT", [16, 128, 2048]); din("bout", [128, 16])
    din("w0T", [3, 64]); din("w1T", [64, 64]); din("w2T", [64, 64]); din("w3T", [64, D])
    din("b0", [64, 1]); din("b1", [64, 1]); din("b2", [64, 1]); din("freq", [64, 1])
    din("zT", [3, L], BF16)
    din("dect", [128, NB * 4096], BF16)
    for n in ("W1R", "W1I", "W4R", "W4IN"):
        din(n, [128, 2048], BF16)
    for n in ("W2RE", "W2IM", "W2IMN", "W3RE", "W3IM", "W3IMN"):
        din(n, [128, 128], BF16)
    din("wgT", [44, 128, 2048], U8); din("wuT", [44, 128, 2048], U8)
    din("wdT", [16, 128, HID], U8)
    din("sg", [128, 44]); din("su", [128, 44]); din("sd", [128, 16])
    din("negc", [1, 128], BF16)

    out_t = nc.dram_tensor("outT", [D, L], F32, kind="ExternalOutput").ap()
    y_dram = nc.dram_tensor("Yscr", [D, L], BF16, kind="Internal").ap()
    x2_dram = nc.dram_tensor("X2scr", [D, L], F32, kind="Internal").ap()

    def sbn(name, shape, dtype=BF16):
        return nc.alloc_sbuf_tensor(name, list(shape), dtype)

    C = {}
    for n in ("W1R", "W1I", "W4R", "W4IN"):
        C[n] = sbn("C" + n, [128, 2048])
    for n in ("W2RE", "W2IM", "W2IMN", "W3RE", "W3IM", "W3IMN"):
        C[n] = sbn("C" + n, [128, 128])
    bmix_t = sbn("bmix_t", [128, 48], F32)
    swT_t = sbn("swT_t", [128, 144], F32)
    sbT_t = sbn("sbT_t", [128, 48], F32)
    fbias_t = sbn("fbias_t", [128, 16], F32)
    bout_t = sbn("bout_t", [128, 16], F32)
    sg_t = sbn("sg_t", [128, 44], F32); su_t = sbn("su_t", [128, 44], F32)
    sd_t = sbn("sd_t", [128, 16], F32)
    w0T_t = sbn("w0T_t", [3, 64]); w1T_t = sbn("w1T_t", [64, 64]); w2T_t = sbn("w2T_t", [64, 64])
    w3T_t = sbn("w3T_t", [64, D])
    b0_t = sbn("b0_t", [64, 1], F32); b1_t = sbn("b1_t", [64, 1], F32)
    b2_t = sbn("b2_t", [64, 1], F32); fq_t = sbn("fq_t", [64, 1], F32)
    hhp_t = sbn("hhp_t", [64, L])
    ones_t = sbn("ones_t", [128, 1])
    onesr_t = sbn("onesr_t", [1, 128], F32)
    negc_t = sbn("negc_t", [1, 128])
    rows_t = sbn("rows_t", [1, 4096], F32)
    sxzb_t = sbn("sxzb_t", [1, 2048])

    def srow(a, b_):
        return rows_t[0:1, a:b_]
    def srow2(a, b_):
        return rows_t[0:1, 1024 + a:1024 + b_]
    def mrow(a, b_):
        return rows_t[0:1, 2048 + a:2048 + b_]
    def vrow(a, b_):
        return rows_t[0:1, 3072 + a:3072 + b_]
    def sxb(a, b_):
        return sxzb_t[0:1, a:b_]
    def szb(a, b_):
        return sxzb_t[0:1, 1024 + a:1024 + b_]
    assert nc.sbuf_bytes_remaining > (229344 - MB), nc.sbuf_bytes_remaining

    def sbm(name, offset, shape, dtype=BF16):
        return nc.alloc_sbuf_tensor_at(name, list(shape), dtype, offset=offset, align_bytes=64)

    P = [sbm(f"P{i+1}", OFF_P[i], [128, 8192]) for i in range(7)]
    T0 = sbm("T0", OFF_T[0], [128, 4096]); T1 = sbm("T1", OFF_T[1], [128, 4096])
    T2 = sbm("T2", OFF_T[2], [128, 4096]); T3 = sbm("T3", OFF_T[3], [128, 4096])
    KVT = sbm("KVT", OFF_KVT, [128, 4096])
    arg_t = sbm("arg_t", OFF_P[0], [64, 2048], F32)
    un_t = sbm("un_t", OFF_P[1], [64, 2048], F32)
    ni_t = sbm("ni_t", OFF_P[2], [64, 2048], I32)
    nf_t = sbm("nf_t", OFF_P[3], [64, 2048], F32)
    hh_t = sbm("hh_t", OFF_P[4], [64, 2048])
    zT_t = sbm("zT_t", OFF_P[5], [3, L])
    CtkR, CtkI = P[0], P[1]
    TCR, TCI = P[2], P[3]
    KRE, KIM = P[0], P[1]
    CtvR, CtvI = P[4], P[5]
    XRE, XIM = P[4], P[5]
    S1 = P[6]
    TER, TEI = P[0], P[1]
    ype = sbm("ype", OFF_P[6], [128, 4096])
    yperm = sbm("yperm", OFF_P[6] + 8192, [128, 4096])
    dect_b = sbm("dect_b", OFF_P[6], [128, 4096])
    xtile = P[6]
    wmixb = sbm("wmixb", OFF_P[4], [128, 12288])
    wo_sl = sbm("wo_sl", OFF_P[0], [128, 4096])
    ytile = sbm("ytile", OFF_P[4], [128, 8192])
    r1_t = sbm("r1_t", OFF_P[5], [128, 8192], F32)
    r1b_t = sbm("r1b_t", OFF_P[1], [128, 8192])
    sq_t = sbm("sq_t", OFF_P[2], [128, 8192])
    z_t = sbm("z_t", MB, [128, 45056])
    x2b_t = sbm("x2b_t", MB + 90112, [128, 16384])
    wd_sl = sbm("wd_sl", MB + 122880, [128, 5632])
    wg_sl = sbm("wg_sl", MB + 134144, [128, 4096])
    wu_sl = sbm("wu_sl", MB + 142336, [128, 4096])
    xr2_t = sbm("xr2_t", MB + 150528, [128, 1024], F32)
    r2f_t = sbm("r2f_t", MB + 154624, [128, 1024], F32)
    mb2_t = sbm("mb2_t", MB + 134144, [128, 1024], F32)
    ib2_t = sbm("ib2_t", MB + 138240, [128, 1024], F32)
    r2b_t = sbm("r2b_t", MB + 158720, [128, 1024])
    sqb_t = sbm("sqb_t", MB + 160768, [128, 1024])

    mb_t = mb2_t
    ib_t = ib2_t
    xr_t = xr2_t

    PS = []
    def psum(name, shape):
        g = nc.psum_tensor(name, list(shape), F32)
        t = g.__enter__(); PS.append(g)
        return t
    PA = psum("PA", [128, 512])
    PB = psum("PB", [128, 512])
    PCe = psum("PCe", [128, 256])
    PCo = psum("PCo", [128, 256])
    PXR = psum("PXR", [128, 512])
    PXI = psum("PXI", [128, 512])
    P2m = psum("P2m", [128, 256])
    PR = psum("PR", [128, 512])

    sems = {}
    for nm in ("ld", "mm", "ev", "tp", "st", "bar"):
        g = nc.semaphore("s_" + nm)
        sems[nm] = g.__enter__()
    cnt = {"ld": 0, "mm": 0, "ev": 0, "tp": 0, "st": 0, "bar": 0}
    ENG = ("sync", "scalar", "vector", "tensor", "gpsimd")

    def barrier():
        cnt["bar"] += 1
        tgt = 5 * cnt["bar"]
        with nc.Block() as b:
            for e in ENG:
                def mk(e=e):
                    def f(eng):
                        eng.sem_inc(sems["bar"], 1)
                        eng.wait_ge(sems["bar"], tgt)
                    return f
                getattr(b, e)(mk())

    with nc.Block() as b:
        @b.gpsimd
        def _(g):
            for nm, t in (("bmix", bmix_t), ("swT", swT_t), ("sbT", sbT_t), ("fbias", fbias_t),
                          ("bout", bout_t), ("sg", sg_t), ("su", su_t), ("sd", sd_t),
                          ("b0", b0_t), ("b1", b1_t), ("b2", b2_t), ("freq", fq_t),
                          ("w0T", w0T_t), ("w1T", w1T_t), ("w2T", w2T_t), ("w3T", w3T_t)):
                g.dma_start(out=t[:], in_=dr[nm][:]).then_inc(sems["ld"], 16)
                cnt["ld"] += 16
            g.memset(ones_t[:], 1.0)
            g.memset(onesr_t[:], 1.0)
        @b.sync
        def _(s):
            for nm, t in (("zT", zT_t), ("negc", negc_t),
                          ("W1R", C["W1R"]), ("W1I", C["W1I"]), ("W4R", C["W4R"]), ("W4IN", C["W4IN"]),
                          ("W2RE", C["W2RE"]), ("W2IM", C["W2IM"]), ("W2IMN", C["W2IMN"]),
                          ("W3RE", C["W3RE"]), ("W3IM", C["W3IM"]), ("W3IMN", C["W3IMN"])):
                s.dma_start(out=t[:], in_=dr[nm][:]).then_inc(sems["ld"], 16)
                cnt["ld"] += 16
            s.wait_ge(sems["ld"], cnt["ld"])
    barrier()

    def mlp_layer(lhsT, rhs_q, bias):
        with nc.Block() as b:
            m0 = cnt["mm"]; e0 = cnt["ev"]
            @b.tensor
            def _(t):
                for j in range(4):
                    if j > 0:
                        t.wait_ge(sems["ev"], e0 + j)
                    ins = t.matmul(PA[:64, :], lhsT, rhs_q[j], start=True, stop=True)
                    ins.then_inc(sems["mm"], 1)
            @b.vector
            def _(v):
                for j in range(4):
                    v.wait_ge(sems["mm"], m0 + j + 1)
                    cols = slice(j * 512, (j + 1) * 512)
                    v.tensor_scalar(arg_t[:, cols], PA[:64, :], bias[:], fq_t[:], AL.add, AL.mult)
                    v.tensor_scalar(un_t[:, cols], arg_t[:, cols], 1.0 / TWO_PI, None, AL.mult)
                    v.tensor_copy(ni_t[:, cols], un_t[:, cols])
                    v.tensor_copy(nf_t[:, cols], ni_t[:, cols])
                    ins = v.scalar_tensor_tensor(arg_t[:, cols], nf_t[:, cols], -TWO_PI,
                                                 arg_t[:, cols], AL.mult, AL.add)
                    ins.then_inc(sems["ev"], 1)
            @b.scalar
            def _(sc):
                for j in range(4):
                    sc.wait_ge(sems["ev"], e0 + j + 1)
                    cols = slice(j * 512, (j + 1) * 512)
                    sc.activation(hh_t[:, cols], arg_t[:, cols], ACT.Sin)
            cnt["mm"] = m0 + 4; cnt["ev"] = e0 + 4
        barrier()

    mlp_layer(w0T_t[:], [zT_t[:, j * 512:(j + 1) * 512] for j in range(4)], b0_t)
    mlp_layer(w1T_t[:], [hh_t[:, j * 512:(j + 1) * 512] for j in range(4)], b1_t)
    mlp_layer(w2T_t[:], [hh_t[:, j * 512:(j + 1) * 512] for j in range(4)], b2_t)
    with nc.Block() as b:
        @b.vector
        def _(v):
            ap_in = bass.AP(hh_t, 0, [[L, 64], [1, 32], [32, 64]])
            v.tensor_copy(hhp_t[:].rearrange("a (s p) -> a s p", s=32), ap_in)
    barrier()

    def F1(srcT, dR, dI):
        with nc.Block() as b:
            m0 = cnt["mm"]; e0 = cnt["ev"]
            @b.tensor
            def _(t):
                for u in range(16):
                    if u >= 1:
                        t.wait_ge(sems["ev"], e0 + 4 * u)
                    rhsE = bass.AP(srcT, u * 128, [[4096, 64], [2048, 2], [1, 128]])
                    rhsO = bass.AP(srcT, 64 * 4096 + u * 128, [[4096, 64], [2048, 2], [1, 128]])
                    wre = C["W1R"][:, u * 128:(u + 1) * 128]
                    wim = C["W1I"][:, u * 128:(u + 1) * 128]
                    t.matmul(PCe[:], wre[0:64, :], rhsE, start=True, stop=True, tile_position=(0, 0))
                    t.matmul(PCo[:], wim[0:64, :], rhsE, start=True, stop=True, tile_position=(0, 0))
                    t.matmul(P2m[:], wre[64:128, :], rhsO, start=True, stop=True, tile_position=(64, 0))
                    ins = t.matmul(PR[:, 0:256], wim[64:128, :], rhsO, start=True, stop=True, tile_position=(64, 0))
                    ins.then_inc(sems["mm"], 1)
            def evac_body(eng, cpy, jobs):
                for u in range(16):
                    eng.wait_ge(sems["mm"], m0 + u + 1)
                    for so, psrc, dst in jobs:
                        ap_out = bass.AP(dst, 8 * u + 4 * so, [[8192, 128], [4096, 2], [128, 32], [1, 4]])
                        ins = cpy(ap_out, psrc)
                        ins.then_inc(sems["ev"], 1)
            @b.vector
            def _(v):
                evac_body(v, v.tensor_copy, ((0, PCe[:], dR), (0, PCo[:], dI)))
            @b.scalar
            def _(sc):
                evac_body(sc, sc.copy, ((1, P2m[:], dR), (1, PR[:, 0:256], dI)))
            cnt["mm"] = m0 + 16; cnt["ev"] = e0 + 64
        barrier()

    def turn(srcR, srcI, dstR, dstI):
        with nc.Block() as b:
            t0_ = cnt["tp"]
            @b.sync
            def _(s):
                s.dma_start_transpose(
                    dstR[:].rearrange("a (g r) -> a g r", g=64),
                    srcR[:].rearrange("a (g f) -> a g f", g=64)).then_inc(sems["tp"], 16)
                s.dma_start_transpose(
                    dstI[:].rearrange("a (g r) -> a g r", g=64),
                    srcI[:].rearrange("a (g f) -> a g f", g=64)).then_inc(sems["tp"], 16)
                s.wait_ge(sems["tp"], t0_ + 32)
            cnt["tp"] = t0_ + 32
        barrier()

    def F2(WRE, WIM, WIMN, srcR, srcI, dstRE, dstIM):
        with nc.Block() as b:
            m0 = cnt["mm"]; e0 = cnt["ev"]
            @b.tensor
            def _(t):
                for blk in range(16):
                    if blk >= 1:
                        t.wait_ge(sems["ev"], e0 + 2 * blk)
                    cols = slice(blk * 512, blk * 512 + 512)
                    t.matmul(PXR[:], C[WRE][:], srcR[:, cols], start=True, stop=False)
                    t.matmul(PXR[:], C[WIMN][:], srcI[:, cols], start=False, stop=True)
                    t.matmul(PXI[:], C[WIM][:], srcR[:, cols], start=True, stop=False)
                    ins = t.matmul(PXI[:], C[WRE][:], srcI[:, cols], start=False, stop=True)
                    ins.then_inc(sems["mm"], 1)
            @b.vector
            def _(v):
                for blk in range(16):
                    v.wait_ge(sems["mm"], m0 + blk + 1)
                    cols = slice(blk * 512, blk * 512 + 512)
                    ins = v.tensor_copy(dstRE[:, cols], PXR[:])
                    ins.then_inc(sems["ev"], 1)
            @b.scalar
            def _(sc):
                for blk in range(16):
                    sc.wait_ge(sems["mm"], m0 + blk + 1)
                    cols = slice(blk * 512, blk * 512 + 512)
                    ins = sc.copy(dstIM[:, cols], PXI[:])
                    ins.then_inc(sems["ev"], 1)
            cnt["mm"] = m0 + 16; cnt["ev"] = e0 + 32
        barrier()

    def conv_batch(bi):
        with nc.Block() as b:
            m0 = cnt["mm"]; e0 = cnt["ev"]; t0_ = cnt["tp"]; l0 = cnt["ld"]
            @b.sync
            def _(s):
                s.dma_start(out=dect_b[:], in_=dr["dect"][:, bi * 4096:(bi + 1) * 4096]).then_inc(sems["tp"], 16)
                cnt["tp"] += 16
            @b.tensor
            def _(t):
                for u in range(16):
                    if u > 0:
                        t.wait_ge(sems["ev"], e0 + u)
                    ins = t.matmul(PA[:, 0:256], hhp_t[:, u * 128:(u + 1) * 128],
                                   w3T_t[:, bi * 256:(bi + 1) * 256], start=True, stop=True)
                    ins.then_inc(sems["mm"], 1)
            @b.vector
            def _(v):
                v.wait_ge(sems["tp"], t0_ + 16)
                for u in range(16):
                    v.wait_ge(sems["mm"], m0 + u + 1)
                    for h in range(2):
                        ins = v.tensor_tensor(
                            KVT[:, h * 2048 + u * 128: h * 2048 + u * 128 + 128],
                            PA[:, h * 128:(h + 1) * 128],
                            dect_b[:, u * 256 + h * 128: u * 256 + (h + 1) * 128], AL.mult)
                    ins.then_inc(sems["ev"], 1)
            @b.gpsimd
            def _(g):
                for dk in range(16):
                    g.dma_start(out=wmixb[:, dk * 768:(dk + 1) * 768],
                                in_=dr["WmixT"][dk * 128:(dk + 1) * 128, bi * 768:(bi + 1) * 768]).then_inc(sems["ld"], 16)
                    cnt["ld"] += 16
                g.wait_ge(sems["ev"], e0 + 16)
                for dk in range(16):
                    g.dma_start(out=xtile[:, dk * 512:(dk + 1) * 512],
                                in_=dr["xT"][dk * 128:(dk + 1) * 128, 0:512]).then_inc(sems["ld"], 16)
                    cnt["ld"] += 16
            cnt["mm"] = m0 + 16; cnt["ev"] = e0 + 16
        barrier()
        F1(KVT, CtkR, CtkI)
        turn(CtkR, CtkI, TCR, TCI)
        F2("W2RE", "W2IM", "W2IMN", TCR, TCI, KRE, KIM)

        SW1 = (PA, PB, PXR, PXI)
        for tt in range(4):
            with nc.Block() as b:
                m0 = cnt["mm"]; e0 = cnt["ev"]; l0 = cnt["ld"] - 256
                @b.tensor
                def _(t):
                    for dk in range(16):
                        t.wait_ge(sems["ld"], l0 + 16 * (dk + 1))
                        for rc in range(4):
                            ins = t.matmul(SW1[rc][:], wmixb[:, dk * 768 + rc * 128: dk * 768 + (rc + 1) * 128],
                                           xtile[:, dk * 512:(dk + 1) * 512],
                                           start=(dk == 0), stop=(dk == 15))
                    ins.then_inc(sems["mm"], 1)
                    t.wait_ge(sems["ev"], e0 + 4)
                    for dk in range(16):
                        for rc in range(4, 6):
                            ins = t.matmul((PA, PB)[rc - 4][:], wmixb[:, dk * 768 + rc * 128: dk * 768 + (rc + 1) * 128],
                                           xtile[:, dk * 512:(dk + 1) * 512],
                                           start=(dk == 0), stop=(dk == 15))
                    ins.then_inc(sems["mm"], 1)
                @b.scalar
                def _(sc):
                    for rc in range(6):
                        sc.wait_ge(sems["mm"], m0 + (1 if rc < 4 else 2))
                        ps = SW1[rc] if rc < 4 else (PA, PB)[rc - 4]
                        role, half = rc // 2, rc % 2
                        dstT = (T0, T1, T2)[role]
                        ins = sc.activation(dstT[:, half * 2048 + tt * 512: half * 2048 + (tt + 1) * 512],
                                            ps[:], ACT.Identity, bias=bmix_t[:, bi * 6 + rc: bi * 6 + rc + 1])
                        ins.then_inc(sems["ev"], 1)
                @b.gpsimd
                def _(g):
                    if tt < 3:
                        g.wait_ge(sems["mm"], m0 + 2)
                        for dk in range(16):
                            g.dma_start(out=xtile[:, dk * 512:(dk + 1) * 512],
                                        in_=dr["xT"][dk * 128:(dk + 1) * 128, (tt + 1) * 512:(tt + 2) * 512]).then_inc(sems["ld"], 16)
                            cnt["ld"] += 16
                cnt["mm"] = m0 + 2; cnt["ev"] = e0 + 6
            barrier()

        with nc.Block() as b:
            def taps(eng, h):
                for (srcT, dstT, role) in ((T0, T3, 0), (T1, T0, 1), (T2, T1, 2)):
                    j = bi * 6 + role * 2 + h
                    cs = h * 2048
                    w0s = swT_t[:, j * 3 + 0: j * 3 + 1]
                    w1s = swT_t[:, j * 3 + 1: j * 3 + 2]
                    w2s = swT_t[:, j * 3 + 2: j * 3 + 3]
                    eng.tensor_scalar(dstT[:, cs:cs + 2048], srcT[:, cs:cs + 2048],
                                      w2s, sbT_t[:, j:j + 1], AL.mult, AL.add)
                    eng.scalar_tensor_tensor(dstT[:, cs + 1:cs + 2048], srcT[:, cs:cs + 2047],
                                             w1s, dstT[:, cs + 1:cs + 2048], AL.mult, AL.add)
                    eng.scalar_tensor_tensor(dstT[:, cs + 2:cs + 2048], srcT[:, cs:cs + 2046],
                                             w0s, dstT[:, cs + 2:cs + 2048], AL.mult, AL.add)
                eng.tensor_tensor(T1[:, h * 2048:(h + 1) * 2048], T1[:, h * 2048:(h + 1) * 2048],
                                  T0[:, h * 2048:(h + 1) * 2048], AL.mult)
                ap_in = bass.AP(T1, h * 2048, [[4096, 128], [1, 32], [32, 64]])
                out_ap = bass.AP(T2, h * 2048, [[4096, 128], [64, 32], [1, 64]])
                ins = eng.tensor_copy(out_ap, ap_in)
                ins.then_inc(sems["ev"], 1)
            @b.vector
            def _(v):
                taps(v, 0)
                taps(v, 1)
            cnt["ev"] += 2
        barrier()
        with nc.Block() as b:
            @b.sync
            def _(s):
                s.dma_start_transpose(
                    KVT[:].rearrange("a (t f) -> a t f", t=32),
                    T2[:].rearrange("a (t f) -> a t f", t=32)).then_inc(sems["tp"], 16)
                cnt["tp"] += 16
                s.wait_ge(sems["tp"], cnt["tp"])
        barrier()
        F1(KVT, CtvR, CtvI)
        turn(CtvR, CtvI, TCR, TCI)
        F2("W2RE", "W2IM", "W2IMN", TCR, TCI, XRE, XIM)

        with nc.Block() as b:
            def prod(eng, c0, c1):
                sl = slice(c0, c1)
                eng.tensor_tensor(S1[:, sl], XRE[:, sl], KIM[:, sl], AL.mult)
                eng.tensor_tensor(XRE[:, sl], XRE[:, sl], KRE[:, sl], AL.mult)
                eng.tensor_tensor(KRE[:, sl], XIM[:, sl], KRE[:, sl], AL.mult)
                eng.tensor_tensor(XIM[:, sl], XIM[:, sl], KIM[:, sl], AL.mult)
                eng.tensor_tensor(XRE[:, sl], XRE[:, sl], XIM[:, sl], AL.subtract)
                eng.tensor_tensor(S1[:, sl], S1[:, sl], KRE[:, sl], AL.add)
            @b.vector
            def _(v):
                prod(v, 0, 5120)
            @b.gpsimd
            def _(g):
                prod(g, 5120, 8192)
        barrier()
        F2("W3RE", "W3IM", "W3IMN", XRE, S1, TCR, TCI)
        turn(TCR, TCI, TER, TEI)
        with nc.Block() as b:
            m0 = cnt["mm"]; e0 = cnt["ev"]
            @b.tensor
            def _(t):
                for u in range(16):
                    if u > 0:
                        t.wait_ge(sems["ev"], e0 + u)
                    for phi in range(2):
                        s = 2 * u + phi
                        rhsR = bass.AP(TER, 4 * s, [[8192, 128], [128, 64], [1, 4]])
                        rhsI = bass.AP(TEI, 4 * s, [[8192, 128], [128, 64], [1, 4]])
                        tpos = (0, 64 * phi)
                        outp = P2m[64 * phi:64 * phi + 64, :]
                        t.matmul(outp, C["W4R"][:, s * 64:s * 64 + 64], rhsR, start=True, stop=False, tile_position=tpos)
                        ins = t.matmul(outp, C["W4IN"][:, s * 64:s * 64 + 64], rhsI, start=False, stop=True, tile_position=tpos)
                    ins.then_inc(sems["mm"], 1)
            @b.vector
            def _(v):
                for u in range(16):
                    v.wait_ge(sems["mm"], m0 + u + 1)
                    v.tensor_copy(bass.AP(ype, u * 128, [[4096, 128], [1, 128]]), P2m[:, 0:128])
                    ins = v.tensor_copy(bass.AP(ype, 2048 + u * 128, [[4096, 128], [1, 128]]), P2m[:, 128:256])
                    ins.then_inc(sems["ev"], 1)
            cnt["mm"] = m0 + 16; cnt["ev"] = e0 + 16
        barrier()
        with nc.Block() as b:
            @b.sync
            def _(s):
                s.dma_start_transpose(
                    yperm[:].rearrange("a (t f) -> a t f", t=32),
                    ype[:].rearrange("a (t f) -> a t f", t=32)).then_inc(sems["tp"], 16)
                cnt["tp"] += 16
                s.wait_ge(sems["tp"], cnt["tp"])
        barrier()
        with nc.Block() as b:
            e0 = cnt["ev"]
            @b.vector
            def _(v):
                for h in range(2):
                    ap_y = bass.AP(yperm, h * 2048, [[4096, 128], [1, 64], [128, 16], [64, 2]])
                    ap_g1o = bass.AP(T2, h * 2048, [[4096, 128], [32, 64], [2, 16], [1, 2]])
                    fb = fbias_t[:, bi * 2 + h: bi * 2 + h + 1]
                    vt_ap = bass.AP(T1, h * 2048, [[4096, 128], [32, 64], [2, 16], [1, 2]])
                    ins = v.scalar_tensor_tensor(ap_g1o, vt_ap, fb, ap_y, AL.mult, AL.add)
                    if h == 1:
                        ins.then_inc(sems["ev"], 1)
                ins = v.tensor_tensor(T0[:, 0:2048], T2[:, 0:2048], T3[:, 0:2048], AL.mult)
                ins.then_inc(sems["ev"], 1)
            @b.gpsimd
            def _(g):
                g.wait_ge(sems["ev"], e0 + 1)
                ins = g.tensor_tensor(T0[:, 2048:4096], T2[:, 2048:4096], T3[:, 2048:4096], AL.mult)
                ins.then_inc(sems["ev"], 1)
            @b.sync
            def _(s):
                s.wait_ge(sems["ev"], e0 + 3)
                for h in range(2):
                    s.dma_start(out=y_dram[(bi * 2 + h) * 128:(bi * 2 + h + 1) * 128, :],
                                in_=T0[:, h * 2048:(h + 1) * 2048]).then_inc(sems["st"], 16)
                    cnt["st"] += 16
                s.wait_ge(sems["st"], cnt["st"])
            cnt["ev"] = e0 + 3
        barrier()

    KPH = int(os.environ.get("KPH", "99"))
    KNB = int(os.environ.get("KNB", str(NB)))
    if KPH >= 1:
        for bi in range(KNB):
            conv_batch(bi)

    for tt in range(4 if KPH >= 2 else 0):
        with nc.Block() as b:
            @b.sync
            def _(s):
                for ck in range(16):
                    s.dma_start(out=ytile[:, ck * 512:(ck + 1) * 512],
                                in_=y_dram[ck * 128:(ck + 1) * 128, tt * 512:(tt + 1) * 512]).then_inc(sems["tp"], 16)
                    cnt["tp"] += 16
                s.wait_ge(sems["tp"], cnt["tp"])
        barrier()
        with nc.Block() as b:
            m0 = cnt["mm"]; e0 = cnt["ev"]; l0 = cnt["ld"]; t0_ = cnt["tp"]
            @b.gpsimd
            def _(g):
                for dk in range(16):
                    if dk >= 2:
                        g.wait_ge(sems["mm"], m0 + dk - 1)
                    buf = dk % 2
                    g.dma_start(out=wo_sl[:, buf * 2048:(buf + 1) * 2048],
                                in_=bass.AP(dr["WoutT"].tensor, dk * 128 * 2048, [[2048, 128], [1, 2048]])).then_inc(sems["ld"], 16)
                    cnt["ld"] += 16
            @b.sync
            def _(s):
                for dk in range(16):
                    if dk >= 2:
                        s.wait_ge(sems["ev"], e0 + 2 * (dk - 1))
                    s.dma_start(out=xr2_t[:, (dk % 2) * 512:(dk % 2) * 512 + 512],
                                in_=dr["xT"][dk * 128:(dk + 1) * 128, tt * 512:(tt + 1) * 512]).then_inc(sems["tp"], 16)
                    cnt["tp"] += 16
            @b.tensor
            def _(t):
                for dk in range(16):
                    buf = dk % 2
                    t.wait_ge(sems["ld"], l0 + 16 * (dk + 1))
                    if dk >= 2:
                        t.wait_ge(sems["ev"], e0 + 2 * (dk - 1))
                    ps = PA if dk % 2 == 0 else PB
                    for ck in range(16):
                        ins = t.matmul(ps[:], wo_sl[:, buf * 2048 + ck * 128: buf * 2048 + (ck + 1) * 128],
                                       ytile[:, ck * 512:(ck + 1) * 512],
                                       start=(ck == 0), stop=(ck == 15))
                    ins.then_inc(sems["mm"], 1)
            @b.scalar
            def _(sc):
                for dk in range(16):
                    sc.wait_ge(sems["mm"], m0 + dk + 1)
                    ps = PA if dk % 2 == 0 else PB
                    ins = sc.activation(bass.AP(r1_t, dk * 512, [[8192, 128], [1, 512]]),
                                        ps[:], ACT.Identity, bias=bout_t[:, dk:dk + 1])
                    ins.then_inc(sems["ev"], 1)
            @b.vector
            def _(v):
                for dk in range(16):
                    v.wait_ge(sems["ev"], e0 + 2 * dk + 1)
                    v.wait_ge(sems["tp"], t0_ + 16 * (dk + 1))
                    r1c = bass.AP(r1_t, dk * 512, [[8192, 128], [1, 512]])
                    v.tensor_tensor(r1c, r1c, xr2_t[:, (dk % 2) * 512:(dk % 2) * 512 + 512], AL.add)
                    ins = v.tensor_copy(bass.AP(r1b_t, dk * 512, [[8192, 128], [1, 512]]), r1c)
                    ins.then_inc(sems["ev"], 1)
            cnt["mm"] = m0 + 16; cnt["ev"] = e0 + 32
        barrier()
        with nc.Block() as b:
            @b.scalar
            def _(sc):
                for dk in range(16):
                    sc.activation(bass.AP(sq_t, dk * 512, [[8192, 128], [1, 512]]),
                                  bass.AP(r1_t, dk * 512, [[8192, 128], [1, 512]]), ACT.Square)
        barrier()
        with nc.Block() as b:
            m0 = cnt["mm"]; e0 = cnt["ev"]
            @b.tensor
            def _(t):
                for ck in range(16):
                    t.matmul(PXR[0:1, :], ones_t[:], bass.AP(r1b_t, ck * 512, [[8192, 128], [1, 512]]),
                             start=(ck == 0), stop=(ck == 15))
                for ck in range(16):
                    ins = t.matmul(PXI[0:1, :], ones_t[:], bass.AP(sq_t, ck * 512, [[8192, 128], [1, 512]]),
                                   start=(ck == 0), stop=(ck == 15))
                ins.then_inc(sems["mm"], 1)
                t.wait_ge(sems["ev"], e0 + 2)
                ins = t.matmul(PR[:], onesr_t[:], srow(0, 512), start=True, stop=True)
                ins.then_inc(sems["mm"], 1)
                t.wait_ge(sems["ev"], e0 + 3)
                ins = t.matmul(PR[:], onesr_t[:], vrow(0, 512), start=True, stop=True)
                ins.then_inc(sems["mm"], 1)
            @b.vector
            def _(v):
                v.wait_ge(sems["mm"], m0 + 1)
                v.tensor_scalar(srow(0, 512), PXR[0:1, :], 1.0 / D, None, AL.mult)
                v.tensor_scalar(srow2(0, 512), PXI[0:1, :], 1.0 / D, None, AL.mult)
                v.tensor_tensor(vrow(0, 512), srow(0, 512), srow(0, 512), AL.mult)
                v.tensor_tensor(vrow(0, 512), srow2(0, 512), vrow(0, 512), AL.subtract)
                v.tensor_scalar(vrow(0, 512), vrow(0, 512), 1e-5, None, AL.add)
                ins = v.reciprocal(mrow(0, 512), vrow(0, 512))
                ins.then_inc(sems["ev"], 1)
                v.wait_ge(sems["mm"], m0 + 2)
                ins = v.tensor_copy(mb_t[:, 0:512], PR[:])
                ins.then_inc(sems["ev"], 1)
                v.wait_ge(sems["mm"], m0 + 3)
                ins = v.tensor_copy(ib_t[:, 0:512], PR[:])
                ins.then_inc(sems["ev"], 1)
            @b.scalar
            def _(sc):
                sc.wait_ge(sems["ev"], e0 + 1)
                ins = sc.activation(vrow(0, 512), mrow(0, 512), ACT.Sqrt)
                ins.then_inc(sems["ev"], 1)
            cnt["mm"] = m0 + 3; cnt["ev"] = e0 + 4
        barrier()
        with nc.Block() as b:
            e0 = cnt["ev"]
            @b.vector
            def _(v):
                for dk in range(16):
                    r1c = bass.AP(r1_t, dk * 512, [[8192, 128], [1, 512]])
                    v.tensor_tensor(r1c, r1c, mb_t[:, 0:512], AL.subtract)
                    ins = v.tensor_tensor(r1c, r1c, ib_t[:, 0:512], AL.mult)
                    ins.then_inc(sems["ev"], 1)
            @b.sync
            def _(s):
                for dk in range(16):
                    s.wait_ge(sems["ev"], e0 + dk + 1)
                    s.dma_start(out=x2_dram[dk * 128:(dk + 1) * 128, tt * 512:(tt + 1) * 512],
                                in_=bass.AP(r1_t, dk * 512, [[8192, 128], [1, 512]])).then_inc(sems["st"], 16)
                    cnt["st"] += 16
                s.wait_ge(sems["st"], cnt["st"])
            cnt["ev"] = e0 + 16
        barrier()

    for t2 in range(2 if KPH >= 3 else 0):
        tb = t2 * 1024
        with nc.Block() as b:
            @b.gpsimd
            def _(g):
                for dk in range(16):
                    g.dma_start(out=x2b_t[:, dk * 1024:(dk + 1) * 1024],
                                in_=x2_dram[dk * 128:(dk + 1) * 128, tb:tb + 1024]).then_inc(sems["ld"], 16)
                    cnt["ld"] += 16
                g.wait_ge(sems["ld"], cnt["ld"])
        barrier()
        with nc.Block() as b:
            m0 = cnt["mm"]
            @b.tensor
            def _(t):
                for tq in range(2):
                    ps = PXR if tq == 0 else PXI
                    for dk in range(16):
                        ins = t.matmul(ps[0:1, :], ones_t[:],
                                       x2b_t[:, dk * 1024 + tq * 512: dk * 1024 + tq * 512 + 512],
                                       start=(dk == 0), stop=(dk == 15))
                ins.then_inc(sems["mm"], 1)
            @b.vector
            def _(v):
                v.wait_ge(sems["mm"], m0 + 1)
                v.tensor_copy(sxb(0, 512), PXR[0:1, :])
                v.tensor_copy(sxb(512, 1024), PXI[0:1, :])
            cnt["mm"] = m0 + 1
        barrier()
        with nc.Block() as b:
            m0 = cnt["mm"]; e0 = cnt["ev"]; l0 = cnt["ld"]
            @b.gpsimd
            def _(g):
                for hk in range(44):
                    if hk >= 2:
                        g.wait_ge(sems["mm"], m0 + 4 * (hk - 1))
                    buf = hk % 2
                    g.dma_start(out=wg_sl[:, buf * 2048:(buf + 1) * 2048],
                                in_=bass.AP(dr["wgT"].tensor, hk * 128 * 2048, [[2048, 128], [1, 2048]])).then_inc(sems["ld"], 16)
                    g.dma_start(out=wu_sl[:, buf * 2048:(buf + 1) * 2048],
                                in_=bass.AP(dr["wuT"].tensor, hk * 128 * 2048, [[2048, 128], [1, 2048]])).then_inc(sems["ld"], 16)
                    cnt["ld"] += 32
            @b.tensor
            def _(t):
                for hk in range(44):
                    buf = hk % 2
                    t.wait_ge(sems["ld"], l0 + 32 * (hk + 1))
                    if hk > 0:
                        t.wait_ge(sems["ev"], e0 + 4 * hk)
                    for tq in range(2):
                        psG = PA if tq == 0 else PXR
                        psU = PB if tq == 0 else PXI
                        for dk in range(16):
                            t.matmul(psG[:], wg_sl[:, buf * 2048 + dk * 128: buf * 2048 + (dk + 1) * 128],
                                     x2b_t[:, dk * 1024 + tq * 512: dk * 1024 + tq * 512 + 512],
                                     start=(dk == 0), stop=False)
                        ins = t.matmul(psG[:], negc_t[:], sxb(tq * 512, (tq + 1) * 512), start=False, stop=True)
                        ins.then_inc(sems["mm"], 1)
                        for dk in range(16):
                            t.matmul(psU[:], wu_sl[:, buf * 2048 + dk * 128: buf * 2048 + (dk + 1) * 128],
                                     x2b_t[:, dk * 1024 + tq * 512: dk * 1024 + tq * 512 + 512],
                                     start=(dk == 0), stop=False)
                        ins = t.matmul(psU[:], negc_t[:], sxb(tq * 512, (tq + 1) * 512), start=False, stop=True)
                        ins.then_inc(sems["mm"], 1)
            @b.scalar
            def _(sc):
                for hk in range(44):
                    for tq in range(2):
                        sc.wait_ge(sems["mm"], m0 + 4 * hk + 2 * tq + 1)
                        psG = PA if tq == 0 else PXR
                        ins = sc.activation(r2b_t[:, tq * 512:(tq + 1) * 512], psG[:],
                                            ACT.Silu, scale=sg_t[:, hk:hk + 1])
                        ins.then_inc(sems["ev"], 1)
            @b.vector
            def _(v):
                for hk in range(44):
                    for tq in range(2):
                        v.wait_ge(sems["mm"], m0 + 4 * hk + 2 * tq + 2)
                        psU = PB if tq == 0 else PXI
                        v.tensor_scalar(sqb_t[:, tq * 512:(tq + 1) * 512], psU[:],
                                        su_t[:, hk:hk + 1], None, AL.mult)
                        v.wait_ge(sems["ev"], e0 + 4 * hk + 2 * tq + 1)
                        ins = v.tensor_tensor(z_t[:, hk * 1024 + tq * 512: hk * 1024 + (tq + 1) * 512],
                                              r2b_t[:, tq * 512:(tq + 1) * 512],
                                              sqb_t[:, tq * 512:(tq + 1) * 512], AL.mult)
                        ins.then_inc(sems["ev"], 1)
            cnt["mm"] = m0 + 176; cnt["ev"] = e0 + 176
        barrier()
        with nc.Block() as b:
            m0 = cnt["mm"]
            @b.tensor
            def _(t):
                for tq in range(2):
                    ps = PXR if tq == 0 else PXI
                    for hk in range(44):
                        ins = t.matmul(ps[0:1, :], ones_t[:],
                                       z_t[:, hk * 1024 + tq * 512: hk * 1024 + tq * 512 + 512],
                                       start=(hk == 0), stop=(hk == 43))
                ins.then_inc(sems["mm"], 1)
            @b.vector
            def _(v):
                v.wait_ge(sems["mm"], m0 + 1)
                v.tensor_copy(szb(0, 512), PXR[0:1, :])
                v.tensor_copy(szb(512, 1024), PXI[0:1, :])
            cnt["mm"] = m0 + 1
        barrier()
        with nc.Block() as b:
            m0 = cnt["mm"]; e0 = cnt["ev"]; l0 = cnt["ld"]; t0_ = cnt["tp"]
            @b.gpsimd
            def _(g):
                for dk in range(16):
                    if dk >= 1:
                        g.wait_ge(sems["mm"], m0 + 2 * dk)
                    g.dma_start(out=wd_sl[:, 0:5632],
                                in_=bass.AP(dr["wdT"].tensor, dk * 128 * HID, [[HID, 128], [1, HID]])).then_inc(sems["ld"], 16)
                    cnt["ld"] += 16
            @b.sync
            def _(s):
                for dk in range(16):
                    if dk >= 1:
                        s.wait_ge(sems["ev"], e0 + 2 * dk)
                    s.dma_start(out=xr2_t[:], in_=x2_dram[dk * 128:(dk + 1) * 128, tb:tb + 1024]).then_inc(sems["tp"], 16)
                    cnt["tp"] += 16
                    for tq in range(2):
                        s.wait_ge(sems["ev"], e0 + 2 * dk + tq + 1)
                        s.dma_start(out=x2_dram[dk * 128:(dk + 1) * 128, tb + tq * 512: tb + (tq + 1) * 512],
                                    in_=r2f_t[:, tq * 512:(tq + 1) * 512]).then_inc(sems["st"], 16)
                        cnt["st"] += 16
                s.wait_ge(sems["st"], cnt["st"])
            @b.tensor
            def _(t):
                for dk in range(16):
                    t.wait_ge(sems["ld"], l0 + 16 * (dk + 1))
                    if dk >= 1:
                        t.wait_ge(sems["ev"], e0 + 2 * dk)
                    for tq in range(2):
                        ps = PA if tq == 0 else PB
                        for hk in range(44):
                            t.matmul(ps[:], wd_sl[:, hk * 128:(hk + 1) * 128],
                                     z_t[:, hk * 1024 + tq * 512: hk * 1024 + tq * 512 + 512],
                                     start=(hk == 0), stop=False)
                        ins = t.matmul(ps[:], negc_t[:], szb(tq * 512, (tq + 1) * 512), start=False, stop=True)
                        ins.then_inc(sems["mm"], 1)
            @b.vector
            def _(v):
                s0_ = cnt["st"] - 32 * 16
                for dk in range(16):
                    for tq in range(2):
                        v.wait_ge(sems["mm"], m0 + 2 * dk + tq + 1)
                        if tq == 0:
                            v.wait_ge(sems["tp"], t0_ + 16 * (dk + 1))
                        if dk >= 1:
                            v.wait_ge(sems["st"], s0_ + 16 * (2 * (dk - 1) + tq + 1))
                        ps = PA if tq == 0 else PB
                        v.tensor_scalar(r2f_t[:, tq * 512:(tq + 1) * 512], ps[:],
                                        sd_t[:, dk:dk + 1], None, AL.mult)
                        ins = v.tensor_tensor(r2f_t[:, tq * 512:(tq + 1) * 512],
                                              r2f_t[:, tq * 512:(tq + 1) * 512],
                                              xr2_t[:, tq * 512:(tq + 1) * 512], AL.add)
                        ins.then_inc(sems["ev"], 1)
            cnt["mm"] = m0 + 32; cnt["ev"] = e0 + 32
        barrier()
        with nc.Block() as b:
            m0 = cnt["mm"]; e0 = cnt["ev"]; l0 = cnt["ld"]
            @b.sync
            def _(s):
                for dk in range(16):
                    if dk > 0:
                        s.wait_ge(sems["mm"], m0 + dk)
                    s.dma_start(out=xr2_t[:], in_=x2_dram[dk * 128:(dk + 1) * 128, tb:tb + 1024]).then_inc(sems["ld"], 16)
                    cnt["ld"] += 16
            @b.vector
            def _(v):
                for dk in range(16):
                    v.wait_ge(sems["ld"], l0 + 16 * (dk + 1))
                    ins = v.tensor_copy(r2b_t[:], xr2_t[:])
                    ins.then_inc(sems["ev"], 1)
            @b.scalar
            def _(sc):
                for dk in range(16):
                    sc.wait_ge(sems["ev"], e0 + 2 * dk + 1)
                    ins = sc.activation(sqb_t[:], r2b_t[:], ACT.Square)
                    ins.then_inc(sems["ev"], 1)
            @b.tensor
            def _(t):
                for dk in range(16):
                    t.wait_ge(sems["ev"], e0 + 2 * (dk + 1))
                    t.matmul(PXR[0:1, :], ones_t[:], r2b_t[:, 0:512], start=(dk == 0), stop=(dk == 15))
                    t.matmul(PXI[0:1, :], ones_t[:], r2b_t[:, 512:1024], start=(dk == 0), stop=(dk == 15))
                    t.matmul(PA[0:1, :], ones_t[:], sqb_t[:, 0:512], start=(dk == 0), stop=(dk == 15))
                    ins = t.matmul(PB[0:1, :], ones_t[:], sqb_t[:, 512:1024], start=(dk == 0), stop=(dk == 15))
                    ins.then_inc(sems["mm"], 1)
            cnt["mm"] = m0 + 16; cnt["ev"] = e0 + 32
        barrier()
        with nc.Block() as b:
            m0 = cnt["mm"]; e0 = cnt["ev"]
            @b.vector
            def _(v):
                v.tensor_scalar(srow(0, 512), PXR[0:1, :], 1.0 / D, None, AL.mult)
                v.tensor_scalar(srow(512, 1024), PXI[0:1, :], 1.0 / D, None, AL.mult)
                v.tensor_scalar(srow2(0, 512), PA[0:1, :], 1.0 / D, None, AL.mult)
                v.tensor_scalar(srow2(512, 1024), PB[0:1, :], 1.0 / D, None, AL.mult)
                v.tensor_tensor(vrow(0, 1024), srow(0, 1024), srow(0, 1024), AL.mult)
                v.tensor_tensor(vrow(0, 1024), srow2(0, 1024), vrow(0, 1024), AL.subtract)
                v.tensor_scalar(vrow(0, 1024), vrow(0, 1024), 1e-5, None, AL.add)
                ins = v.reciprocal(mrow(0, 1024), vrow(0, 1024))
                ins.then_inc(sems["ev"], 1)
                for j in range(4):
                    v.wait_ge(sems["mm"], m0 + j + 1)
                    dst = (mb2_t, mb2_t, ib2_t, ib2_t)[j]
                    ins = v.tensor_copy(dst[:, (j % 2) * 512:(j % 2) * 512 + 512], PR[:])
                    ins.then_inc(sems["ev"], 1)
            @b.scalar
            def _(sc):
                sc.wait_ge(sems["ev"], e0 + 1)
                ins = sc.activation(vrow(0, 1024), mrow(0, 1024), ACT.Sqrt)
                ins.then_inc(sems["ev"], 1)
            @b.tensor
            def _(t):
                srcs = (srow(0, 512), srow(512, 1024), vrow(0, 512), vrow(512, 1024))
                for j in range(4):
                    t.wait_ge(sems["ev"], e0 + 2 + j)
                    ins = t.matmul(PR[:], onesr_t[:], srcs[j], start=True, stop=True)
                    ins.then_inc(sems["mm"], 1)
            cnt["mm"] = m0 + 4; cnt["ev"] = e0 + 6
        barrier()
        with nc.Block() as b:
            e0 = cnt["ev"]; l0 = cnt["ld"]; s0 = cnt["st"]
            @b.sync
            def _(s):
                for dk in range(16):
                    if dk > 0:
                        s.wait_ge(sems["ev"], e0 + dk)
                    s.dma_start(out=xr2_t[:], in_=x2_dram[dk * 128:(dk + 1) * 128, tb:tb + 1024]).then_inc(sems["ld"], 16)
                    cnt["ld"] += 16
            @b.vector
            def _(v):
                for dk in range(16):
                    v.wait_ge(sems["ld"], l0 + 16 * (dk + 1))
                    if dk > 0:
                        v.wait_ge(sems["st"], s0 + 16 * dk)
                    v.tensor_tensor(r2f_t[:], xr2_t[:], mb2_t[:], AL.subtract)
                    ins = v.tensor_tensor(r2f_t[:], r2f_t[:], ib2_t[:], AL.mult)
                    ins.then_inc(sems["ev"], 1)
            @b.scalar
            def _(sc):
                for dk in range(16):
                    sc.wait_ge(sems["ev"], e0 + dk + 1)
                    sc.dma_start(out=out_t[dk * 128:(dk + 1) * 128, tb:tb + 1024],
                                 in_=r2f_t[:]).then_inc(sems["st"], 16)
                    cnt["st"] += 16
                sc.wait_ge(sems["st"], cnt["st"])
            cnt["ev"] = e0 + 16
        barrier()

    barrier()

    _NC_CACHE["nc"] = nc
    return nc


def _prep_shared(inputs):
    f32 = np.float32
    w = np.asarray(inputs["L0_attn_in_proj_w"], f32)
    WmixT = np.ascontiguousarray(
        w.reshape(3, NB, 256, D).transpose(1, 0, 2, 3).reshape(NB * 3 * 256, D).T)
    bmix = np.ascontiguousarray(
        np.asarray(inputs["L0_attn_in_proj_b"], f32).reshape(3, NB, 256)
        .transpose(1, 0, 2).reshape(48, 128).T)
    sw = np.asarray(inputs["L0_attn_short_w"], f32)[:, 0, :]
    swT = np.ascontiguousarray(
        sw.reshape(3, NB, 256, 3).transpose(1, 0, 2, 3).reshape(48, 128, 3)
        .transpose(1, 0, 2).reshape(128, 144))
    sbT = np.ascontiguousarray(
        np.asarray(inputs["L0_attn_short_b"], f32).reshape(3, NB, 256)
        .transpose(1, 0, 2).reshape(48, 128).T)
    fbias = np.ascontiguousarray(np.asarray(inputs["L0_attn_filter_bias"], f32).reshape(16, 128).T)
    wo = np.asarray(inputs["L0_attn_out_proj_w"], f32)
    WoutT = np.ascontiguousarray(
        wo.reshape(16, 128, 16, 128).transpose(0, 3, 2, 1).reshape(16, 128, 2048))
    bout = np.ascontiguousarray(np.asarray(inputs["L0_attn_out_proj_b"], f32).reshape(16, 128).T)
    zT, dect = _consts_filter()
    dft = _consts_dft()

    def u8view(a):
        a = np.ascontiguousarray(np.asarray(a, np.int32))
        return a.view(np.uint8).reshape(a.shape[0], a.shape[1], 4)[..., 0]

    def pack_gu(a):
        t = a.reshape(44, 128, 16, 128)
        return np.ascontiguousarray(t.transpose(0, 3, 2, 1).reshape(44, 128, 2048))
    def pack_d(a):
        t = a.reshape(16, 128, 44, 128)
        return np.ascontiguousarray(t.transpose(0, 3, 2, 1).reshape(16, 128, 5632))
    wgT = pack_gu(u8view(inputs["L0_ffn_wq_gate"]))
    wuT = pack_gu(u8view(inputs["L0_ffn_wq_up"]))
    wdT = pack_d(u8view(inputs["L0_ffn_wq_down"]))
    sg = np.ascontiguousarray(np.asarray(inputs["L0_ffn_s_gate"], f32).reshape(44, 128).T)
    su = np.ascontiguousarray(np.asarray(inputs["L0_ffn_s_up"], f32).reshape(44, 128).T)
    sd = np.ascontiguousarray(np.asarray(inputs["L0_ffn_s_down"], f32).reshape(16, 128).T)
    shared = {
        "WmixT": WmixT, "bmix": bmix, "swT": swT, "sbT": sbT, "fbias": fbias,
        "WoutT": WoutT, "bout": bout,
        "w0T": np.ascontiguousarray(np.asarray(inputs["L0_attn_mlp_w0"], f32).T),
        "w1T": np.ascontiguousarray(np.asarray(inputs["L0_attn_mlp_w1"], f32).T),
        "w2T": np.ascontiguousarray(np.asarray(inputs["L0_attn_mlp_w2"], f32).T),
        "w3T": np.ascontiguousarray(np.asarray(inputs["L0_attn_mlp_w3"], f32).T),
        "b0": np.asarray(inputs["L0_attn_mlp_b0"], f32).reshape(64, 1),
        "b1": np.asarray(inputs["L0_attn_mlp_b1"], f32).reshape(64, 1),
        "b2": np.asarray(inputs["L0_attn_mlp_b2"], f32).reshape(64, 1),
        "freq": np.ascontiguousarray(np.asarray(inputs["L0_attn_sin_freq"], f32).reshape(64, 1)),
        "zT": zT, "dect": dect,
        "wgT": wgT, "wuT": wuT, "wdT": wdT, "sg": sg, "su": su, "sd": sd,
        "negc": np.full((1, 128), -128.0, bf16),
    }
    shared.update(dft)
    return shared


def kernel(**inputs):
    x = np.asarray(inputs["L0_attn_x"], np.float32)
    shared = _prep_shared(inputs)
    nc = build_program()
    in_maps = []
    for b in range(8):
        m = dict(shared)
        m["xT"] = np.ascontiguousarray(x[b].T)
        in_maps.append(m)
    res = run_bass_kernel_spmd(nc, in_maps, core_ids=list(range(8)))
    out = np.empty((8, L, D), np.float32)
    for b in range(8):
        out[b] = res.results[b]["outT"].T
    return out


# version 11
# speedup vs baseline: 1.6253x; 1.1780x over previous
import math
import os
import numpy as np
import ml_dtypes
import concourse.bass as bass
import concourse.mybir as mybir
from concourse.bass_utils import run_bass_kernel_spmd

F32 = mybir.dt.float32
BF16 = mybir.dt.bfloat16
U8 = mybir.dt.uint8
I32 = mybir.dt.int32
AL = mybir.AluOpType
ACT = mybir.ActivationFunctionType
bf16 = ml_dtypes.bfloat16

L = 2048
D = 2048
HID = 5632
NB = 8
TWO_PI = 2.0 * math.pi

MB = 66048
SEG = 16384
OFF_P = [MB + i * SEG for i in range(7)]
OFF_B = MB + 7 * SEG
OFF_T = [OFF_B + i * 8192 for i in range(4)]
OFF_KVT = OFF_B + 4 * 8192


def _consts_dft():
    s_ = np.arange(32); p_ = np.arange(64); r_ = np.arange(128); q_ = np.arange(32)
    th1 = 2 * np.pi * (np.einsum('p,r->pr', p_, r_)[None] / 128
                       + np.einsum('s,r->sr', s_, r_)[:, None, :] / 4096)
    W1c, W1s = np.cos(th1), -np.sin(th1)
    W1R = np.zeros((128, 2048), np.float32); W1I = np.zeros((128, 2048), np.float32)
    for u in range(16):
        for phi in range(2):
            W1R[phi * 64:phi * 64 + 64, u * 128:u * 128 + 128] = W1c[2 * u + phi]
            W1I[phi * 64:phi * 64 + 64, u * 128:u * 128 + 128] = W1s[2 * u + phi]
    def bdp(m):
        out = np.zeros((128, 128), np.float32)
        for c4 in range(4):
            idx = np.arange(32) * 4 + c4
            out[np.ix_(idx, idx)] = m
        return out
    th2 = 2 * np.pi * np.einsum('s,q->sq', s_, q_) / 32
    W2re, W2im = np.cos(th2), -np.sin(th2)
    th3 = 2 * np.pi * np.einsum('q,s->qs', q_, s_) / 32
    W3re, W3im = np.cos(th3) / 4096.0, np.sin(th3) / 4096.0
    th4 = 2 * np.pi * (np.einsum('p,r->pr', p_, r_)[None] / 128
                       + np.einsum('s,r->sr', s_, r_)[:, None, :] / 4096)
    W4re, W4im = np.cos(th4), np.sin(th4)
    W4R = np.zeros((128, 2048), np.float32); W4IN = np.zeros((128, 2048), np.float32)
    for s in range(32):
        W4R[:, s * 64:s * 64 + 64] = W4re[s].T
        W4IN[:, s * 64:s * 64 + 64] = -W4im[s].T
    c = {"W1R": W1R, "W1I": W1I, "W4R": W4R, "W4IN": W4IN,
         "W2RE": bdp(W2re), "W2IM": bdp(W2im), "W2IMN": bdp(-W2im),
         "W3RE": bdp(W3re), "W3IM": bdp(W3im), "W3IMN": bdp(-W3im)}
    return {k: v.astype(bf16) for k, v in c.items()}


def _consts_filter():
    t = np.linspace(0.0, 1.0, L)[:, None]
    t_resc = np.linspace(0.0, L - 1.0, L)
    w = 2.0 * np.pi * t_resc / L
    f = np.linspace(0.0001, 0.0, 1)
    ang = f[None, :] * w[:, None]
    z = np.concatenate([t, np.cos(ang), -np.sin(ang)], axis=-1)
    zT = np.ascontiguousarray(z.T).astype(bf16)
    deltas = np.abs(np.linspace(np.log(0.01) / 1.5, np.log(0.01) / 0.3, D))
    decay = np.exp(-t * deltas[None, :]).astype(np.float32)
    dect = np.zeros((128, NB * 4096), np.float32)
    pp = np.arange(64)
    for b in range(NB):
        for u in range(16):
            for phi in range(2):
                tv = 2 * u + phi + 32 * pp
                dect[phi * 64:phi * 64 + 64, b * 4096 + u * 256:b * 4096 + (u + 1) * 256] = \
                    decay[tv][:, b * 256:(b + 1) * 256]
    return zT, dect.astype(bf16)


_NC_CACHE = {}


def build_program():
    if "nc" in _NC_CACHE:
        return _NC_CACHE["nc"]
    nc = bass.Bass("TRN2", target_bir_lowering=False)

    dr = {}
    def din(name, shape, dtype=F32):
        dr[name] = nc.dram_tensor(name, list(shape), dtype, kind="ExternalInput").ap()

    din("xT", [D, L])
    din("WmixT", [D, 3 * D])
    din("bmix", [128, 48]); din("swT", [128, 144]); din("sbT", [128, 48])
    din("fbias", [128, 16])
    din("WoutT", [16, 128, 2048]); din("bout", [128, 16])
    din("w0T", [3, 64]); din("w1T", [64, 64]); din("w2T", [64, 64]); din("w3T", [64, D])
    din("b0", [64, 1]); din("b1", [64, 1]); din("b2", [64, 1]); din("freq", [64, 1])
    din("zT", [3, L], BF16)
    din("dect", [128, NB * 4096], BF16)
    for n in ("W1R", "W1I", "W4R", "W4IN"):
        din(n, [128, 2048], BF16)
    for n in ("W2RE", "W2IM", "W2IMN", "W3RE", "W3IM", "W3IMN"):
        din(n, [128, 128], BF16)
    din("wgT", [44, 128, 2048], U8); din("wuT", [44, 128, 2048], U8)
    din("wdT", [16, 128, HID], U8)
    din("sg", [128, 44]); din("su", [128, 44]); din("sd", [128, 16])
    din("negc", [1, 128], BF16)

    out_t = nc.dram_tensor("outT", [D, L], F32, kind="ExternalOutput").ap()
    y_dram = nc.dram_tensor("Yscr", [D, L], BF16, kind="Internal").ap()
    x2_dram = nc.dram_tensor("X2scr", [D, L], F32, kind="Internal").ap()

    def sbn(name, shape, dtype=BF16):
        return nc.alloc_sbuf_tensor(name, list(shape), dtype)

    C = {}
    for n in ("W1R", "W1I", "W4R", "W4IN"):
        C[n] = sbn("C" + n, [128, 2048])
    for n in ("W2RE", "W2IM", "W2IMN", "W3RE", "W3IM", "W3IMN"):
        C[n] = sbn("C" + n, [128, 128])
    bmix_t = sbn("bmix_t", [128, 48], F32)
    swT_t = sbn("swT_t", [128, 144], F32)
    sbT_t = sbn("sbT_t", [128, 48], F32)
    fbias_t = sbn("fbias_t", [128, 16], F32)
    bout_t = sbn("bout_t", [128, 16], F32)
    sg_t = sbn("sg_t", [128, 44], F32); su_t = sbn("su_t", [128, 44], F32)
    sd_t = sbn("sd_t", [128, 16], F32)
    w0T_t = sbn("w0T_t", [3, 64]); w1T_t = sbn("w1T_t", [64, 64]); w2T_t = sbn("w2T_t", [64, 64])
    w3T_t = sbn("w3T_t", [64, D])
    b0_t = sbn("b0_t", [64, 1], F32); b1_t = sbn("b1_t", [64, 1], F32)
    b2_t = sbn("b2_t", [64, 1], F32); fq_t = sbn("fq_t", [64, 1], F32)
    hhp_t = sbn("hhp_t", [64, L])
    ones_t = sbn("ones_t", [128, 1])
    onesr_t = sbn("onesr_t", [1, 128], F32)
    negc_t = sbn("negc_t", [1, 128])
    rows_t = sbn("rows_t", [1, 4096], F32)
    sxzb_t = sbn("sxzb_t", [1, 2048])

    def srow(a, b_):
        return rows_t[0:1, a:b_]
    def srow2(a, b_):
        return rows_t[0:1, 1024 + a:1024 + b_]
    def mrow(a, b_):
        return rows_t[0:1, 2048 + a:2048 + b_]
    def vrow(a, b_):
        return rows_t[0:1, 3072 + a:3072 + b_]
    def sxb(a, b_):
        return sxzb_t[0:1, a:b_]
    def szb(a, b_):
        return sxzb_t[0:1, 1024 + a:1024 + b_]
    assert nc.sbuf_bytes_remaining > (229344 - MB), nc.sbuf_bytes_remaining

    def sbm(name, offset, shape, dtype=BF16):
        return nc.alloc_sbuf_tensor_at(name, list(shape), dtype, offset=offset, align_bytes=64)

    P = [sbm(f"P{i+1}", OFF_P[i], [128, 8192]) for i in range(7)]
    T0 = sbm("T0", OFF_T[0], [128, 4096]); T1 = sbm("T1", OFF_T[1], [128, 4096])
    T2 = sbm("T2", OFF_T[2], [128, 4096]); T3 = sbm("T3", OFF_T[3], [128, 4096])
    KVT = sbm("KVT", OFF_KVT, [128, 4096])
    arg_t = sbm("arg_t", OFF_P[0], [64, 2048], F32)
    un_t = sbm("un_t", OFF_P[1], [64, 2048], F32)
    ni_t = sbm("ni_t", OFF_P[2], [64, 2048], I32)
    nf_t = sbm("nf_t", OFF_P[3], [64, 2048], F32)
    hh_t = sbm("hh_t", OFF_P[4], [64, 2048])
    zT_t = sbm("zT_t", OFF_P[5], [3, L])
    CtkR, CtkI = P[0], P[1]
    TCR, TCI = P[2], P[3]
    KRE, KIM = P[0], P[1]
    CtvR, CtvI = P[4], P[5]
    XRE, XIM = P[4], P[5]
    S1 = P[6]
    TER, TEI = P[0], P[1]
    ype = sbm("ype", OFF_P[6], [128, 4096])
    yperm = sbm("yperm", OFF_P[6] + 8192, [128, 4096])
    dect_b = sbm("dect_b", OFF_P[6], [128, 4096])
    xtile = P[6]
    wmixb = sbm("wmixb", OFF_P[4], [128, 12288])
    wo_sl = sbm("wo_sl", OFF_P[0], [128, 4096])
    ytile = sbm("ytile", OFF_P[4], [128, 8192])
    r1_t = sbm("r1_t", OFF_P[5], [128, 8192], F32)
    r1b_t = sbm("r1b_t", OFF_P[1], [128, 8192])
    sq_t = sbm("sq_t", OFF_P[2], [128, 8192])
    z_t = sbm("z_t", MB, [128, 45056])
    x2b_t = sbm("x2b_t", MB + 90112, [128, 16384])
    wd_sl = sbm("wd_sl", MB + 122880, [128, 5632])
    wg_sl = sbm("wg_sl", MB + 134144, [128, 4096])
    wu_sl = sbm("wu_sl", MB + 142336, [128, 4096])
    xr2_t = sbm("xr2_t", MB + 150528, [128, 1024], F32)
    r2f_t = sbm("r2f_t", MB + 154624, [128, 1024], F32)
    mb2_t = sbm("mb2_t", MB + 134144, [128, 1024], F32)
    ib2_t = sbm("ib2_t", MB + 138240, [128, 1024], F32)
    r2b_t = sbm("r2b_t", MB + 158720, [128, 1024])
    sqb_t = sbm("sqb_t", MB + 160768, [128, 1024])

    mb_t = mb2_t
    ib_t = ib2_t
    xr_t = xr2_t

    PS = []
    def psum(name, shape):
        g = nc.psum_tensor(name, list(shape), F32)
        t = g.__enter__(); PS.append(g)
        return t
    PA = psum("PA", [128, 512])
    PB = psum("PB", [128, 512])
    PCe = psum("PCe", [128, 256])
    PCo = psum("PCo", [128, 256])
    PXR = psum("PXR", [128, 512])
    PXI = psum("PXI", [128, 512])
    P2m = psum("P2m", [128, 256])
    PR = psum("PR", [128, 512])

    sems = {}
    for nm in ("ld", "mm", "ev", "tp", "st", "bar"):
        g = nc.semaphore("s_" + nm)
        sems[nm] = g.__enter__()
    cnt = {"ld": 0, "mm": 0, "ev": 0, "tp": 0, "st": 0, "bar": 0}
    ENG = ("sync", "scalar", "vector", "tensor", "gpsimd")

    def barrier():
        cnt["bar"] += 1
        tgt = 5 * cnt["bar"]
        with nc.Block() as b:
            for e in ENG:
                def mk(e=e):
                    def f(eng):
                        eng.sem_inc(sems["bar"], 1)
                        eng.wait_ge(sems["bar"], tgt)
                    return f
                getattr(b, e)(mk())

    with nc.Block() as b:
        @b.gpsimd
        def _(g):
            for nm, t in (("bmix", bmix_t), ("swT", swT_t), ("sbT", sbT_t), ("fbias", fbias_t),
                          ("bout", bout_t), ("sg", sg_t), ("su", su_t), ("sd", sd_t),
                          ("b0", b0_t), ("b1", b1_t), ("b2", b2_t), ("freq", fq_t),
                          ("w0T", w0T_t), ("w1T", w1T_t), ("w2T", w2T_t), ("w3T", w3T_t)):
                g.dma_start(out=t[:], in_=dr[nm][:]).then_inc(sems["ld"], 16)
                cnt["ld"] += 16
            g.memset(ones_t[:], 1.0)
            g.memset(onesr_t[:], 1.0)
        @b.sync
        def _(s):
            for nm, t in (("zT", zT_t), ("negc", negc_t),
                          ("W1R", C["W1R"]), ("W1I", C["W1I"]), ("W4R", C["W4R"]), ("W4IN", C["W4IN"]),
                          ("W2RE", C["W2RE"]), ("W2IM", C["W2IM"]), ("W2IMN", C["W2IMN"]),
                          ("W3RE", C["W3RE"]), ("W3IM", C["W3IM"]), ("W3IMN", C["W3IMN"])):
                s.dma_start(out=t[:], in_=dr[nm][:]).then_inc(sems["ld"], 16)
                cnt["ld"] += 16
            s.wait_ge(sems["ld"], cnt["ld"])
    barrier()

    def mlp_layer(lhsT, rhs_q, bias):
        with nc.Block() as b:
            m0 = cnt["mm"]; e0 = cnt["ev"]
            @b.tensor
            def _(t):
                for j in range(4):
                    if j > 0:
                        t.wait_ge(sems["ev"], e0 + j)
                    ins = t.matmul(PA[:64, :], lhsT, rhs_q[j], start=True, stop=True)
                    ins.then_inc(sems["mm"], 1)
            @b.vector
            def _(v):
                for j in range(4):
                    v.wait_ge(sems["mm"], m0 + j + 1)
                    cols = slice(j * 512, (j + 1) * 512)
                    v.tensor_scalar(arg_t[:, cols], PA[:64, :], bias[:], fq_t[:], AL.add, AL.mult)
                    v.tensor_scalar(un_t[:, cols], arg_t[:, cols], 1.0 / TWO_PI, None, AL.mult)
                    v.tensor_copy(ni_t[:, cols], un_t[:, cols])
                    v.tensor_copy(nf_t[:, cols], ni_t[:, cols])
                    ins = v.scalar_tensor_tensor(arg_t[:, cols], nf_t[:, cols], -TWO_PI,
                                                 arg_t[:, cols], AL.mult, AL.add)
                    ins.then_inc(sems["ev"], 1)
            @b.scalar
            def _(sc):
                for j in range(4):
                    sc.wait_ge(sems["ev"], e0 + j + 1)
                    cols = slice(j * 512, (j + 1) * 512)
                    sc.activation(hh_t[:, cols], arg_t[:, cols], ACT.Sin)
            cnt["mm"] = m0 + 4; cnt["ev"] = e0 + 4
        barrier()

    mlp_layer(w0T_t[:], [zT_t[:, j * 512:(j + 1) * 512] for j in range(4)], b0_t)
    mlp_layer(w1T_t[:], [hh_t[:, j * 512:(j + 1) * 512] for j in range(4)], b1_t)
    mlp_layer(w2T_t[:], [hh_t[:, j * 512:(j + 1) * 512] for j in range(4)], b2_t)
    with nc.Block() as b:
        @b.vector
        def _(v):
            ap_in = bass.AP(hh_t, 0, [[L, 64], [1, 32], [32, 64]])
            v.tensor_copy(hhp_t[:].rearrange("a (s p) -> a s p", s=32), ap_in)
    barrier()

    def F1(srcT, dR, dI):
        with nc.Block() as b:
            m0 = cnt["mm"]; e0 = cnt["ev"]
            @b.tensor
            def _(t):
                for u in range(16):
                    if u >= 1:
                        t.wait_ge(sems["ev"], e0 + 4 * u)
                    rhsE = bass.AP(srcT, u * 128, [[4096, 64], [2048, 2], [1, 128]])
                    rhsO = bass.AP(srcT, 64 * 4096 + u * 128, [[4096, 64], [2048, 2], [1, 128]])
                    wre = C["W1R"][:, u * 128:(u + 1) * 128]
                    wim = C["W1I"][:, u * 128:(u + 1) * 128]
                    t.matmul(PCe[:], wre[0:64, :], rhsE, start=True, stop=True, tile_position=(0, 0))
                    t.matmul(PCo[:], wim[0:64, :], rhsE, start=True, stop=True, tile_position=(0, 0))
                    t.matmul(P2m[:], wre[64:128, :], rhsO, start=True, stop=True, tile_position=(64, 0))
                    ins = t.matmul(PR[:, 0:256], wim[64:128, :], rhsO, start=True, stop=True, tile_position=(64, 0))
                    ins.then_inc(sems["mm"], 1)
            def evac_body(eng, cpy, jobs):
                for u in range(16):
                    eng.wait_ge(sems["mm"], m0 + u + 1)
                    for so, psrc, dst in jobs:
                        ap_out = bass.AP(dst, 8 * u + 4 * so, [[8192, 128], [4096, 2], [128, 32], [1, 4]])
                        ins = cpy(ap_out, psrc)
                        ins.then_inc(sems["ev"], 1)
            @b.vector
            def _(v):
                evac_body(v, v.tensor_copy, ((0, PCe[:], dR), (0, PCo[:], dI)))
            @b.scalar
            def _(sc):
                evac_body(sc, sc.copy, ((1, P2m[:], dR), (1, PR[:, 0:256], dI)))
            cnt["mm"] = m0 + 16; cnt["ev"] = e0 + 64
        barrier()

    def turn(srcR, srcI, dstR, dstI, wait_ev=None, bar=True):
        with nc.Block() as b:
            t0_ = cnt["tp"]
            @b.sync
            def _(s):
                if wait_ev is not None:
                    s.wait_ge(sems["ev"], wait_ev)
                s.dma_start_transpose(
                    dstR[:].rearrange("a (g r) -> a g r", g=64),
                    srcR[:].rearrange("a (g f) -> a g f", g=64)).then_inc(sems["tp"], 16)
                s.dma_start_transpose(
                    dstI[:].rearrange("a (g r) -> a g r", g=64),
                    srcI[:].rearrange("a (g f) -> a g f", g=64)).then_inc(sems["tp"], 16)
                if bar:
                    s.wait_ge(sems["tp"], t0_ + 32)
            cnt["tp"] = t0_ + 32
        if bar:
            barrier()

    def F2(WRE, WIM, WIMN, srcR, srcI, dstRE, dstIM, evac="split", wait_tp=None, bar=True):
        with nc.Block() as b:
            m0 = cnt["mm"]; e0 = cnt["ev"]
            @b.tensor
            def _(t):
                if wait_tp is not None:
                    t.wait_ge(sems["tp"], wait_tp)
                for blk in range(16):
                    if blk >= 2:
                        t.wait_ge(sems["ev"], e0 + 2 * (blk - 1))
                    cols = slice(blk * 512, blk * 512 + 512)
                    pR, pI = (PXR, PXI) if blk % 2 == 0 else (PA, PB)
                    t.matmul(pR[:], C[WRE][:], srcR[:, cols], start=True, stop=False)
                    t.matmul(pR[:], C[WIMN][:], srcI[:, cols], start=False, stop=True)
                    t.matmul(pI[:], C[WIM][:], srcR[:, cols], start=True, stop=False)
                    ins = t.matmul(pI[:], C[WRE][:], srcI[:, cols], start=False, stop=True)
                    ins.then_inc(sems["mm"], 1)
            if evac == "split":
                @b.vector
                def _(v):
                    for blk in range(16):
                        v.wait_ge(sems["mm"], m0 + blk + 1)
                        cols = slice(blk * 512, blk * 512 + 512)
                        pR = PXR if blk % 2 == 0 else PA
                        ins = v.tensor_copy(dstRE[:, cols], pR[:])
                        ins.then_inc(sems["ev"], 1)
                @b.scalar
                def _(sc):
                    for blk in range(16):
                        sc.wait_ge(sems["mm"], m0 + blk + 1)
                        cols = slice(blk * 512, blk * 512 + 512)
                        pI = PXI if blk % 2 == 0 else PB
                        ins = sc.copy(dstIM[:, cols], pI[:])
                        ins.then_inc(sems["ev"], 1)
            else:
                @b.scalar
                def _(sc):
                    for blk in range(16):
                        sc.wait_ge(sems["mm"], m0 + blk + 1)
                        cols = slice(blk * 512, blk * 512 + 512)
                        pR = PXR if blk % 2 == 0 else PA
                        pI = PXI if blk % 2 == 0 else PB
                        sc.copy(dstRE[:, cols], pR[:])
                        ins = sc.copy(dstIM[:, cols], pI[:])
                        ins.then_inc(sems["ev"], 2)
            cnt["mm"] = m0 + 16; cnt["ev"] = e0 + 32
        if bar:
            barrier()

    def conv_batch(bi):
        with nc.Block() as b:
            m0 = cnt["mm"]; e0 = cnt["ev"]; t0_ = cnt["tp"]; l0 = cnt["ld"]
            @b.sync
            def _(s):
                s.dma_start(out=dect_b[:], in_=dr["dect"][:, bi * 4096:(bi + 1) * 4096]).then_inc(sems["tp"], 16)
                cnt["tp"] += 16
            @b.tensor
            def _(t):
                for u in range(16):
                    if u > 0:
                        t.wait_ge(sems["ev"], e0 + u)
                    ins = t.matmul(PA[:, 0:256], hhp_t[:, u * 128:(u + 1) * 128],
                                   w3T_t[:, bi * 256:(bi + 1) * 256], start=True, stop=True)
                    ins.then_inc(sems["mm"], 1)
            @b.vector
            def _(v):
                v.wait_ge(sems["tp"], t0_ + 16)
                for u in range(16):
                    v.wait_ge(sems["mm"], m0 + u + 1)
                    for h in range(2):
                        ins = v.tensor_tensor(
                            KVT[:, h * 2048 + u * 128: h * 2048 + u * 128 + 128],
                            PA[:, h * 128:(h + 1) * 128],
                            dect_b[:, u * 256 + h * 128: u * 256 + (h + 1) * 128], AL.mult)
                    ins.then_inc(sems["ev"], 1)
            @b.gpsimd
            def _(g):
                for dk in range(16):
                    g.dma_start(out=wmixb[:, dk * 768:(dk + 1) * 768],
                                in_=dr["WmixT"][dk * 128:(dk + 1) * 128, bi * 768:(bi + 1) * 768]).then_inc(sems["ld"], 16)
                    cnt["ld"] += 16
            cnt["mm"] = m0 + 16; cnt["ev"] = e0 + 16
        barrier()
        F1(KVT, CtkR, CtkI)
        TPK = cnt["tp"] + 32
        turn(CtkR, CtkI, TCR, TCI, bar=False)

        SW1 = (PA, PB, PXR, PXI)
        l_x0 = cnt["ld"]
        with nc.Block() as b:
            m0 = cnt["mm"]; e0 = cnt["ev"]; l0 = cnt["ld"]
            @b.gpsimd
            def _(g):
                for tt in range(4):
                    if tt == 1:
                        g.wait_ge(sems["tp"], TPK)
                    if tt >= 2:
                        g.wait_ge(sems["mm"], m0 + 2 * (tt - 1))
                    dst = xtile if tt % 2 == 0 else CtkR
                    for dk in range(16):
                        g.dma_start(out=bass.AP(dst, dk * 512, [[8192, 128], [1, 512]]),
                                    in_=dr["xT"][dk * 128:(dk + 1) * 128, tt * 512:(tt + 1) * 512]).then_inc(sems["ld"], 16)
                        cnt["ld"] += 16
            @b.tensor
            def _(t):
                for tt in range(4):
                    srcx = xtile if tt % 2 == 0 else CtkR
                    for dk in range(16):
                        t.wait_ge(sems["ld"], l0 + 256 * tt + 16 * (dk + 1))
                        if dk == 0 and tt >= 1:
                            t.wait_ge(sems["ev"], e0 + 6 * tt)
                        for rc in range(4):
                            ins = t.matmul(SW1[rc][:], wmixb[:, dk * 768 + rc * 128: dk * 768 + (rc + 1) * 128],
                                           bass.AP(srcx, dk * 512, [[8192, 128], [1, 512]]),
                                           start=(dk == 0), stop=(dk == 15))
                    ins.then_inc(sems["mm"], 1)
                    t.wait_ge(sems["ev"], e0 + 6 * tt + 4)
                    for dk in range(16):
                        for rc in range(4, 6):
                            ins = t.matmul((PA, PB)[rc - 4][:], wmixb[:, dk * 768 + rc * 128: dk * 768 + (rc + 1) * 128],
                                           bass.AP(srcx, dk * 512, [[8192, 128], [1, 512]]),
                                           start=(dk == 0), stop=(dk == 15))
                    ins.then_inc(sems["mm"], 1)
            @b.scalar
            def _(sc):
                for tt in range(4):
                    for rc in range(6):
                        sc.wait_ge(sems["mm"], m0 + 2 * tt + (1 if rc < 4 else 2))
                        ps = SW1[rc] if rc < 4 else (PA, PB)[rc - 4]
                        role, half = rc // 2, rc % 2
                        dstT = (T0, T1, T2)[role]
                        ins = sc.activation(dstT[:, half * 2048 + tt * 512: half * 2048 + (tt + 1) * 512],
                                            ps[:], ACT.Identity, bias=bmix_t[:, bi * 6 + rc: bi * 6 + rc + 1])
                        ins.then_inc(sems["ev"], 1)
            cnt["mm"] = m0 + 8; cnt["ev"] = e0 + 24
        barrier()

        F2("W2RE", "W2IM", "W2IMN", TCR, TCI, KRE, KIM, evac="scalar", bar=False, wait_tp=TPK)
        with nc.Block() as b:
            e0 = cnt["ev"]
            @b.vector
            def _(v):
                def taps(h):
                    for (srcT, dstT, role) in ((T0, T3, 0), (T1, T0, 1), (T2, T1, 2)):
                        j = bi * 6 + role * 2 + h
                        cs = h * 2048
                        w0s = swT_t[:, j * 3 + 0: j * 3 + 1]
                        w1s = swT_t[:, j * 3 + 1: j * 3 + 2]
                        w2s = swT_t[:, j * 3 + 2: j * 3 + 3]
                        v.tensor_scalar(dstT[:, cs:cs + 2048], srcT[:, cs:cs + 2048],
                                        w2s, sbT_t[:, j:j + 1], AL.mult, AL.add)
                        v.scalar_tensor_tensor(dstT[:, cs + 1:cs + 2048], srcT[:, cs:cs + 2047],
                                               w1s, dstT[:, cs + 1:cs + 2048], AL.mult, AL.add)
                        v.scalar_tensor_tensor(dstT[:, cs + 2:cs + 2048], srcT[:, cs:cs + 2046],
                                               w0s, dstT[:, cs + 2:cs + 2048], AL.mult, AL.add)
                    v.tensor_tensor(T1[:, h * 2048:(h + 1) * 2048], T1[:, h * 2048:(h + 1) * 2048],
                                    T0[:, h * 2048:(h + 1) * 2048], AL.mult)
                    ap_in = bass.AP(T1, h * 2048, [[4096, 128], [1, 32], [32, 64]])
                    out_ap = bass.AP(T2, h * 2048, [[4096, 128], [64, 32], [1, 64]])
                    ins = v.tensor_copy(out_ap, ap_in)
                    ins.then_inc(sems["ev"], 1)
                taps(0)
                taps(1)
            cnt["ev"] = e0 + 2
        barrier()
        with nc.Block() as b:
            @b.sync
            def _(s):
                s.dma_start_transpose(
                    KVT[:].rearrange("a (t f) -> a t f", t=32),
                    T2[:].rearrange("a (t f) -> a t f", t=32)).then_inc(sems["tp"], 16)
                cnt["tp"] += 16
                s.wait_ge(sems["tp"], cnt["tp"])
        barrier()
        F1(KVT, CtvR, CtvI)
        turn(CtvR, CtvI, TCR, TCI)
        F2("W2RE", "W2IM", "W2IMN", TCR, TCI, XRE, XIM)
        with nc.Block() as b:
            def prod(eng, c0, c1):
                sl = slice(c0, c1)
                eng.tensor_tensor(S1[:, sl], XRE[:, sl], KIM[:, sl], AL.mult)
                eng.tensor_tensor(XRE[:, sl], XRE[:, sl], KRE[:, sl], AL.mult)
                eng.tensor_tensor(KRE[:, sl], XIM[:, sl], KRE[:, sl], AL.mult)
                eng.tensor_tensor(XIM[:, sl], XIM[:, sl], KIM[:, sl], AL.mult)
                eng.tensor_tensor(XRE[:, sl], XRE[:, sl], XIM[:, sl], AL.subtract)
                eng.tensor_tensor(S1[:, sl], S1[:, sl], KRE[:, sl], AL.add)
            @b.vector
            def _(v):
                prod(v, 0, 6144)
            @b.gpsimd
            def _(g):
                prod(g, 6144, 8192)
        barrier()
        F2("W3RE", "W3IM", "W3IMN", XRE, S1, TCR, TCI)
        turn(TCR, TCI, TER, TEI)
        with nc.Block() as b:
            m0 = cnt["mm"]; e0 = cnt["ev"]
            @b.tensor
            def _(t):
                for u in range(16):
                    if u > 0:
                        t.wait_ge(sems["ev"], e0 + u)
                    for phi in range(2):
                        s = 2 * u + phi
                        rhsR = bass.AP(TER, 4 * s, [[8192, 128], [128, 64], [1, 4]])
                        rhsI = bass.AP(TEI, 4 * s, [[8192, 128], [128, 64], [1, 4]])
                        tpos = (0, 64 * phi)
                        outp = P2m[64 * phi:64 * phi + 64, :]
                        t.matmul(outp, C["W4R"][:, s * 64:s * 64 + 64], rhsR, start=True, stop=False, tile_position=tpos)
                        ins = t.matmul(outp, C["W4IN"][:, s * 64:s * 64 + 64], rhsI, start=False, stop=True, tile_position=tpos)
                    ins.then_inc(sems["mm"], 1)
            @b.vector
            def _(v):
                for u in range(16):
                    v.wait_ge(sems["mm"], m0 + u + 1)
                    v.tensor_copy(bass.AP(ype, u * 128, [[4096, 128], [1, 128]]), P2m[:, 0:128])
                    ins = v.tensor_copy(bass.AP(ype, 2048 + u * 128, [[4096, 128], [1, 128]]), P2m[:, 128:256])
                    ins.then_inc(sems["ev"], 1)
            cnt["mm"] = m0 + 16; cnt["ev"] = e0 + 16
        barrier()
        with nc.Block() as b:
            @b.sync
            def _(s):
                s.dma_start_transpose(
                    yperm[:].rearrange("a (t f) -> a t f", t=32),
                    ype[:].rearrange("a (t f) -> a t f", t=32)).then_inc(sems["tp"], 16)
                cnt["tp"] += 16
                s.wait_ge(sems["tp"], cnt["tp"])
        barrier()
        with nc.Block() as b:
            e0 = cnt["ev"]
            @b.vector
            def _(v):
                for h in range(2):
                    ap_y = bass.AP(yperm, h * 2048, [[4096, 128], [1, 64], [128, 16], [64, 2]])
                    ap_g1o = bass.AP(T2, h * 2048, [[4096, 128], [32, 64], [2, 16], [1, 2]])
                    fb = fbias_t[:, bi * 2 + h: bi * 2 + h + 1]
                    vt_ap = bass.AP(T1, h * 2048, [[4096, 128], [32, 64], [2, 16], [1, 2]])
                    ins = v.scalar_tensor_tensor(ap_g1o, vt_ap, fb, ap_y, AL.mult, AL.add)
                    if h == 1:
                        ins.then_inc(sems["ev"], 1)
                ins = v.tensor_tensor(T0[:, 0:2048], T2[:, 0:2048], T3[:, 0:2048], AL.mult)
                ins.then_inc(sems["ev"], 1)
            @b.gpsimd
            def _(g):
                g.wait_ge(sems["ev"], e0 + 1)
                ins = g.tensor_tensor(T0[:, 2048:4096], T2[:, 2048:4096], T3[:, 2048:4096], AL.mult)
                ins.then_inc(sems["ev"], 1)
            @b.sync
            def _(s):
                s.wait_ge(sems["ev"], e0 + 3)
                for h in range(2):
                    s.dma_start(out=y_dram[(bi * 2 + h) * 128:(bi * 2 + h + 1) * 128, :],
                                in_=T0[:, h * 2048:(h + 1) * 2048]).then_inc(sems["st"], 16)
                    cnt["st"] += 16
                s.wait_ge(sems["st"], cnt["st"])
            cnt["ev"] = e0 + 3
        barrier()

    KPH = int(os.environ.get("KPH", "99"))
    KNB = int(os.environ.get("KNB", str(NB)))
    if KPH >= 1:
        for bi in range(KNB):
            conv_batch(bi)

    for tt in range(4 if KPH >= 2 else 0):
        with nc.Block() as b:
            @b.sync
            def _(s):
                for ck in range(16):
                    s.dma_start(out=ytile[:, ck * 512:(ck + 1) * 512],
                                in_=y_dram[ck * 128:(ck + 1) * 128, tt * 512:(tt + 1) * 512]).then_inc(sems["tp"], 16)
                    cnt["tp"] += 16
                s.wait_ge(sems["tp"], cnt["tp"])
        barrier()
        with nc.Block() as b:
            m0 = cnt["mm"]; e0 = cnt["ev"]; l0 = cnt["ld"]; t0_ = cnt["tp"]
            @b.gpsimd
            def _(g):
                for dk in range(16):
                    if dk >= 2:
                        g.wait_ge(sems["mm"], m0 + dk - 1)
                    buf = dk % 2
                    g.dma_start(out=wo_sl[:, buf * 2048:(buf + 1) * 2048],
                                in_=bass.AP(dr["WoutT"].tensor, dk * 128 * 2048, [[2048, 128], [1, 2048]])).then_inc(sems["ld"], 16)
                    cnt["ld"] += 16
            @b.sync
            def _(s):
                for dk in range(16):
                    if dk >= 2:
                        s.wait_ge(sems["ev"], e0 + 2 * (dk - 1))
                    s.dma_start(out=xr2_t[:, (dk % 2) * 512:(dk % 2) * 512 + 512],
                                in_=dr["xT"][dk * 128:(dk + 1) * 128, tt * 512:(tt + 1) * 512]).then_inc(sems["tp"], 16)
                    cnt["tp"] += 16
            @b.tensor
            def _(t):
                for dk in range(16):
                    buf = dk % 2
                    t.wait_ge(sems["ld"], l0 + 16 * (dk + 1))
                    if dk >= 2:
                        t.wait_ge(sems["ev"], e0 + 2 * (dk - 1))
                    ps = PA if dk % 2 == 0 else PB
                    for ck in range(16):
                        ins = t.matmul(ps[:], wo_sl[:, buf * 2048 + ck * 128: buf * 2048 + (ck + 1) * 128],
                                       ytile[:, ck * 512:(ck + 1) * 512],
                                       start=(ck == 0), stop=(ck == 15))
                    ins.then_inc(sems["mm"], 1)
            @b.scalar
            def _(sc):
                for dk in range(16):
                    sc.wait_ge(sems["mm"], m0 + dk + 1)
                    ps = PA if dk % 2 == 0 else PB
                    ins = sc.activation(bass.AP(r1_t, dk * 512, [[8192, 128], [1, 512]]),
                                        ps[:], ACT.Identity, bias=bout_t[:, dk:dk + 1])
                    ins.then_inc(sems["ev"], 1)
            @b.vector
            def _(v):
                for dk in range(16):
                    v.wait_ge(sems["ev"], e0 + 2 * dk + 1)
                    v.wait_ge(sems["tp"], t0_ + 16 * (dk + 1))
                    r1c = bass.AP(r1_t, dk * 512, [[8192, 128], [1, 512]])
                    v.tensor_tensor(r1c, r1c, xr2_t[:, (dk % 2) * 512:(dk % 2) * 512 + 512], AL.add)
                    ins = v.tensor_copy(bass.AP(r1b_t, dk * 512, [[8192, 128], [1, 512]]), r1c)
                    ins.then_inc(sems["ev"], 1)
            cnt["mm"] = m0 + 16; cnt["ev"] = e0 + 32
        barrier()
        with nc.Block() as b:
            @b.scalar
            def _(sc):
                for dk in range(16):
                    sc.activation(bass.AP(sq_t, dk * 512, [[8192, 128], [1, 512]]),
                                  bass.AP(r1_t, dk * 512, [[8192, 128], [1, 512]]), ACT.Square)
        barrier()
        with nc.Block() as b:
            m0 = cnt["mm"]; e0 = cnt["ev"]
            @b.tensor
            def _(t):
                for ck in range(16):
                    t.matmul(PXR[0:1, :], ones_t[:], bass.AP(r1b_t, ck * 512, [[8192, 128], [1, 512]]),
                             start=(ck == 0), stop=(ck == 15))
                for ck in range(16):
                    ins = t.matmul(PXI[0:1, :], ones_t[:], bass.AP(sq_t, ck * 512, [[8192, 128], [1, 512]]),
                                   start=(ck == 0), stop=(ck == 15))
                ins.then_inc(sems["mm"], 1)
                t.wait_ge(sems["ev"], e0 + 2)
                ins = t.matmul(PR[:], onesr_t[:], srow(0, 512), start=True, stop=True)
                ins.then_inc(sems["mm"], 1)
                t.wait_ge(sems["ev"], e0 + 3)
                ins = t.matmul(PR[:], onesr_t[:], vrow(0, 512), start=True, stop=True)
                ins.then_inc(sems["mm"], 1)
            @b.vector
            def _(v):
                v.wait_ge(sems["mm"], m0 + 1)
                v.tensor_scalar(srow(0, 512), PXR[0:1, :], 1.0 / D, None, AL.mult)
                v.tensor_scalar(srow2(0, 512), PXI[0:1, :], 1.0 / D, None, AL.mult)
                v.tensor_tensor(vrow(0, 512), srow(0, 512), srow(0, 512), AL.mult)
                v.tensor_tensor(vrow(0, 512), srow2(0, 512), vrow(0, 512), AL.subtract)
                v.tensor_scalar(vrow(0, 512), vrow(0, 512), 1e-5, None, AL.add)
                ins = v.reciprocal(mrow(0, 512), vrow(0, 512))
                ins.then_inc(sems["ev"], 1)
                v.wait_ge(sems["mm"], m0 + 2)
                ins = v.tensor_copy(mb_t[:, 0:512], PR[:])
                ins.then_inc(sems["ev"], 1)
                v.wait_ge(sems["mm"], m0 + 3)
                ins = v.tensor_copy(ib_t[:, 0:512], PR[:])
                ins.then_inc(sems["ev"], 1)
            @b.scalar
            def _(sc):
                sc.wait_ge(sems["ev"], e0 + 1)
                ins = sc.activation(vrow(0, 512), mrow(0, 512), ACT.Sqrt)
                ins.then_inc(sems["ev"], 1)
            cnt["mm"] = m0 + 3; cnt["ev"] = e0 + 4
        barrier()
        with nc.Block() as b:
            e0 = cnt["ev"]
            @b.vector
            def _(v):
                for dk in range(16):
                    r1c = bass.AP(r1_t, dk * 512, [[8192, 128], [1, 512]])
                    v.tensor_tensor(r1c, r1c, mb_t[:, 0:512], AL.subtract)
                    ins = v.tensor_tensor(r1c, r1c, ib_t[:, 0:512], AL.mult)
                    ins.then_inc(sems["ev"], 1)
            @b.sync
            def _(s):
                for dk in range(16):
                    s.wait_ge(sems["ev"], e0 + dk + 1)
                    s.dma_start(out=x2_dram[dk * 128:(dk + 1) * 128, tt * 512:(tt + 1) * 512],
                                in_=bass.AP(r1_t, dk * 512, [[8192, 128], [1, 512]])).then_inc(sems["st"], 16)
                    cnt["st"] += 16
                s.wait_ge(sems["st"], cnt["st"])
            cnt["ev"] = e0 + 16
        barrier()

    for t2 in range(2 if KPH >= 3 else 0):
        tb = t2 * 1024
        with nc.Block() as b:
            @b.gpsimd
            def _(g):
                for dk in range(16):
                    g.dma_start(out=x2b_t[:, dk * 1024:(dk + 1) * 1024],
                                in_=x2_dram[dk * 128:(dk + 1) * 128, tb:tb + 1024]).then_inc(sems["ld"], 16)
                    cnt["ld"] += 16
                g.wait_ge(sems["ld"], cnt["ld"])
        barrier()
        with nc.Block() as b:
            m0 = cnt["mm"]
            @b.tensor
            def _(t):
                for tq in range(2):
                    ps = PXR if tq == 0 else PXI
                    for dk in range(16):
                        ins = t.matmul(ps[0:1, :], ones_t[:],
                                       x2b_t[:, dk * 1024 + tq * 512: dk * 1024 + tq * 512 + 512],
                                       start=(dk == 0), stop=(dk == 15))
                ins.then_inc(sems["mm"], 1)
            @b.vector
            def _(v):
                v.wait_ge(sems["mm"], m0 + 1)
                v.tensor_copy(sxb(0, 512), PXR[0:1, :])
                v.tensor_copy(sxb(512, 1024), PXI[0:1, :])
            cnt["mm"] = m0 + 1
        barrier()
        with nc.Block() as b:
            m0 = cnt["mm"]; e0 = cnt["ev"]; l0 = cnt["ld"]
            @b.gpsimd
            def _(g):
                for hk in range(44):
                    if hk >= 2:
                        g.wait_ge(sems["mm"], m0 + 4 * (hk - 1))
                    buf = hk % 2
                    g.dma_start(out=wg_sl[:, buf * 2048:(buf + 1) * 2048],
                                in_=bass.AP(dr["wgT"].tensor, hk * 128 * 2048, [[2048, 128], [1, 2048]])).then_inc(sems["ld"], 16)
                    g.dma_start(out=wu_sl[:, buf * 2048:(buf + 1) * 2048],
                                in_=bass.AP(dr["wuT"].tensor, hk * 128 * 2048, [[2048, 128], [1, 2048]])).then_inc(sems["ld"], 16)
                    cnt["ld"] += 32
            @b.tensor
            def _(t):
                for hk in range(44):
                    buf = hk % 2
                    t.wait_ge(sems["ld"], l0 + 32 * (hk + 1))
                    if hk > 0:
                        t.wait_ge(sems["ev"], e0 + 4 * hk)
                    for tq in range(2):
                        psG = PA if tq == 0 else PXR
                        psU = PB if tq == 0 else PXI
                        for dk in range(16):
                            t.matmul(psG[:], wg_sl[:, buf * 2048 + dk * 128: buf * 2048 + (dk + 1) * 128],
                                     x2b_t[:, dk * 1024 + tq * 512: dk * 1024 + tq * 512 + 512],
                                     start=(dk == 0), stop=False)
                        ins = t.matmul(psG[:], negc_t[:], sxb(tq * 512, (tq + 1) * 512), start=False, stop=True)
                        ins.then_inc(sems["mm"], 1)
                        for dk in range(16):
                            t.matmul(psU[:], wu_sl[:, buf * 2048 + dk * 128: buf * 2048 + (dk + 1) * 128],
                                     x2b_t[:, dk * 1024 + tq * 512: dk * 1024 + tq * 512 + 512],
                                     start=(dk == 0), stop=False)
                        ins = t.matmul(psU[:], negc_t[:], sxb(tq * 512, (tq + 1) * 512), start=False, stop=True)
                        ins.then_inc(sems["mm"], 1)
            @b.scalar
            def _(sc):
                for hk in range(44):
                    for tq in range(2):
                        sc.wait_ge(sems["mm"], m0 + 4 * hk + 2 * tq + 1)
                        psG = PA if tq == 0 else PXR
                        ins = sc.activation(r2b_t[:, tq * 512:(tq + 1) * 512], psG[:],
                                            ACT.Silu, scale=sg_t[:, hk:hk + 1])
                        ins.then_inc(sems["ev"], 1)
            @b.vector
            def _(v):
                for hk in range(44):
                    for tq in range(2):
                        v.wait_ge(sems["mm"], m0 + 4 * hk + 2 * tq + 2)
                        psU = PB if tq == 0 else PXI
                        v.tensor_scalar(sqb_t[:, tq * 512:(tq + 1) * 512], psU[:],
                                        su_t[:, hk:hk + 1], None, AL.mult)
                        v.wait_ge(sems["ev"], e0 + 4 * hk + 2 * tq + 1)
                        ins = v.tensor_tensor(z_t[:, hk * 1024 + tq * 512: hk * 1024 + (tq + 1) * 512],
                                              r2b_t[:, tq * 512:(tq + 1) * 512],
                                              sqb_t[:, tq * 512:(tq + 1) * 512], AL.mult)
                        ins.then_inc(sems["ev"], 1)
            cnt["mm"] = m0 + 176; cnt["ev"] = e0 + 176
        barrier()
        with nc.Block() as b:
            m0 = cnt["mm"]
            @b.tensor
            def _(t):
                for tq in range(2):
                    ps = PXR if tq == 0 else PXI
                    for hk in range(44):
                        ins = t.matmul(ps[0:1, :], ones_t[:],
                                       z_t[:, hk * 1024 + tq * 512: hk * 1024 + tq * 512 + 512],
                                       start=(hk == 0), stop=(hk == 43))
                ins.then_inc(sems["mm"], 1)
            @b.vector
            def _(v):
                v.wait_ge(sems["mm"], m0 + 1)
                v.tensor_copy(szb(0, 512), PXR[0:1, :])
                v.tensor_copy(szb(512, 1024), PXI[0:1, :])
            cnt["mm"] = m0 + 1
        barrier()
        with nc.Block() as b:
            m0 = cnt["mm"]; e0 = cnt["ev"]; l0 = cnt["ld"]; t0_ = cnt["tp"]
            @b.gpsimd
            def _(g):
                for dk in range(16):
                    if dk >= 1:
                        g.wait_ge(sems["mm"], m0 + 2 * dk)
                    g.dma_start(out=wd_sl[:, 0:5632],
                                in_=bass.AP(dr["wdT"].tensor, dk * 128 * HID, [[HID, 128], [1, HID]])).then_inc(sems["ld"], 16)
                    cnt["ld"] += 16
            @b.sync
            def _(s):
                for dk in range(16):
                    if dk >= 1:
                        s.wait_ge(sems["ev"], e0 + 2 * dk)
                    s.dma_start(out=xr2_t[:], in_=x2_dram[dk * 128:(dk + 1) * 128, tb:tb + 1024]).then_inc(sems["tp"], 16)
                    cnt["tp"] += 16
                    for tq in range(2):
                        s.wait_ge(sems["ev"], e0 + 2 * dk + tq + 1)
                        s.dma_start(out=x2_dram[dk * 128:(dk + 1) * 128, tb + tq * 512: tb + (tq + 1) * 512],
                                    in_=r2f_t[:, tq * 512:(tq + 1) * 512]).then_inc(sems["st"], 16)
                        cnt["st"] += 16
                s.wait_ge(sems["st"], cnt["st"])
            @b.tensor
            def _(t):
                for dk in range(16):
                    t.wait_ge(sems["ld"], l0 + 16 * (dk + 1))
                    if dk >= 1:
                        t.wait_ge(sems["ev"], e0 + 2 * dk)
                    for tq in range(2):
                        ps = PA if tq == 0 else PB
                        for hk in range(44):
                            t.matmul(ps[:], wd_sl[:, hk * 128:(hk + 1) * 128],
                                     z_t[:, hk * 1024 + tq * 512: hk * 1024 + tq * 512 + 512],
                                     start=(hk == 0), stop=False)
                        ins = t.matmul(ps[:], negc_t[:], szb(tq * 512, (tq + 1) * 512), start=False, stop=True)
                        ins.then_inc(sems["mm"], 1)
            @b.vector
            def _(v):
                s0_ = cnt["st"] - 32 * 16
                for dk in range(16):
                    for tq in range(2):
                        v.wait_ge(sems["mm"], m0 + 2 * dk + tq + 1)
                        if tq == 0:
                            v.wait_ge(sems["tp"], t0_ + 16 * (dk + 1))
                        if dk >= 1:
                            v.wait_ge(sems["st"], s0_ + 16 * (2 * (dk - 1) + tq + 1))
                        ps = PA if tq == 0 else PB
                        v.tensor_scalar(r2f_t[:, tq * 512:(tq + 1) * 512], ps[:],
                                        sd_t[:, dk:dk + 1], None, AL.mult)
                        ins = v.tensor_tensor(r2f_t[:, tq * 512:(tq + 1) * 512],
                                              r2f_t[:, tq * 512:(tq + 1) * 512],
                                              xr2_t[:, tq * 512:(tq + 1) * 512], AL.add)
                        ins.then_inc(sems["ev"], 1)
            cnt["mm"] = m0 + 32; cnt["ev"] = e0 + 32
        barrier()
        with nc.Block() as b:
            m0 = cnt["mm"]; e0 = cnt["ev"]; l0 = cnt["ld"]
            @b.sync
            def _(s):
                for dk in range(16):
                    if dk > 0:
                        s.wait_ge(sems["mm"], m0 + dk)
                    s.dma_start(out=xr2_t[:], in_=x2_dram[dk * 128:(dk + 1) * 128, tb:tb + 1024]).then_inc(sems["ld"], 16)
                    cnt["ld"] += 16
            @b.vector
            def _(v):
                for dk in range(16):
                    v.wait_ge(sems["ld"], l0 + 16 * (dk + 1))
                    ins = v.tensor_copy(r2b_t[:], xr2_t[:])
                    ins.then_inc(sems["ev"], 1)
            @b.scalar
            def _(sc):
                for dk in range(16):
                    sc.wait_ge(sems["ev"], e0 + 2 * dk + 1)
                    ins = sc.activation(sqb_t[:], r2b_t[:], ACT.Square)
                    ins.then_inc(sems["ev"], 1)
            @b.tensor
            def _(t):
                for dk in range(16):
                    t.wait_ge(sems["ev"], e0 + 2 * (dk + 1))
                    t.matmul(PXR[0:1, :], ones_t[:], r2b_t[:, 0:512], start=(dk == 0), stop=(dk == 15))
                    t.matmul(PXI[0:1, :], ones_t[:], r2b_t[:, 512:1024], start=(dk == 0), stop=(dk == 15))
                    t.matmul(PA[0:1, :], ones_t[:], sqb_t[:, 0:512], start=(dk == 0), stop=(dk == 15))
                    ins = t.matmul(PB[0:1, :], ones_t[:], sqb_t[:, 512:1024], start=(dk == 0), stop=(dk == 15))
                    ins.then_inc(sems["mm"], 1)
            cnt["mm"] = m0 + 16; cnt["ev"] = e0 + 32
        barrier()
        with nc.Block() as b:
            m0 = cnt["mm"]; e0 = cnt["ev"]
            @b.vector
            def _(v):
                v.tensor_scalar(srow(0, 512), PXR[0:1, :], 1.0 / D, None, AL.mult)
                v.tensor_scalar(srow(512, 1024), PXI[0:1, :], 1.0 / D, None, AL.mult)
                v.tensor_scalar(srow2(0, 512), PA[0:1, :], 1.0 / D, None, AL.mult)
                v.tensor_scalar(srow2(512, 1024), PB[0:1, :], 1.0 / D, None, AL.mult)
                v.tensor_tensor(vrow(0, 1024), srow(0, 1024), srow(0, 1024), AL.mult)
                v.tensor_tensor(vrow(0, 1024), srow2(0, 1024), vrow(0, 1024), AL.subtract)
                v.tensor_scalar(vrow(0, 1024), vrow(0, 1024), 1e-5, None, AL.add)
                ins = v.reciprocal(mrow(0, 1024), vrow(0, 1024))
                ins.then_inc(sems["ev"], 1)
                for j in range(4):
                    v.wait_ge(sems["mm"], m0 + j + 1)
                    dst = (mb2_t, mb2_t, ib2_t, ib2_t)[j]
                    ins = v.tensor_copy(dst[:, (j % 2) * 512:(j % 2) * 512 + 512], PR[:])
                    ins.then_inc(sems["ev"], 1)
            @b.scalar
            def _(sc):
                sc.wait_ge(sems["ev"], e0 + 1)
                ins = sc.activation(vrow(0, 1024), mrow(0, 1024), ACT.Sqrt)
                ins.then_inc(sems["ev"], 1)
            @b.tensor
            def _(t):
                srcs = (srow(0, 512), srow(512, 1024), vrow(0, 512), vrow(512, 1024))
                for j in range(4):
                    t.wait_ge(sems["ev"], e0 + 2 + j)
                    ins = t.matmul(PR[:], onesr_t[:], srcs[j], start=True, stop=True)
                    ins.then_inc(sems["mm"], 1)
            cnt["mm"] = m0 + 4; cnt["ev"] = e0 + 6
        barrier()
        with nc.Block() as b:
            e0 = cnt["ev"]; l0 = cnt["ld"]; s0 = cnt["st"]
            @b.sync
            def _(s):
                for dk in range(16):
                    if dk > 0:
                        s.wait_ge(sems["ev"], e0 + dk)
                    s.dma_start(out=xr2_t[:], in_=x2_dram[dk * 128:(dk + 1) * 128, tb:tb + 1024]).then_inc(sems["ld"], 16)
                    cnt["ld"] += 16
            @b.vector
            def _(v):
                for dk in range(16):
                    v.wait_ge(sems["ld"], l0 + 16 * (dk + 1))
                    if dk > 0:
                        v.wait_ge(sems["st"], s0 + 16 * dk)
                    v.tensor_tensor(r2f_t[:], xr2_t[:], mb2_t[:], AL.subtract)
                    ins = v.tensor_tensor(r2f_t[:], r2f_t[:], ib2_t[:], AL.mult)
                    ins.then_inc(sems["ev"], 1)
            @b.scalar
            def _(sc):
                for dk in range(16):
                    sc.wait_ge(sems["ev"], e0 + dk + 1)
                    sc.dma_start(out=out_t[dk * 128:(dk + 1) * 128, tb:tb + 1024],
                                 in_=r2f_t[:]).then_inc(sems["st"], 16)
                    cnt["st"] += 16
                sc.wait_ge(sems["st"], cnt["st"])
            cnt["ev"] = e0 + 16
        barrier()

    barrier()

    _NC_CACHE["nc"] = nc
    return nc


def _prep_shared(inputs):
    f32 = np.float32
    w = np.asarray(inputs["L0_attn_in_proj_w"], f32)
    WmixT = np.ascontiguousarray(
        w.reshape(3, NB, 256, D).transpose(1, 0, 2, 3).reshape(NB * 3 * 256, D).T)
    bmix = np.ascontiguousarray(
        np.asarray(inputs["L0_attn_in_proj_b"], f32).reshape(3, NB, 256)
        .transpose(1, 0, 2).reshape(48, 128).T)
    sw = np.asarray(inputs["L0_attn_short_w"], f32)[:, 0, :]
    swT = np.ascontiguousarray(
        sw.reshape(3, NB, 256, 3).transpose(1, 0, 2, 3).reshape(48, 128, 3)
        .transpose(1, 0, 2).reshape(128, 144))
    sbT = np.ascontiguousarray(
        np.asarray(inputs["L0_attn_short_b"], f32).reshape(3, NB, 256)
        .transpose(1, 0, 2).reshape(48, 128).T)
    fbias = np.ascontiguousarray(np.asarray(inputs["L0_attn_filter_bias"], f32).reshape(16, 128).T)
    wo = np.asarray(inputs["L0_attn_out_proj_w"], f32)
    WoutT = np.ascontiguousarray(
        wo.reshape(16, 128, 16, 128).transpose(0, 3, 2, 1).reshape(16, 128, 2048))
    bout = np.ascontiguousarray(np.asarray(inputs["L0_attn_out_proj_b"], f32).reshape(16, 128).T)
    zT, dect = _consts_filter()
    dft = _consts_dft()

    def u8view(a):
        a = np.ascontiguousarray(np.asarray(a, np.int32))
        return a.view(np.uint8).reshape(a.shape[0], a.shape[1], 4)[..., 0]

    def pack_gu(a):
        t = a.reshape(44, 128, 16, 128)
        return np.ascontiguousarray(t.transpose(0, 3, 2, 1).reshape(44, 128, 2048))
    def pack_d(a):
        t = a.reshape(16, 128, 44, 128)
        return np.ascontiguousarray(t.transpose(0, 3, 2, 1).reshape(16, 128, 5632))
    wgT = pack_gu(u8view(inputs["L0_ffn_wq_gate"]))
    wuT = pack_gu(u8view(inputs["L0_ffn_wq_up"]))
    wdT = pack_d(u8view(inputs["L0_ffn_wq_down"]))
    sg = np.ascontiguousarray(np.asarray(inputs["L0_ffn_s_gate"], f32).reshape(44, 128).T)
    su = np.ascontiguousarray(np.asarray(inputs["L0_ffn_s_up"], f32).reshape(44, 128).T)
    sd = np.ascontiguousarray(np.asarray(inputs["L0_ffn_s_down"], f32).reshape(16, 128).T)
    shared = {
        "WmixT": WmixT, "bmix": bmix, "swT": swT, "sbT": sbT, "fbias": fbias,
        "WoutT": WoutT, "bout": bout,
        "w0T": np.ascontiguousarray(np.asarray(inputs["L0_attn_mlp_w0"], f32).T),
        "w1T": np.ascontiguousarray(np.asarray(inputs["L0_attn_mlp_w1"], f32).T),
        "w2T": np.ascontiguousarray(np.asarray(inputs["L0_attn_mlp_w2"], f32).T),
        "w3T": np.ascontiguousarray(np.asarray(inputs["L0_attn_mlp_w3"], f32).T),
        "b0": np.asarray(inputs["L0_attn_mlp_b0"], f32).reshape(64, 1),
        "b1": np.asarray(inputs["L0_attn_mlp_b1"], f32).reshape(64, 1),
        "b2": np.asarray(inputs["L0_attn_mlp_b2"], f32).reshape(64, 1),
        "freq": np.ascontiguousarray(np.asarray(inputs["L0_attn_sin_freq"], f32).reshape(64, 1)),
        "zT": zT, "dect": dect,
        "wgT": wgT, "wuT": wuT, "wdT": wdT, "sg": sg, "su": su, "sd": sd,
        "negc": np.full((1, 128), -128.0, bf16),
    }
    shared.update(dft)
    return shared


def kernel(**inputs):
    x = np.asarray(inputs["L0_attn_x"], np.float32)
    shared = _prep_shared(inputs)
    nc = build_program()
    in_maps = []
    for b in range(8):
        m = dict(shared)
        m["xT"] = np.ascontiguousarray(x[b].T)
        in_maps.append(m)
    res = run_bass_kernel_spmd(nc, in_maps, core_ids=list(range(8)))
    out = np.empty((8, L, D), np.float32)
    for b in range(8):
        out[b] = res.results[b]["outT"].T
    return out


# version 14
# speedup vs baseline: 1.6580x; 1.0201x over previous
import math
import os
import numpy as np
import ml_dtypes
import concourse.bass as bass
import concourse.mybir as mybir
from concourse.bass_utils import run_bass_kernel_spmd

F32 = mybir.dt.float32
BF16 = mybir.dt.bfloat16
U8 = mybir.dt.uint8
I32 = mybir.dt.int32
AL = mybir.AluOpType
ACT = mybir.ActivationFunctionType
bf16 = ml_dtypes.bfloat16

L = 2048
D = 2048
HID = 5632
NB = 8
TWO_PI = 2.0 * math.pi

MB = 66048
SEG = 16384
OFF_P = [MB + i * SEG for i in range(7)]
OFF_B = MB + 7 * SEG
OFF_T = [OFF_B + i * 8192 for i in range(4)]
OFF_KVT = OFF_B + 4 * 8192


def _consts_dft():
    s_ = np.arange(32); p_ = np.arange(64); r_ = np.arange(128); q_ = np.arange(32)
    th1 = 2 * np.pi * (np.einsum('p,r->pr', p_, r_)[None] / 128
                       + np.einsum('s,r->sr', s_, r_)[:, None, :] / 4096)
    W1c, W1s = np.cos(th1), -np.sin(th1)
    W1R = np.zeros((128, 2048), np.float32); W1I = np.zeros((128, 2048), np.float32)
    for u in range(16):
        for phi in range(2):
            W1R[phi * 64:phi * 64 + 64, u * 128:u * 128 + 128] = W1c[2 * u + phi]
            W1I[phi * 64:phi * 64 + 64, u * 128:u * 128 + 128] = W1s[2 * u + phi]
    def bdp(m):
        out = np.zeros((128, 128), np.float32)
        for c4 in range(4):
            idx = np.arange(32) * 4 + c4
            out[np.ix_(idx, idx)] = m
        return out
    th2 = 2 * np.pi * np.einsum('s,q->sq', s_, q_) / 32
    W2re, W2im = np.cos(th2), -np.sin(th2)
    th3 = 2 * np.pi * np.einsum('q,s->qs', q_, s_) / 32
    W3re, W3im = np.cos(th3) / 4096.0, np.sin(th3) / 4096.0
    th4 = 2 * np.pi * (np.einsum('p,r->pr', p_, r_)[None] / 128
                       + np.einsum('s,r->sr', s_, r_)[:, None, :] / 4096)
    W4re, W4im = np.cos(th4), np.sin(th4)
    W4R = np.zeros((128, 2048), np.float32); W4IN = np.zeros((128, 2048), np.float32)
    for s in range(32):
        W4R[:, s * 64:s * 64 + 64] = W4re[s].T
        W4IN[:, s * 64:s * 64 + 64] = -W4im[s].T
    c = {"W1R": W1R, "W1I": W1I, "W4R": W4R, "W4IN": W4IN,
         "W2RE": bdp(W2re), "W2IM": bdp(W2im), "W2IMN": bdp(-W2im),
         "W3RE": bdp(W3re), "W3IM": bdp(W3im), "W3IMN": bdp(-W3im)}
    return {k: v.astype(bf16) for k, v in c.items()}


def _consts_filter():
    t = np.linspace(0.0, 1.0, L)[:, None]
    t_resc = np.linspace(0.0, L - 1.0, L)
    w = 2.0 * np.pi * t_resc / L
    f = np.linspace(0.0001, 0.0, 1)
    ang = f[None, :] * w[:, None]
    z = np.concatenate([t, np.cos(ang), -np.sin(ang)], axis=-1)
    zT = np.ascontiguousarray(z.T).astype(bf16)
    deltas = np.abs(np.linspace(np.log(0.01) / 1.5, np.log(0.01) / 0.3, D))
    decay = np.exp(-t * deltas[None, :]).astype(np.float32)
    dect = np.zeros((128, NB * 4096), np.float32)
    pp = np.arange(64)
    for b in range(NB):
        for u in range(16):
            for phi in range(2):
                tv = 2 * u + phi + 32 * pp
                dect[phi * 64:phi * 64 + 64, b * 4096 + u * 256:b * 4096 + (u + 1) * 256] = \
                    decay[tv][:, b * 256:(b + 1) * 256]
    return zT, dect.astype(bf16)


_NC_CACHE = {}


def build_program():
    if "nc" in _NC_CACHE:
        return _NC_CACHE["nc"]
    nc = bass.Bass("TRN2", target_bir_lowering=False)

    dr = {}
    def din(name, shape, dtype=F32):
        dr[name] = nc.dram_tensor(name, list(shape), dtype, kind="ExternalInput").ap()

    din("xT", [D, L])
    din("WmixT", [D, 3 * D])
    din("bmix", [128, 48]); din("swT", [128, 144]); din("sbT", [128, 48])
    din("fbias", [128, 16])
    din("WoutT", [16, 128, 2048]); din("bout", [128, 16])
    din("w0T", [3, 64]); din("w1T", [64, 64]); din("w2T", [64, 64]); din("w3T", [64, D])
    din("b0", [64, 1]); din("b1", [64, 1]); din("b2", [64, 1]); din("freq", [64, 1])
    din("zT", [3, L], BF16)
    din("dect", [128, NB * 4096], BF16)
    for n in ("W1R", "W1I", "W4R", "W4IN"):
        din(n, [128, 2048], BF16)
    for n in ("W2RE", "W2IM", "W2IMN", "W3RE", "W3IM", "W3IMN"):
        din(n, [128, 128], BF16)
    din("wgT", [44, 128, 2048], U8); din("wuT", [44, 128, 2048], U8)
    din("wdT", [16, 128, HID], U8)
    din("sg", [128, 44]); din("su", [128, 44]); din("sd", [128, 16])
    din("negc", [1, 128], BF16)

    out_t = nc.dram_tensor("outT", [D, L], F32, kind="ExternalOutput").ap()
    y_dram = nc.dram_tensor("Yscr", [D, L], BF16, kind="Internal").ap()
    x2_dram = nc.dram_tensor("X2scr", [D, L], F32, kind="Internal").ap()

    def sbn(name, shape, dtype=BF16):
        return nc.alloc_sbuf_tensor(name, list(shape), dtype)

    C = {}
    for n in ("W1R", "W1I", "W4R", "W4IN"):
        C[n] = sbn("C" + n, [128, 2048])
    for n in ("W2RE", "W2IM", "W2IMN", "W3RE", "W3IM", "W3IMN"):
        C[n] = sbn("C" + n, [128, 128])
    bmix_t = sbn("bmix_t", [128, 48], F32)
    swT_t = sbn("swT_t", [128, 144], F32)
    sbT_t = sbn("sbT_t", [128, 48], F32)
    fbias_t = sbn("fbias_t", [128, 16], F32)
    bout_t = sbn("bout_t", [128, 16], F32)
    sg_t = sbn("sg_t", [128, 44], F32); su_t = sbn("su_t", [128, 44], F32)
    sd_t = sbn("sd_t", [128, 16], F32)
    w0T_t = sbn("w0T_t", [3, 64]); w1T_t = sbn("w1T_t", [64, 64]); w2T_t = sbn("w2T_t", [64, 64])
    w3T_t = sbn("w3T_t", [64, D])
    b0_t = sbn("b0_t", [64, 1], F32); b1_t = sbn("b1_t", [64, 1], F32)
    b2_t = sbn("b2_t", [64, 1], F32); fq_t = sbn("fq_t", [64, 1], F32)
    hhp_t = sbn("hhp_t", [64, L])
    ones_t = sbn("ones_t", [128, 1])
    onesr_t = sbn("onesr_t", [1, 128], F32)
    negc_t = sbn("negc_t", [1, 128])
    rows_t = sbn("rows_t", [1, 4096], F32)
    sxzb_t = sbn("sxzb_t", [1, 2048])

    def srow(a, b_):
        return rows_t[0:1, a:b_]
    def srow2(a, b_):
        return rows_t[0:1, 1024 + a:1024 + b_]
    def mrow(a, b_):
        return rows_t[0:1, 2048 + a:2048 + b_]
    def vrow(a, b_):
        return rows_t[0:1, 3072 + a:3072 + b_]
    def sxb(a, b_):
        return sxzb_t[0:1, a:b_]
    def szb(a, b_):
        return sxzb_t[0:1, 1024 + a:1024 + b_]
    assert nc.sbuf_bytes_remaining > (229344 - MB), nc.sbuf_bytes_remaining

    def sbm(name, offset, shape, dtype=BF16):
        return nc.alloc_sbuf_tensor_at(name, list(shape), dtype, offset=offset, align_bytes=64)

    P = [sbm(f"P{i+1}", OFF_P[i], [128, 8192]) for i in range(7)]
    T0 = sbm("T0", OFF_T[0], [128, 4096]); T1 = sbm("T1", OFF_T[1], [128, 4096])
    T2 = sbm("T2", OFF_T[2], [128, 4096]); T3 = sbm("T3", OFF_T[3], [128, 4096])
    KVT = sbm("KVT", OFF_KVT, [128, 4096])
    arg_t = sbm("arg_t", OFF_P[0], [64, 2048], F32)
    un_t = sbm("un_t", OFF_P[1], [64, 2048], F32)
    ni_t = sbm("ni_t", OFF_P[2], [64, 2048], I32)
    nf_t = sbm("nf_t", OFF_P[3], [64, 2048], F32)
    hh_t = sbm("hh_t", OFF_P[4], [64, 2048])
    zT_t = sbm("zT_t", OFF_P[5], [3, L])
    CtkR, CtkI = P[0], P[1]
    TCR, TCI = P[2], P[3]
    KRE, KIM = P[0], P[1]
    CtvR, CtvI = P[4], P[5]
    XRE, XIM = P[4], P[5]
    S1 = P[6]
    TER, TEI = P[0], P[1]
    ype = sbm("ype", OFF_P[6], [128, 4096])
    yperm = sbm("yperm", OFF_P[6] + 8192, [128, 4096])
    dect_b = sbm("dect_b", OFF_P[6], [128, 4096])
    xtile = P[6]
    wmixb = sbm("wmixb", OFF_P[4], [128, 12288])
    wo_sl = sbm("wo_sl", OFF_P[0], [128, 4096])
    ytile = sbm("ytile", OFF_P[4], [128, 8192])
    r1_t = sbm("r1_t", OFF_P[5], [128, 8192], F32)
    r1b_t = sbm("r1b_t", OFF_P[1], [128, 8192])
    sq_t = sbm("sq_t", OFF_P[2], [128, 8192])
    z_t = sbm("z_t", MB, [128, 45056])
    x2b_t = sbm("x2b_t", MB + 90112, [128, 16384])
    wd_sl = sbm("wd_sl", MB + 122880, [128, 5632])
    wg_sl = sbm("wg_sl", MB + 134144, [128, 4096])
    wu_sl = sbm("wu_sl", MB + 142336, [128, 4096])
    xr2_t = sbm("xr2_t", MB + 150528, [128, 1024], F32)
    r2f_t = sbm("r2f_t", MB + 154624, [128, 1024], F32)
    mb2_t = sbm("mb2_t", MB + 134144, [128, 1024], F32)
    ib2_t = sbm("ib2_t", MB + 138240, [128, 1024], F32)
    r2b_t = sbm("r2b_t", MB + 158720, [128, 1024])
    sqb_t = sbm("sqb_t", MB + 160768, [128, 1024])

    mb_t = mb2_t
    ib_t = ib2_t
    xr_t = xr2_t

    PS = []
    def psum(name, shape):
        g = nc.psum_tensor(name, list(shape), F32)
        t = g.__enter__(); PS.append(g)
        return t
    PA = psum("PA", [128, 512])
    PB = psum("PB", [128, 512])
    PCe = psum("PCe", [128, 512])
    PCo = psum("PCo", [128, 512])
    PXR = psum("PXR", [128, 512])
    PXI = psum("PXI", [128, 512])
    P2m = psum("P2m", [128, 256])
    PR = psum("PR", [128, 512])

    sems = {}
    for nm in ("ld", "mm", "ev", "tp", "st", "bar"):
        g = nc.semaphore("s_" + nm)
        sems[nm] = g.__enter__()
    cnt = {"ld": 0, "mm": 0, "ev": 0, "tp": 0, "st": 0, "bar": 0}
    ENG = ("sync", "scalar", "vector", "tensor", "gpsimd")

    def barrier():
        cnt["bar"] += 1
        tgt = 5 * cnt["bar"]
        with nc.Block() as b:
            for e in ENG:
                def mk(e=e):
                    def f(eng):
                        eng.sem_inc(sems["bar"], 1)
                        eng.wait_ge(sems["bar"], tgt)
                    return f
                getattr(b, e)(mk())

    with nc.Block() as b:
        @b.gpsimd
        def _(g):
            for nm, t in (("bmix", bmix_t), ("swT", swT_t), ("sbT", sbT_t), ("fbias", fbias_t),
                          ("bout", bout_t), ("sg", sg_t), ("su", su_t), ("sd", sd_t),
                          ("b0", b0_t), ("b1", b1_t), ("b2", b2_t), ("freq", fq_t),
                          ("w0T", w0T_t), ("w1T", w1T_t), ("w2T", w2T_t), ("w3T", w3T_t)):
                g.dma_start(out=t[:], in_=dr[nm][:]).then_inc(sems["ld"], 16)
                cnt["ld"] += 16
            g.memset(ones_t[:], 1.0)
            g.memset(onesr_t[:], 1.0)
        @b.sync
        def _(s):
            for nm, t in (("zT", zT_t), ("negc", negc_t),
                          ("W1R", C["W1R"]), ("W1I", C["W1I"]), ("W4R", C["W4R"]), ("W4IN", C["W4IN"]),
                          ("W2RE", C["W2RE"]), ("W2IM", C["W2IM"]), ("W2IMN", C["W2IMN"]),
                          ("W3RE", C["W3RE"]), ("W3IM", C["W3IM"]), ("W3IMN", C["W3IMN"])):
                s.dma_start(out=t[:], in_=dr[nm][:]).then_inc(sems["ld"], 16)
                cnt["ld"] += 16
            s.wait_ge(sems["ld"], cnt["ld"])
    barrier()

    def mlp_layer(lhsT, rhs_q, bias):
        with nc.Block() as b:
            m0 = cnt["mm"]; e0 = cnt["ev"]
            @b.tensor
            def _(t):
                for j in range(4):
                    if j > 0:
                        t.wait_ge(sems["ev"], e0 + j)
                    ins = t.matmul(PA[:64, :], lhsT, rhs_q[j], start=True, stop=True)
                    ins.then_inc(sems["mm"], 1)
            @b.vector
            def _(v):
                for j in range(4):
                    v.wait_ge(sems["mm"], m0 + j + 1)
                    cols = slice(j * 512, (j + 1) * 512)
                    v.tensor_scalar(arg_t[:, cols], PA[:64, :], bias[:], fq_t[:], AL.add, AL.mult)
                    v.tensor_scalar(un_t[:, cols], arg_t[:, cols], 1.0 / TWO_PI, None, AL.mult)
                    v.tensor_copy(ni_t[:, cols], un_t[:, cols])
                    v.tensor_copy(nf_t[:, cols], ni_t[:, cols])
                    ins = v.scalar_tensor_tensor(arg_t[:, cols], nf_t[:, cols], -TWO_PI,
                                                 arg_t[:, cols], AL.mult, AL.add)
                    ins.then_inc(sems["ev"], 1)
            @b.scalar
            def _(sc):
                for j in range(4):
                    sc.wait_ge(sems["ev"], e0 + j + 1)
                    cols = slice(j * 512, (j + 1) * 512)
                    sc.activation(hh_t[:, cols], arg_t[:, cols], ACT.Sin)
            cnt["mm"] = m0 + 4; cnt["ev"] = e0 + 4
        barrier()

    mlp_layer(w0T_t[:], [zT_t[:, j * 512:(j + 1) * 512] for j in range(4)], b0_t)
    mlp_layer(w1T_t[:], [hh_t[:, j * 512:(j + 1) * 512] for j in range(4)], b1_t)
    mlp_layer(w2T_t[:], [hh_t[:, j * 512:(j + 1) * 512] for j in range(4)], b2_t)
    with nc.Block() as b:
        @b.vector
        def _(v):
            ap_in = bass.AP(hh_t, 0, [[L, 64], [1, 32], [32, 64]])
            v.tensor_copy(hhp_t[:].rearrange("a (s p) -> a s p", s=32), ap_in)
    barrier()

    def F1(srcT, dR, dI, bar=True, wait_tp=None):
        with nc.Block() as b:
            m0 = cnt["mm"]; e0 = cnt["ev"]
            @b.tensor
            def _(t):
                if wait_tp is not None:
                    t.wait_ge(sems["tp"], wait_tp)
                for u in range(16):
                    if u >= 1:
                        t.wait_ge(sems["ev"], e0 + 4 * u)
                    rhsE = bass.AP(srcT, u * 128, [[4096, 64], [2048, 2], [1, 128]])
                    rhsO = bass.AP(srcT, 64 * 4096 + u * 128, [[4096, 64], [2048, 2], [1, 128]])
                    wre = C["W1R"][:, u * 128:(u + 1) * 128]
                    wim = C["W1I"][:, u * 128:(u + 1) * 128]
                    t.matmul(PCe[:, 0:256], wre[0:64, :], rhsE, start=True, stop=True, tile_position=(0, 0))
                    t.matmul(PCo[:, 0:256], wim[0:64, :], rhsE, start=True, stop=True, tile_position=(0, 0))
                    t.matmul(P2m[:], wre[64:128, :], rhsO, start=True, stop=True, tile_position=(64, 0))
                    ins = t.matmul(PR[:, 0:256], wim[64:128, :], rhsO, start=True, stop=True, tile_position=(64, 0))
                    ins.then_inc(sems["mm"], 1)
            def evac_body(eng, cpy, jobs):
                for u in range(16):
                    eng.wait_ge(sems["mm"], m0 + u + 1)
                    for so, psrc, dst in jobs:
                        ap_out = bass.AP(dst, 8 * u + 4 * so, [[8192, 128], [4096, 2], [128, 32], [1, 4]])
                        ins = cpy(ap_out, psrc)
                        ins.then_inc(sems["ev"], 1)
            @b.vector
            def _(v):
                evac_body(v, v.tensor_copy, ((0, PCe[:, 0:256], dR), (0, PCo[:, 0:256], dI)))
            @b.scalar
            def _(sc):
                evac_body(sc, sc.copy, ((1, P2m[:], dR), (1, PR[:, 0:256], dI)))
            cnt["mm"] = m0 + 16; cnt["ev"] = e0 + 64
        if bar:
            barrier()

    def turn(srcR, srcI, dstR, dstI, wait_ev=None, bar=True):
        with nc.Block() as b:
            t0_ = cnt["tp"]
            @b.sync
            def _(s):
                if wait_ev is not None:
                    s.wait_ge(sems["ev"], wait_ev)
                s.dma_start_transpose(
                    dstR[:].rearrange("a (g r) -> a g r", g=64),
                    srcR[:].rearrange("a (g f) -> a g f", g=64)).then_inc(sems["tp"], 16)
                s.dma_start_transpose(
                    dstI[:].rearrange("a (g r) -> a g r", g=64),
                    srcI[:].rearrange("a (g f) -> a g f", g=64)).then_inc(sems["tp"], 16)
                if bar:
                    s.wait_ge(sems["tp"], t0_ + 32)
            cnt["tp"] = t0_ + 32
        if bar:
            barrier()

    def F2(WRE, WIM, WIMN, srcR, srcI, dstRE, dstIM, evac="split", wait_tp=None, wait_ev=None, bar=True):
        with nc.Block() as b:
            m0 = cnt["mm"]; e0 = cnt["ev"]
            @b.tensor
            def _(t):
                if wait_tp is not None:
                    t.wait_ge(sems["tp"], wait_tp)
                if wait_ev is not None:
                    t.wait_ge(sems["ev"], wait_ev)
                for blk in range(16):
                    if blk >= 2:
                        t.wait_ge(sems["ev"], e0 + 2 * (blk - 1))
                    cols = slice(blk * 512, blk * 512 + 512)
                    pR, pI = (PXR, PXI) if blk % 2 == 0 else (PA, PB)
                    t.matmul(pR[:], C[WRE][:], srcR[:, cols], start=True, stop=False)
                    t.matmul(pR[:], C[WIMN][:], srcI[:, cols], start=False, stop=True)
                    t.matmul(pI[:], C[WIM][:], srcR[:, cols], start=True, stop=False)
                    ins = t.matmul(pI[:], C[WRE][:], srcI[:, cols], start=False, stop=True)
                    ins.then_inc(sems["mm"], 1)
            if evac == "split":
                @b.vector
                def _(v):
                    for blk in range(16):
                        v.wait_ge(sems["mm"], m0 + blk + 1)
                        cols = slice(blk * 512, blk * 512 + 512)
                        pR = PXR if blk % 2 == 0 else PA
                        ins = v.tensor_copy(dstRE[:, cols], pR[:])
                        ins.then_inc(sems["ev"], 1)
                @b.scalar
                def _(sc):
                    for blk in range(16):
                        sc.wait_ge(sems["mm"], m0 + blk + 1)
                        cols = slice(blk * 512, blk * 512 + 512)
                        pI = PXI if blk % 2 == 0 else PB
                        ins = sc.copy(dstIM[:, cols], pI[:])
                        ins.then_inc(sems["ev"], 1)
            else:
                @b.scalar
                def _(sc):
                    for blk in range(16):
                        sc.wait_ge(sems["mm"], m0 + blk + 1)
                        cols = slice(blk * 512, blk * 512 + 512)
                        pR = PXR if blk % 2 == 0 else PA
                        pI = PXI if blk % 2 == 0 else PB
                        sc.copy(dstRE[:, cols], pR[:])
                        ins = sc.copy(dstIM[:, cols], pI[:])
                        ins.then_inc(sems["ev"], 2)
            cnt["mm"] = m0 + 16; cnt["ev"] = e0 + 32
        if bar:
            barrier()

    def conv_batch(bi):
        with nc.Block() as b:
            m0 = cnt["mm"]; e0 = cnt["ev"]; t0_ = cnt["tp"]; l0 = cnt["ld"]
            @b.sync
            def _(s):
                s.dma_start(out=dect_b[:], in_=dr["dect"][:, bi * 4096:(bi + 1) * 4096]).then_inc(sems["tp"], 16)
                cnt["tp"] += 16
            @b.tensor
            def _(t):
                for u in range(16):
                    if u > 0:
                        t.wait_ge(sems["ev"], e0 + u)
                    ins = t.matmul(PA[:, 0:256], hhp_t[:, u * 128:(u + 1) * 128],
                                   w3T_t[:, bi * 256:(bi + 1) * 256], start=True, stop=True)
                    ins.then_inc(sems["mm"], 1)
            @b.vector
            def _(v):
                v.wait_ge(sems["tp"], t0_ + 16)
                for u in range(16):
                    v.wait_ge(sems["mm"], m0 + u + 1)
                    for h in range(2):
                        ins = v.tensor_tensor(
                            KVT[:, h * 2048 + u * 128: h * 2048 + u * 128 + 128],
                            PA[:, h * 128:(h + 1) * 128],
                            dect_b[:, u * 256 + h * 128: u * 256 + (h + 1) * 128], AL.mult)
                    ins.then_inc(sems["ev"], 1)
            @b.gpsimd
            def _(g):
                for dk in range(16):
                    g.dma_start(out=wmixb[:, dk * 768:(dk + 1) * 768],
                                in_=dr["WmixT"][dk * 128:(dk + 1) * 128, bi * 768:(bi + 1) * 768]).then_inc(sems["ld"], 16)
                    cnt["ld"] += 16
            cnt["mm"] = m0 + 16; cnt["ev"] = e0 + 16
        barrier()
        F1(KVT, CtkR, CtkI)
        TPK = cnt["tp"] + 32
        turn(CtkR, CtkI, TCR, TCI, bar=False)

        SW1 = (PA, PB, PXR, PXI)
        l_x0 = cnt["ld"]
        with nc.Block() as b:
            m0 = cnt["mm"]; e0 = cnt["ev"]; l0 = cnt["ld"]
            @b.gpsimd
            def _(g):
                for tt in range(4):
                    if tt == 1:
                        g.wait_ge(sems["tp"], TPK)
                    if tt >= 2:
                        g.wait_ge(sems["mm"], m0 + 2 * (tt - 1))
                    dst = xtile if tt % 2 == 0 else CtkR
                    for dk in range(16):
                        g.dma_start(out=bass.AP(dst, dk * 512, [[8192, 128], [1, 512]]),
                                    in_=dr["xT"][dk * 128:(dk + 1) * 128, tt * 512:(tt + 1) * 512]).then_inc(sems["ld"], 16)
                        cnt["ld"] += 16
            @b.tensor
            def _(t):
                for tt in range(4):
                    srcx = xtile if tt % 2 == 0 else CtkR
                    for dk in range(16):
                        t.wait_ge(sems["ld"], l0 + 256 * tt + 16 * (dk + 1))
                        if dk == 0 and tt >= 1:
                            t.wait_ge(sems["ev"], e0 + 6 * tt)
                        for rc in range(4):
                            ins = t.matmul(SW1[rc][:], wmixb[:, dk * 768 + rc * 128: dk * 768 + (rc + 1) * 128],
                                           bass.AP(srcx, dk * 512, [[8192, 128], [1, 512]]),
                                           start=(dk == 0), stop=(dk == 15))
                    ins.then_inc(sems["mm"], 1)
                    t.wait_ge(sems["ev"], e0 + 6 * tt + 4)
                    for dk in range(16):
                        for rc in range(4, 6):
                            ins = t.matmul((PA, PB)[rc - 4][:], wmixb[:, dk * 768 + rc * 128: dk * 768 + (rc + 1) * 128],
                                           bass.AP(srcx, dk * 512, [[8192, 128], [1, 512]]),
                                           start=(dk == 0), stop=(dk == 15))
                    ins.then_inc(sems["mm"], 1)
            @b.scalar
            def _(sc):
                for tt in range(4):
                    for rc in range(6):
                        sc.wait_ge(sems["mm"], m0 + 2 * tt + (1 if rc < 4 else 2))
                        ps = SW1[rc] if rc < 4 else (PA, PB)[rc - 4]
                        role, half = rc // 2, rc % 2
                        dstT = (T0, T1, T2)[role]
                        ins = sc.activation(dstT[:, half * 2048 + tt * 512: half * 2048 + (tt + 1) * 512],
                                            ps[:], ACT.Identity, bias=bmix_t[:, bi * 6 + rc: bi * 6 + rc + 1])
                        ins.then_inc(sems["ev"], 1)
            cnt["mm"] = m0 + 8; cnt["ev"] = e0 + 24
        barrier()

        F2("W2RE", "W2IM", "W2IMN", TCR, TCI, KRE, KIM, evac="scalar", bar=False, wait_tp=TPK)
        with nc.Block() as b:
            e0 = cnt["ev"]
            @b.vector
            def _(v):
                def taps(h):
                    for (srcT, dstT, role) in ((T0, T3, 0), (T1, T0, 1), (T2, T1, 2)):
                        j = bi * 6 + role * 2 + h
                        cs = h * 2048
                        w0s = swT_t[:, j * 3 + 0: j * 3 + 1]
                        w1s = swT_t[:, j * 3 + 1: j * 3 + 2]
                        w2s = swT_t[:, j * 3 + 2: j * 3 + 3]
                        v.tensor_scalar(dstT[:, cs:cs + 2048], srcT[:, cs:cs + 2048],
                                        w2s, sbT_t[:, j:j + 1], AL.mult, AL.add)
                        v.scalar_tensor_tensor(dstT[:, cs + 1:cs + 2048], srcT[:, cs:cs + 2047],
                                               w1s, dstT[:, cs + 1:cs + 2048], AL.mult, AL.add)
                        v.scalar_tensor_tensor(dstT[:, cs + 2:cs + 2048], srcT[:, cs:cs + 2046],
                                               w0s, dstT[:, cs + 2:cs + 2048], AL.mult, AL.add)
                    v.tensor_tensor(T1[:, h * 2048:(h + 1) * 2048], T1[:, h * 2048:(h + 1) * 2048],
                                    T0[:, h * 2048:(h + 1) * 2048], AL.mult)
                    ap_in = bass.AP(T1, h * 2048, [[4096, 128], [1, 32], [32, 64]])
                    out_ap = bass.AP(T2, h * 2048, [[4096, 128], [64, 32], [1, 64]])
                    ins = v.tensor_copy(out_ap, ap_in)
                    ins.then_inc(sems["ev"], 1)
                taps(0)
                taps(1)
            cnt["ev"] = e0 + 2
        barrier()
        with nc.Block() as b:
            e_pre = cnt["ev"]
            @b.sync
            def _(s):
                s.wait_ge(sems["ev"], e_pre)
                s.dma_start_transpose(
                    KVT[:].rearrange("a (t f) -> a t f", t=32),
                    T2[:].rearrange("a (t f) -> a t f", t=32)).then_inc(sems["tp"], 16)
                cnt["tp"] += 16
        F1(KVT, CtvR, CtvI, bar=False, wait_tp=cnt["tp"])
        turn(CtvR, CtvI, TCR, TCI, wait_ev=cnt["ev"], bar=False)
        F2("W2RE", "W2IM", "W2IMN", TCR, TCI, XRE, XIM, wait_tp=cnt["tp"], bar=False)
        with nc.Block() as b:
            e_pre = cnt["ev"]
            def prod(eng, c0, c1):
                eng.wait_ge(sems["ev"], e_pre)
                sl = slice(c0, c1)
                eng.tensor_tensor(S1[:, sl], XRE[:, sl], KIM[:, sl], AL.mult)
                eng.tensor_tensor(XRE[:, sl], XRE[:, sl], KRE[:, sl], AL.mult)
                eng.tensor_tensor(KRE[:, sl], XIM[:, sl], KRE[:, sl], AL.mult)
                eng.tensor_tensor(XIM[:, sl], XIM[:, sl], KIM[:, sl], AL.mult)
                eng.tensor_tensor(XRE[:, sl], XRE[:, sl], XIM[:, sl], AL.subtract)
                ins = eng.tensor_tensor(S1[:, sl], S1[:, sl], KRE[:, sl], AL.add)
                ins.then_inc(sems["ev"], 1)
            @b.vector
            def _(v):
                prod(v, 0, 6144)
            @b.gpsimd
            def _(g):
                prod(g, 6144, 8192)
            cnt["ev"] += 2
        F2("W3RE", "W3IM", "W3IMN", XRE, S1, TCR, TCI, wait_ev=cnt["ev"], bar=False)
        turn(TCR, TCI, TER, TEI, wait_ev=cnt["ev"], bar=False)
        with nc.Block() as b:
            m0 = cnt["mm"]; e0 = cnt["ev"]; tp_pre = cnt["tp"]
            @b.tensor
            def _(t):
                t.wait_ge(sems["tp"], tp_pre)
                for u in range(16):
                    if u > 0:
                        t.wait_ge(sems["ev"], e0 + u)
                    for phi in range(2):
                        s = 2 * u + phi
                        rhsR = bass.AP(TER, 4 * s, [[8192, 128], [128, 64], [1, 4]])
                        rhsI = bass.AP(TEI, 4 * s, [[8192, 128], [128, 64], [1, 4]])
                        tpos = (0, 64 * phi)
                        outp = P2m[64 * phi:64 * phi + 64, :]
                        t.matmul(outp, C["W4R"][:, s * 64:s * 64 + 64], rhsR, start=True, stop=False, tile_position=tpos)
                        ins = t.matmul(outp, C["W4IN"][:, s * 64:s * 64 + 64], rhsI, start=False, stop=True, tile_position=tpos)
                    ins.then_inc(sems["mm"], 1)
            @b.vector
            def _(v):
                for u in range(16):
                    v.wait_ge(sems["mm"], m0 + u + 1)
                    v.tensor_copy(bass.AP(ype, u * 128, [[4096, 128], [1, 128]]), P2m[:, 0:128])
                    ins = v.tensor_copy(bass.AP(ype, 2048 + u * 128, [[4096, 128], [1, 128]]), P2m[:, 128:256])
                    ins.then_inc(sems["ev"], 1)
            cnt["mm"] = m0 + 16; cnt["ev"] = e0 + 16
        with nc.Block() as b:
            e_pre = cnt["ev"]
            @b.sync
            def _(s):
                s.wait_ge(sems["ev"], e_pre)
                s.dma_start_transpose(
                    yperm[:].rearrange("a (t f) -> a t f", t=32),
                    ype[:].rearrange("a (t f) -> a t f", t=32)).then_inc(sems["tp"], 16)
                cnt["tp"] += 16
        with nc.Block() as b:
            e0 = cnt["ev"]; tp_pre = cnt["tp"]
            @b.vector
            def _(v):
                v.wait_ge(sems["tp"], tp_pre)
                for h in range(2):
                    ap_y = bass.AP(yperm, h * 2048, [[4096, 128], [1, 64], [128, 16], [64, 2]])
                    ap_g1o = bass.AP(T2, h * 2048, [[4096, 128], [32, 64], [2, 16], [1, 2]])
                    fb = fbias_t[:, bi * 2 + h: bi * 2 + h + 1]
                    vt_ap = bass.AP(T1, h * 2048, [[4096, 128], [32, 64], [2, 16], [1, 2]])
                    ins = v.scalar_tensor_tensor(ap_g1o, vt_ap, fb, ap_y, AL.mult, AL.add)
                    if h == 1:
                        ins.then_inc(sems["ev"], 1)
                ins = v.tensor_tensor(T0[:, 0:2048], T2[:, 0:2048], T3[:, 0:2048], AL.mult)
                ins.then_inc(sems["ev"], 1)
            @b.gpsimd
            def _(g):
                g.wait_ge(sems["ev"], e0 + 1)
                ins = g.tensor_tensor(T0[:, 2048:4096], T2[:, 2048:4096], T3[:, 2048:4096], AL.mult)
                ins.then_inc(sems["ev"], 1)
            @b.sync
            def _(s):
                s.wait_ge(sems["ev"], e0 + 3)
                for h in range(2):
                    s.dma_start(out=y_dram[(bi * 2 + h) * 128:(bi * 2 + h + 1) * 128, :],
                                in_=T0[:, h * 2048:(h + 1) * 2048]).then_inc(sems["st"], 16)
                    cnt["st"] += 16
                s.wait_ge(sems["st"], cnt["st"])
            cnt["ev"] = e0 + 3
        barrier()

    KPH = int(os.environ.get("KPH", "99"))
    KNB = int(os.environ.get("KNB", str(NB)))
    if KPH >= 1:
        for bi in range(KNB):
            conv_batch(bi)

    for tt in range(4 if KPH >= 2 else 0):
        with nc.Block() as b:
            @b.sync
            def _(s):
                for ck in range(16):
                    s.dma_start(out=ytile[:, ck * 512:(ck + 1) * 512],
                                in_=y_dram[ck * 128:(ck + 1) * 128, tt * 512:(tt + 1) * 512]).then_inc(sems["tp"], 16)
                    cnt["tp"] += 16
                s.wait_ge(sems["tp"], cnt["tp"])
        barrier()
        with nc.Block() as b:
            m0 = cnt["mm"]; e0 = cnt["ev"]; l0 = cnt["ld"]; t0_ = cnt["tp"]
            @b.gpsimd
            def _(g):
                for dk in range(16):
                    if dk >= 2:
                        g.wait_ge(sems["mm"], m0 + 2 * (dk - 2) + 1)
                    buf = dk % 2
                    g.dma_start(out=wo_sl[:, buf * 2048:(buf + 1) * 2048],
                                in_=bass.AP(dr["WoutT"].tensor, dk * 128 * 2048, [[2048, 128], [1, 2048]])).then_inc(sems["ld"], 16)
                    cnt["ld"] += 16
            @b.sync
            def _(s):
                for dk in range(16):
                    if dk >= 2:
                        s.wait_ge(sems["ev"], e0 + 3 * (dk - 2) + 2)
                    s.dma_start(out=xr2_t[:, (dk % 2) * 512:(dk % 2) * 512 + 512],
                                in_=dr["xT"][dk * 128:(dk + 1) * 128, tt * 512:(tt + 1) * 512]).then_inc(sems["tp"], 16)
                    cnt["tp"] += 16
            @b.tensor
            def _(t):
                for dk in range(16):
                    buf = dk % 2
                    t.wait_ge(sems["ld"], l0 + 16 * (dk + 1))
                    if dk >= 2:
                        t.wait_ge(sems["ev"], e0 + 3 * (dk - 1))
                    ps = PA if dk % 2 == 0 else PB
                    for ck in range(16):
                        ins = t.matmul(ps[:], wo_sl[:, buf * 2048 + ck * 128: buf * 2048 + (ck + 1) * 128],
                                       ytile[:, ck * 512:(ck + 1) * 512],
                                       start=(ck == 0), stop=(ck == 15))
                    ins.then_inc(sems["mm"], 1)
                    t.wait_ge(sems["ev"], e0 + 3 * dk + 3)
                    t.matmul(PCe[0:1, :], ones_t[:], bass.AP(r1b_t, dk * 512, [[8192, 128], [1, 512]]),
                             start=(dk == 0), stop=(dk == 15))
                    ins = t.matmul(PCo[0:1, :], ones_t[:], bass.AP(sq_t, dk * 512, [[8192, 128], [1, 512]]),
                                   start=(dk == 0), stop=(dk == 15))
                    ins.then_inc(sems["mm"], 1)
            @b.scalar
            def _(sc):
                for dk in range(16):
                    sc.wait_ge(sems["mm"], m0 + 2 * dk + 1)
                    ps = PA if dk % 2 == 0 else PB
                    ins = sc.activation(bass.AP(r1_t, dk * 512, [[8192, 128], [1, 512]]),
                                        ps[:], ACT.Identity, bias=bout_t[:, dk:dk + 1])
                    ins.then_inc(sems["ev"], 1)
                    sc.wait_ge(sems["ev"], e0 + 3 * dk + 2)
                    ins = sc.activation(bass.AP(sq_t, dk * 512, [[8192, 128], [1, 512]]),
                                        bass.AP(r1_t, dk * 512, [[8192, 128], [1, 512]]), ACT.Square)
                    ins.then_inc(sems["ev"], 1)
            @b.vector
            def _(v):
                for dk in range(16):
                    v.wait_ge(sems["ev"], e0 + 3 * dk + 1)
                    v.wait_ge(sems["tp"], t0_ + 16 * (dk + 1))
                    r1c = bass.AP(r1_t, dk * 512, [[8192, 128], [1, 512]])
                    v.tensor_tensor(r1c, r1c, xr2_t[:, (dk % 2) * 512:(dk % 2) * 512 + 512], AL.add)
                    ins = v.tensor_copy(bass.AP(r1b_t, dk * 512, [[8192, 128], [1, 512]]), r1c)
                    ins.then_inc(sems["ev"], 1)
            cnt["mm"] = m0 + 32; cnt["ev"] = e0 + 48
        barrier()
        with nc.Block() as b:
            m0 = cnt["mm"]; e0 = cnt["ev"]
            @b.tensor
            def _(t):
                t.wait_ge(sems["ev"], e0 + 2)
                ins = t.matmul(PR[:], onesr_t[:], srow(0, 512), start=True, stop=True)
                ins.then_inc(sems["mm"], 1)
                t.wait_ge(sems["ev"], e0 + 3)
                ins = t.matmul(PR[:], onesr_t[:], vrow(0, 512), start=True, stop=True)
                ins.then_inc(sems["mm"], 1)
            @b.vector
            def _(v):
                v.tensor_scalar(srow(0, 512), PCe[0:1, :], 1.0 / D, None, AL.mult)
                v.tensor_scalar(srow2(0, 512), PCo[0:1, :], 1.0 / D, None, AL.mult)
                v.tensor_tensor(vrow(0, 512), srow(0, 512), srow(0, 512), AL.mult)
                v.tensor_tensor(vrow(0, 512), srow2(0, 512), vrow(0, 512), AL.subtract)
                v.tensor_scalar(vrow(0, 512), vrow(0, 512), 1e-5, None, AL.add)
                ins = v.reciprocal(mrow(0, 512), vrow(0, 512))
                ins.then_inc(sems["ev"], 1)
                v.wait_ge(sems["mm"], m0 + 1)
                ins = v.tensor_copy(mb_t[:, 0:512], PR[:])
                ins.then_inc(sems["ev"], 1)
                v.wait_ge(sems["mm"], m0 + 2)
                ins = v.tensor_copy(ib_t[:, 0:512], PR[:])
                ins.then_inc(sems["ev"], 1)
            @b.scalar
            def _(sc):
                sc.wait_ge(sems["ev"], e0 + 1)
                ins = sc.activation(vrow(0, 512), mrow(0, 512), ACT.Sqrt)
                ins.then_inc(sems["ev"], 1)
            cnt["mm"] = m0 + 2; cnt["ev"] = e0 + 4
        barrier()
        with nc.Block() as b:
            e0 = cnt["ev"]
            @b.vector
            def _(v):
                for dk in range(16):
                    r1c = bass.AP(r1_t, dk * 512, [[8192, 128], [1, 512]])
                    v.tensor_tensor(r1c, r1c, mb_t[:, 0:512], AL.subtract)
                    ins = v.tensor_tensor(r1c, r1c, ib_t[:, 0:512], AL.mult)
                    ins.then_inc(sems["ev"], 1)
            @b.sync
            def _(s):
                for dk in range(16):
                    s.wait_ge(sems["ev"], e0 + dk + 1)
                    s.dma_start(out=x2_dram[dk * 128:(dk + 1) * 128, tt * 512:(tt + 1) * 512],
                                in_=bass.AP(r1_t, dk * 512, [[8192, 128], [1, 512]])).then_inc(sems["st"], 16)
                    cnt["st"] += 16
                s.wait_ge(sems["st"], cnt["st"])
            cnt["ev"] = e0 + 16
        barrier()

    for t2 in range(2 if KPH >= 3 else 0):
        tb = t2 * 1024
        with nc.Block() as b:
            @b.gpsimd
            def _(g):
                for dk in range(16):
                    g.dma_start(out=x2b_t[:, dk * 1024:(dk + 1) * 1024],
                                in_=x2_dram[dk * 128:(dk + 1) * 128, tb:tb + 1024]).then_inc(sems["ld"], 16)
                    cnt["ld"] += 16
                g.wait_ge(sems["ld"], cnt["ld"])
        barrier()
        with nc.Block() as b:
            m0 = cnt["mm"]
            @b.tensor
            def _(t):
                for tq in range(2):
                    ps = PXR if tq == 0 else PXI
                    for dk in range(16):
                        ins = t.matmul(ps[0:1, :], ones_t[:],
                                       x2b_t[:, dk * 1024 + tq * 512: dk * 1024 + tq * 512 + 512],
                                       start=(dk == 0), stop=(dk == 15))
                ins.then_inc(sems["mm"], 1)
            @b.vector
            def _(v):
                v.wait_ge(sems["mm"], m0 + 1)
                v.tensor_copy(sxb(0, 512), PXR[0:1, :])
                v.tensor_copy(sxb(512, 1024), PXI[0:1, :])
            cnt["mm"] = m0 + 1
        barrier()
        with nc.Block() as b:
            m0 = cnt["mm"]; e0 = cnt["ev"]; l0 = cnt["ld"]
            @b.gpsimd
            def _(g):
                for hk in range(44):
                    if hk >= 2:
                        g.wait_ge(sems["mm"], m0 + 4 * (hk - 1))
                    buf = hk % 2
                    g.dma_start(out=wg_sl[:, buf * 2048:(buf + 1) * 2048],
                                in_=bass.AP(dr["wgT"].tensor, hk * 128 * 2048, [[2048, 128], [1, 2048]])).then_inc(sems["ld"], 16)
                    g.dma_start(out=wu_sl[:, buf * 2048:(buf + 1) * 2048],
                                in_=bass.AP(dr["wuT"].tensor, hk * 128 * 2048, [[2048, 128], [1, 2048]])).then_inc(sems["ld"], 16)
                    cnt["ld"] += 32
            @b.tensor
            def _(t):
                for hk in range(44):
                    buf = hk % 2
                    t.wait_ge(sems["ld"], l0 + 32 * (hk + 1))
                    if hk > 0:
                        t.wait_ge(sems["ev"], e0 + 4 * hk)
                    for tq in range(2):
                        psG = PA if tq == 0 else PXR
                        psU = PB if tq == 0 else PXI
                        for dk in range(16):
                            t.matmul(psG[:], wg_sl[:, buf * 2048 + dk * 128: buf * 2048 + (dk + 1) * 128],
                                     x2b_t[:, dk * 1024 + tq * 512: dk * 1024 + tq * 512 + 512],
                                     start=(dk == 0), stop=False)
                        ins = t.matmul(psG[:], negc_t[:], sxb(tq * 512, (tq + 1) * 512), start=False, stop=True)
                        ins.then_inc(sems["mm"], 1)
                        for dk in range(16):
                            t.matmul(psU[:], wu_sl[:, buf * 2048 + dk * 128: buf * 2048 + (dk + 1) * 128],
                                     x2b_t[:, dk * 1024 + tq * 512: dk * 1024 + tq * 512 + 512],
                                     start=(dk == 0), stop=False)
                        ins = t.matmul(psU[:], negc_t[:], sxb(tq * 512, (tq + 1) * 512), start=False, stop=True)
                        ins.then_inc(sems["mm"], 1)
            @b.scalar
            def _(sc):
                for hk in range(44):
                    for tq in range(2):
                        sc.wait_ge(sems["mm"], m0 + 4 * hk + 2 * tq + 1)
                        psG = PA if tq == 0 else PXR
                        ins = sc.activation(r2b_t[:, tq * 512:(tq + 1) * 512], psG[:],
                                            ACT.Silu, scale=sg_t[:, hk:hk + 1])
                        ins.then_inc(sems["ev"], 1)
            @b.vector
            def _(v):
                for hk in range(44):
                    for tq in range(2):
                        v.wait_ge(sems["mm"], m0 + 4 * hk + 2 * tq + 2)
                        psU = PB if tq == 0 else PXI
                        v.tensor_scalar(sqb_t[:, tq * 512:(tq + 1) * 512], psU[:],
                                        su_t[:, hk:hk + 1], None, AL.mult)
                        v.wait_ge(sems["ev"], e0 + 4 * hk + 2 * tq + 1)
                        ins = v.tensor_tensor(z_t[:, hk * 1024 + tq * 512: hk * 1024 + (tq + 1) * 512],
                                              r2b_t[:, tq * 512:(tq + 1) * 512],
                                              sqb_t[:, tq * 512:(tq + 1) * 512], AL.mult)
                        ins.then_inc(sems["ev"], 1)
            cnt["mm"] = m0 + 176; cnt["ev"] = e0 + 176
        barrier()
        with nc.Block() as b:
            m0 = cnt["mm"]
            @b.tensor
            def _(t):
                for tq in range(2):
                    ps = PXR if tq == 0 else PXI
                    for hk in range(44):
                        ins = t.matmul(ps[0:1, :], ones_t[:],
                                       z_t[:, hk * 1024 + tq * 512: hk * 1024 + tq * 512 + 512],
                                       start=(hk == 0), stop=(hk == 43))
                ins.then_inc(sems["mm"], 1)
            @b.vector
            def _(v):
                v.wait_ge(sems["mm"], m0 + 1)
                v.tensor_copy(szb(0, 512), PXR[0:1, :])
                v.tensor_copy(szb(512, 1024), PXI[0:1, :])
            cnt["mm"] = m0 + 1
        barrier()
        with nc.Block() as b:
            m0 = cnt["mm"]; e0 = cnt["ev"]; l0 = cnt["ld"]; t0_ = cnt["tp"]
            @b.gpsimd
            def _(g):
                for dk in range(16):
                    if dk >= 1:
                        g.wait_ge(sems["mm"], m0 + 4 * (dk - 1) + 2)
                    g.dma_start(out=wd_sl[:, 0:5632],
                                in_=bass.AP(dr["wdT"].tensor, dk * 128 * HID, [[HID, 128], [1, HID]])).then_inc(sems["ld"], 16)
                    cnt["ld"] += 16
            @b.sync
            def _(s):
                for dk in range(16):
                    if dk >= 1:
                        s.wait_ge(sems["ev"], e0 + 6 * (dk - 1) + 4)
                    s.dma_start(out=xr2_t[:], in_=x2_dram[dk * 128:(dk + 1) * 128, tb:tb + 1024]).then_inc(sems["tp"], 16)
                    cnt["tp"] += 16
                    for tq in range(2):
                        s.wait_ge(sems["ev"], e0 + 6 * dk + 3 * tq + 1)
                        s.dma_start(out=x2_dram[dk * 128:(dk + 1) * 128, tb + tq * 512: tb + (tq + 1) * 512],
                                    in_=r2f_t[:, tq * 512:(tq + 1) * 512]).then_inc(sems["st"], 16)
                        cnt["st"] += 16
                s.wait_ge(sems["st"], cnt["st"])
            @b.tensor
            def _(t):
                for dk in range(16):
                    t.wait_ge(sems["ld"], l0 + 16 * (dk + 1))
                    if dk >= 1:
                        t.wait_ge(sems["ev"], e0 + 6 * (dk - 1) + 4)
                    for tq in range(2):
                        ps = PA if tq == 0 else PB
                        for hk in range(44):
                            t.matmul(ps[:], wd_sl[:, hk * 128:(hk + 1) * 128],
                                     z_t[:, hk * 1024 + tq * 512: hk * 1024 + tq * 512 + 512],
                                     start=(hk == 0), stop=False)
                        ins = t.matmul(ps[:], negc_t[:], szb(tq * 512, (tq + 1) * 512), start=False, stop=True)
                        ins.then_inc(sems["mm"], 1)
                    for tq in range(2):
                        t.wait_ge(sems["ev"], e0 + 6 * dk + 3 * tq + 3)
                        psum_sum = PXR if tq == 0 else PXI
                        psum_sq = PCe if tq == 0 else PCo
                        t.matmul(psum_sum[0:1, :], ones_t[:], r2b_t[:, tq * 512:(tq + 1) * 512],
                                 start=(dk == 0), stop=(dk == 15))
                        ins = t.matmul(psum_sq[0:1, :], ones_t[:], sqb_t[:, tq * 512:(tq + 1) * 512],
                                       start=(dk == 0), stop=(dk == 15))
                        ins.then_inc(sems["mm"], 1)
            @b.vector
            def _(v):
                s0_ = cnt["st"] - 32 * 16
                for dk in range(16):
                    for tq in range(2):
                        v.wait_ge(sems["mm"], m0 + 4 * dk + tq + 1)
                        if tq == 0:
                            v.wait_ge(sems["tp"], t0_ + 16 * (dk + 1))
                        if dk >= 1:
                            v.wait_ge(sems["st"], s0_ + 16 * (2 * (dk - 1) + tq + 1))
                            v.wait_ge(sems["mm"], m0 + 4 * (dk - 1) + 3 + tq)
                        ps = PA if tq == 0 else PB
                        v.tensor_scalar(r2f_t[:, tq * 512:(tq + 1) * 512], ps[:],
                                        sd_t[:, dk:dk + 1], None, AL.mult)
                        ins = v.tensor_tensor(r2f_t[:, tq * 512:(tq + 1) * 512],
                                              r2f_t[:, tq * 512:(tq + 1) * 512],
                                              xr2_t[:, tq * 512:(tq + 1) * 512], AL.add)
                        ins.then_inc(sems["ev"], 1)
                        ins = v.tensor_copy(r2b_t[:, tq * 512:(tq + 1) * 512],
                                            r2f_t[:, tq * 512:(tq + 1) * 512])
                        ins.then_inc(sems["ev"], 1)
            @b.scalar
            def _(sc):
                for dk in range(16):
                    for tq in range(2):
                        sc.wait_ge(sems["ev"], e0 + 6 * dk + 3 * tq + 2)
                        if dk >= 1:
                            sc.wait_ge(sems["mm"], m0 + 4 * (dk - 1) + 3 + tq)
                        ins = sc.activation(sqb_t[:, tq * 512:(tq + 1) * 512],
                                            r2f_t[:, tq * 512:(tq + 1) * 512], ACT.Square)
                        ins.then_inc(sems["ev"], 1)
            cnt["mm"] = m0 + 64; cnt["ev"] = e0 + 96
        barrier()
        with nc.Block() as b:
            m0 = cnt["mm"]; e0 = cnt["ev"]
            @b.vector
            def _(v):
                v.tensor_scalar(srow(0, 512), PXR[0:1, :], 1.0 / D, None, AL.mult)
                v.tensor_scalar(srow(512, 1024), PXI[0:1, :], 1.0 / D, None, AL.mult)
                v.tensor_scalar(srow2(0, 512), PCe[0:1, :], 1.0 / D, None, AL.mult)
                v.tensor_scalar(srow2(512, 1024), PCo[0:1, :], 1.0 / D, None, AL.mult)
                v.tensor_tensor(vrow(0, 1024), srow(0, 1024), srow(0, 1024), AL.mult)
                v.tensor_tensor(vrow(0, 1024), srow2(0, 1024), vrow(0, 1024), AL.subtract)
                v.tensor_scalar(vrow(0, 1024), vrow(0, 1024), 1e-5, None, AL.add)
                ins = v.reciprocal(mrow(0, 1024), vrow(0, 1024))
                ins.then_inc(sems["ev"], 1)
                for j in range(4):
                    v.wait_ge(sems["mm"], m0 + j + 1)
                    dst = (mb2_t, mb2_t, ib2_t, ib2_t)[j]
                    ins = v.tensor_copy(dst[:, (j % 2) * 512:(j % 2) * 512 + 512], PR[:])
                    ins.then_inc(sems["ev"], 1)
            @b.scalar
            def _(sc):
                sc.wait_ge(sems["ev"], e0 + 1)
                ins = sc.activation(vrow(0, 1024), mrow(0, 1024), ACT.Sqrt)
                ins.then_inc(sems["ev"], 1)
            @b.tensor
            def _(t):
                srcs = (srow(0, 512), srow(512, 1024), vrow(0, 512), vrow(512, 1024))
                for j in range(4):
                    t.wait_ge(sems["ev"], e0 + 2 + j)
                    ins = t.matmul(PR[:], onesr_t[:], srcs[j], start=True, stop=True)
                    ins.then_inc(sems["mm"], 1)
            cnt["mm"] = m0 + 4; cnt["ev"] = e0 + 6
        barrier()
        with nc.Block() as b:
            e0 = cnt["ev"]; l0 = cnt["ld"]; s0 = cnt["st"]
            @b.sync
            def _(s):
                for dk in range(16):
                    if dk > 0:
                        s.wait_ge(sems["ev"], e0 + dk)
                    s.dma_start(out=xr2_t[:], in_=x2_dram[dk * 128:(dk + 1) * 128, tb:tb + 1024]).then_inc(sems["ld"], 16)
                    cnt["ld"] += 16
            @b.vector
            def _(v):
                for dk in range(16):
                    v.wait_ge(sems["ld"], l0 + 16 * (dk + 1))
                    if dk > 0:
                        v.wait_ge(sems["st"], s0 + 16 * dk)
                    v.tensor_tensor(r2f_t[:], xr2_t[:], mb2_t[:], AL.subtract)
                    ins = v.tensor_tensor(r2f_t[:], r2f_t[:], ib2_t[:], AL.mult)
                    ins.then_inc(sems["ev"], 1)
            @b.scalar
            def _(sc):
                for dk in range(16):
                    sc.wait_ge(sems["ev"], e0 + dk + 1)
                    sc.dma_start(out=out_t[dk * 128:(dk + 1) * 128, tb:tb + 1024],
                                 in_=r2f_t[:]).then_inc(sems["st"], 16)
                    cnt["st"] += 16
                sc.wait_ge(sems["st"], cnt["st"])
            cnt["ev"] = e0 + 16
        barrier()

    barrier()

    _NC_CACHE["nc"] = nc
    return nc


def _prep_shared(inputs):
    f32 = np.float32
    w = np.asarray(inputs["L0_attn_in_proj_w"], f32)
    WmixT = np.ascontiguousarray(
        w.reshape(3, NB, 256, D).transpose(1, 0, 2, 3).reshape(NB * 3 * 256, D).T)
    bmix = np.ascontiguousarray(
        np.asarray(inputs["L0_attn_in_proj_b"], f32).reshape(3, NB, 256)
        .transpose(1, 0, 2).reshape(48, 128).T)
    sw = np.asarray(inputs["L0_attn_short_w"], f32)[:, 0, :]
    swT = np.ascontiguousarray(
        sw.reshape(3, NB, 256, 3).transpose(1, 0, 2, 3).reshape(48, 128, 3)
        .transpose(1, 0, 2).reshape(128, 144))
    sbT = np.ascontiguousarray(
        np.asarray(inputs["L0_attn_short_b"], f32).reshape(3, NB, 256)
        .transpose(1, 0, 2).reshape(48, 128).T)
    fbias = np.ascontiguousarray(np.asarray(inputs["L0_attn_filter_bias"], f32).reshape(16, 128).T)
    wo = np.asarray(inputs["L0_attn_out_proj_w"], f32)
    WoutT = np.ascontiguousarray(
        wo.reshape(16, 128, 16, 128).transpose(0, 3, 2, 1).reshape(16, 128, 2048))
    bout = np.ascontiguousarray(np.asarray(inputs["L0_attn_out_proj_b"], f32).reshape(16, 128).T)
    zT, dect = _consts_filter()
    dft = _consts_dft()

    def u8view(a):
        a = np.ascontiguousarray(np.asarray(a, np.int32))
        return a.view(np.uint8).reshape(a.shape[0], a.shape[1], 4)[..., 0]

    def pack_gu(a):
        t = a.reshape(44, 128, 16, 128)
        return np.ascontiguousarray(t.transpose(0, 3, 2, 1).reshape(44, 128, 2048))
    def pack_d(a):
        t = a.reshape(16, 128, 44, 128)
        return np.ascontiguousarray(t.transpose(0, 3, 2, 1).reshape(16, 128, 5632))
    wgT = pack_gu(u8view(inputs["L0_ffn_wq_gate"]))
    wuT = pack_gu(u8view(inputs["L0_ffn_wq_up"]))
    wdT = pack_d(u8view(inputs["L0_ffn_wq_down"]))
    sg = np.ascontiguousarray(np.asarray(inputs["L0_ffn_s_gate"], f32).reshape(44, 128).T)
    su = np.ascontiguousarray(np.asarray(inputs["L0_ffn_s_up"], f32).reshape(44, 128).T)
    sd = np.ascontiguousarray(np.asarray(inputs["L0_ffn_s_down"], f32).reshape(16, 128).T)
    shared = {
        "WmixT": WmixT, "bmix": bmix, "swT": swT, "sbT": sbT, "fbias": fbias,
        "WoutT": WoutT, "bout": bout,
        "w0T": np.ascontiguousarray(np.asarray(inputs["L0_attn_mlp_w0"], f32).T),
        "w1T": np.ascontiguousarray(np.asarray(inputs["L0_attn_mlp_w1"], f32).T),
        "w2T": np.ascontiguousarray(np.asarray(inputs["L0_attn_mlp_w2"], f32).T),
        "w3T": np.ascontiguousarray(np.asarray(inputs["L0_attn_mlp_w3"], f32).T),
        "b0": np.asarray(inputs["L0_attn_mlp_b0"], f32).reshape(64, 1),
        "b1": np.asarray(inputs["L0_attn_mlp_b1"], f32).reshape(64, 1),
        "b2": np.asarray(inputs["L0_attn_mlp_b2"], f32).reshape(64, 1),
        "freq": np.ascontiguousarray(np.asarray(inputs["L0_attn_sin_freq"], f32).reshape(64, 1)),
        "zT": zT, "dect": dect,
        "wgT": wgT, "wuT": wuT, "wdT": wdT, "sg": sg, "su": su, "sd": sd,
        "negc": np.full((1, 128), -128.0, bf16),
    }
    shared.update(dft)
    return shared


def kernel(**inputs):
    x = np.asarray(inputs["L0_attn_x"], np.float32)
    shared = _prep_shared(inputs)
    nc = build_program()
    in_maps = []
    for b in range(8):
        m = dict(shared)
        m["xT"] = np.ascontiguousarray(x[b].T)
        in_maps.append(m)
    res = run_bass_kernel_spmd(nc, in_maps, core_ids=list(range(8)))
    out = np.empty((8, L, D), np.float32)
    for b in range(8):
        out[b] = res.results[b]["outT"].T
    return out
